# version 07
# baseline (speedup 1.0000x reference)
import math
import bisect
from contextlib import ExitStack
import numpy as np
import ml_dtypes
import concourse.bass as bass
import concourse.mybir as mybir
from concourse.bass_utils import run_bass_kernel_spmd

F32 = mybir.dt.float32
BF16 = mybir.dt.bfloat16
AF = mybir.ActivationFunctionType
ALU = mybir.AluOpType
AX = mybir.AxisListType

ENG2ATTR = {'pe': 'tensor', 'dve': 'vector', 'act': 'scalar',
            'pool': 'gpsimd', 'sp': 'sync'}

C = 2048
H = 16
D = 128
R = 64
DELTA_I = 0.75
SCALE = 1.0 / math.sqrt(D)
SCALE_G = SCALE * D / R
NEG = -1.0e30
EPS = 1e-5
E_MOE = 16
I_MOE = 1408
I_SH = 5632
NIC = I_MOE // 128
NCC = C // 128

DUMP = set()
TSLOTS = 8
A1B = True
A1B_G = True
A1B_L = True
A1B_ROWS = True
A1A = True
A1B_KR = True
A1B_VR = True
A1B_VR2 = True
PHASES = {'a0', 'a1', 'a2', 'f'}


class Prog:

    def __init__(self, nc):
        self.nc = nc
        self.engines = list(ENG2ATTR)
        self.sem = {e: nc.alloc_semaphore(f'S_{e}') for e in self.engines}
        self.ops = {e: [] for e in self.engines}
        self.inc_idx = {e: [] for e in self.engines}
        self.inc_val = {e: [] for e in self.engines}
        self.count = {e: 0 for e in self.engines}
        self.clock = {e: {} for e in self.engines}
        self.res = {}
        self.pending = {e: [] for e in self.engines}
        self.streams = {}
        self.pending_tp = set()
        self.tp_flush = None

    def _stream(self, sname):
        sname = 'q_' + sname
        if sname not in self.streams:
            self.sem[sname] = self.nc.alloc_semaphore(f'S_{sname}')
            self.streams[sname] = 0
        return sname

    def _observe(self, peng, pidx):
        o = self.ops[peng][pidx]
        if o['dma']:
            return (o['stream'], self.streams[o['stream']])
        ii = self.inc_idx[peng]
        j = bisect.bisect_left(ii, pidx)
        if j < len(ii):
            return (peng, self.inc_val[peng][j])
        ops = self.ops[peng]
        k = len(ops) - 1
        while k >= 0 and (ops[k]['fn'] is None or ops[k]['dma']):
            k -= 1
        assert k >= 0 and k >= pidx, f'no op to observe {peng}:{pidx}'
        self.count[peng] += 1
        ops[k]['inc'] = 1
        self.inc_idx[peng].append(k)
        self.inc_val[peng].append(self.count[peng])
        return (peng, self.count[peng])

    def op(self, eng, fn, reads=(), writes=(), dma=False, stream=None):
        if self.pending_tp and ((set(reads) | set(writes)) & self.pending_tp):
            self.tp_flush()
        if dma:
            stream = self._stream(stream or eng)
        idx = len(self.ops[eng])
        waits = {}

        def need(tok):
            peng, pidx = tok
            if peng == eng and peng == 'pe':
                return
            sname, val = self._observe(peng, pidx)
            if self.clock[eng].get(sname, 0) >= val:
                return
            waits[sname] = max(waits.get(sname, 0), val)

        for tok in self.pending[eng]:
            need(tok)
        self.pending[eng] = []
        for r in reads:
            st = self.res.get(r)
            if st and st['w'] is not None:
                need(st['w'])
        for w in writes:
            st = self.res.get(w)
            if st:
                if st['w'] is not None:
                    need(st['w'])
                for t in st['r']:
                    need(t)
        tok = (eng, idx)
        for r in reads:
            self.res.setdefault(r, {'w': None, 'r': []})['r'].append(tok)
        for w in writes:
            self.res[w] = {'w': tok, 'r': []}
        for sname, val in waits.items():
            self.clock[eng][sname] = val
        o = {'fn': fn, 'waits': sorted(waits.items()), 'inc': None, 'dma': dma,
             'stream': stream}
        self.ops[eng].append(o)
        if dma:
            self.streams[stream] += 16
            o['inc'] = 16
        return tok

    def dma(self, q, out_ap, in_ap, reads=(), writes=(), slow=False,
            stream=None):
        return self.op(q, lambda E: E.dma_start(
            out_ap, in_ap, allow_slow_non_contiguous=slow),
            reads=reads, writes=writes, dma=True, stream=stream)

    def barrier(self):
        if self.pending_tp:
            self.tp_flush()
        toks = []
        for e in self.engines:
            ops = self.ops[e]
            k = len(ops) - 1
            while k >= 0 and ops[k]['fn'] is None:
                k -= 1
            if k >= 0:
                toks.append((e, k))
        for e in self.engines:
            self.pending[e] = list(self.pending[e]) + \
                [t for t in toks if t[0] != e]

    def wait_for(self, eng, tokens):
        waits = {}
        for (peng, pidx) in tokens:
            sname, val = self._observe(peng, pidx)
            if self.clock[eng].get(sname, 0) >= val:
                continue
            waits[sname] = max(waits.get(sname, 0), val)
        for sname, val in waits.items():
            self.clock[eng][sname] = val
        self.ops[eng].append({'fn': None, 'waits': sorted(waits.items()),
                              'inc': None, 'dma': False})

    def finalize(self):
        if self.pending_tp:
            self.tp_flush()
        waits = {}
        for sname, total in self.streams.items():
            if self.clock['sp'].get(sname, 0) < total:
                waits[sname] = total
                self.clock['sp'][sname] = total
        self.ops['sp'].append({'fn': None, 'waits': sorted(waits.items()),
                               'inc': None, 'dma': False, 'stream': None})

    def emit(self):
        semh = self.sem
        with self.nc.Block() as block:
            for eng in self.engines:
                ops = self.ops[eng]
                if not ops:
                    continue

                def make_body(ops=ops, eng=eng):
                    def body(E):
                        for o in ops:
                            for sname, val in o['waits']:
                                E.wait_ge(semh[sname], val)
                            if o['fn'] is not None:
                                ins = o['fn'](E)
                                if o['inc'] is not None:
                                    tgt = o['stream'] if o['dma'] else eng
                                    ins.then_inc(semh[tgt], o['inc'])
                    return body
                getattr(block, ENG2ATTR[eng])(make_body())


def make_consts(T):
    inv_freq = 1.0 / (10000.0 ** (np.arange(0, D, 2, dtype=np.float64) / D))
    t = np.arange(T, dtype=np.float64)
    freqs = np.outer(t, inv_freq)
    emb = np.concatenate([freqs, freqs], axis=-1)
    cos = np.cos(emb).astype(np.float32)
    sin = np.sin(emb).astype(np.float32)
    sin_s = sin.copy()
    sin_s[:, 0::2] = -sin_s[:, 0::2]
    j = np.arange(128)[:, None]
    i = np.arange(128)[None, :]
    mask_at = np.where(i >= j, NEG, 0.0).astype(np.float32)
    mask_bt = np.where(j > i, NEG, 0.0).astype(np.float32)
    ident = np.eye(128, dtype=ml_dtypes.bfloat16)
    return dict(cos_tab=cos, sin_tab=sin_s, mask_at=mask_at, mask_bt=mask_bt,
                ident=ident)


def build(T=2048):
    nc = bass.Bass('TRN2', target_bir_lowering=False)
    NTC = T // 128
    NG = T // 512
    HH = H // 2

    def dram(name, shape, dtype):
        kind = 'ExternalOutput' if name in DUMP else 'Internal'
        return nc.dram_tensor(name, shape, dtype, kind=kind)

    x_in = nc.dram_tensor('x', [T, C], F32, kind='ExternalInput')
    w_qkv = nc.dram_tensor('w_qkv', [3 * C, C], F32, kind='ExternalInput')
    w_o = nc.dram_tensor('w_o', [C, C], F32, kind='ExternalInput')
    pk = nc.dram_tensor('pk', [R, D], F32, kind='ExternalInput')
    pv = nc.dram_tensor('pv', [R, D], F32, kind='ExternalInput')
    uo = nc.dram_tensor('uo', [D, R], F32, kind='ExternalInput')
    info_w1 = nc.dram_tensor('info_w1', [512, C], F32, kind='ExternalInput')
    info_w2 = nc.dram_tensor('info_w2', [1, 512], F32, kind='ExternalInput')
    router_w = nc.dram_tensor('router_w', [E_MOE, C], F32, kind='ExternalInput')
    w_gate = nc.dram_tensor('w_gate', [E_MOE, NIC, 128, NCC, 128], F32, kind='ExternalInput')
    w_up = nc.dram_tensor('w_up', [E_MOE, NIC, 128, NCC, 128], F32, kind='ExternalInput')
    w_down = nc.dram_tensor('w_down', [E_MOE, I_MOE, C], F32, kind='ExternalInput')
    sh_gate = nc.dram_tensor('sh_gate', [4 * NIC, 128, NCC, 128], F32, kind='ExternalInput')
    sh_up = nc.dram_tensor('sh_up', [4 * NIC, 128, NCC, 128], F32, kind='ExternalInput')
    sh_down = nc.dram_tensor('sh_down', [I_SH, C], F32, kind='ExternalInput')
    sh_gate_w = nc.dram_tensor('sh_gate_w', [C, 1], F32, kind='ExternalInput')
    cos_in = nc.dram_tensor('cos_tab', [T, D], F32, kind='ExternalInput')
    sin_in = nc.dram_tensor('sin_tab', [T, D], F32, kind='ExternalInput')
    ma_in = nc.dram_tensor('mask_at', [128, 128], F32, kind='ExternalInput')
    mb_in = nc.dram_tensor('mask_bt', [128, 128], F32, kind='ExternalInput')
    id_in = nc.dram_tensor('ident', [128, 128], BF16, kind='ExternalInput')

    out = nc.dram_tensor('out', [T, C], F32, kind='ExternalOutput')

    xT_scr = dram('xT_scr', [NTC, 128, C], BF16)
    q_scr = dram('q_scr', [H, 128, T], BF16)
    ctx_scr = dram('ctx_scr', [H, 128, T], BF16)
    x1_scr = dram('x1_scr', [T, C], F32)
    x1T_scr = dram('x1T_scr', [NCC, 128, T], BF16)

    p = Prog(nc)

    PS0 = nc.alloc_psum_tensor('PS0', [128, 2048], F32)
    PS1 = nc.alloc_psum_tensor('PS1', [128, 1024], F32)
    PS2 = nc.alloc_psum_tensor('PS2', [128, 512], F32)
    PST = nc.alloc_psum_tensor('PST', [128, 1024], BF16)

    sb = nc.alloc_sbuf_tensor
    id_sb = sb('id_sb', [128, 128], BF16)
    pkT_sb = sb('pkT_sb', [128, R], BF16)
    pvT_sb = sb('pvT_sb', [128, R], BF16)
    uoT_sb = sb('uoT_sb', [64, 128], BF16)
    gate_sb = sb('gate_sb', [128, NTC], F32)
    ones128 = sb('ones128', [128, 1], BF16)
    ones1f = sb('ones1f', [1, 128], F32)
    sg_sb = sb('sg_sb', [128, NTC], F32)
    comb_sb = sb('comb_sb', [128, NTC * E_MOE], F32)
    st_sb = sb('st_sb', [128, 16], F32)

    p.dma('sp', id_sb[:, :], id_in.ap(), writes=['ident'], stream='ident')
    p.op('dve', lambda E: E.memset(ones128[:, :], 1.0), writes=['ones128'])
    p.op('dve', lambda E: E.memset(ones1f[:, :], 1.0), writes=['ones1f'])

    _tq = []
    _ectr = [0]
    _tflushing = [False]

    def tp_flush():
        if _tflushing[0] or not _tq:
            p.pending_tp.clear()
            return
        _tflushing[0] = True
        batch = list(_tq)
        del _tq[:]
        toks = []
        for (in_ap, in_keys, dst_ap, dst_keys, kpart, mfree, i) in batch:
            out_ap = PST[0:mfree, i * 128:i * 128 + kpart]
            tok = p.op('pe',
                       lambda E, out_ap=out_ap, in_ap=in_ap, kpart=kpart:
                       E.transpose(out_ap, in_ap, id_sb[0:kpart, 0:kpart]),
                       reads=list(in_keys) + ['ident'], writes=[f'PSTq{i}'])
            toks.append(tok)
        for (_, _, _, _, _, _, i) in batch:
            p.res[f'PSTq{i}'] = {'w': toks[-1], 'r': []}
        for (in_ap, in_keys, dst_ap, dst_keys, kpart, mfree, i) in batch:
            out_ap = PST[0:mfree, i * 128:i * 128 + kpart]
            _ectr[0] += 1
            if True:
                p.op('dve', lambda E, dst_ap=dst_ap, out_ap=out_ap:
                     E.tensor_copy(dst_ap, out_ap),
                     reads=[f'PSTq{i}'], writes=list(dst_keys))
            else:
                p.op('act', lambda E, dst_ap=dst_ap, out_ap=out_ap:
                     E.copy(dst_ap, out_ap),
                     reads=[f'PSTq{i}'], writes=list(dst_keys))
        p.pending_tp.clear()
        _tflushing[0] = False

    p.tp_flush = tp_flush

    def transpose_tile(in_ap, in_keys, dst_ap, dst_keys, kpart=128, mfree=128):
        if len(_tq) == 8:
            tp_flush()
        i = len(_tq)
        _tq.append((in_ap, list(in_keys), dst_ap, list(dst_keys), kpart,
                    mfree, i))
        p.pending_tp.update(in_keys)
        p.pending_tp.update(dst_keys)
        p.pending_tp.add(f'PSTq{i}')

    def maybe_dump(name, ap, shape, reads, dtype=F32):
        if name in DUMP:
            t = nc.dram_tensor(name, shape, dtype, kind='ExternalOutput')
            p.dma('sp', t.ap(), ap, reads=reads, stream='dump')

    def ln_epilogue(r_ap, r_keys, dst_ap, dst_key, junk_ap, junk_key):
        p.op('dve', lambda E: E.reduce_sum(st_sb[:, 0:1], r_ap, axis=AX.X),
             reads=r_keys, writes=['st0'])
        p.op('dve', lambda E: E.tensor_single_scalar(
            st_sb[:, 1:2], st_sb[:, 0:1], -1.0 / C, ALU.mult),
            reads=['st0'], writes=['st1'])
        p.op('dve', lambda E: E.scalar_tensor_tensor(
            junk_ap, r_ap, st_sb[:, 1:2], r_ap, ALU.add, ALU.mult,
            accum_out=st_sb[:, 2:3]),
            reads=r_keys + ['st1'], writes=[junk_key, 'st2'])
        p.op('dve', lambda E: E.tensor_scalar(
            st_sb[:, 3:4], st_sb[:, 2:3], 1.0 / C, EPS, ALU.mult, ALU.add),
            reads=['st2'], writes=['st3'])
        p.op('act', lambda E: E.activation(st_sb[:, 4:5], st_sb[:, 3:4],
                                           AF.Sqrt),
             reads=['st3'], writes=['st4'])
        p.op('dve', lambda E: E.reciprocal(st_sb[:, 5:6], st_sb[:, 4:5]),
             reads=['st4'], writes=['st5'])
        p.op('dve', lambda E: E.tensor_scalar(
            dst_ap, r_ap, st_sb[:, 1:2], st_sb[:, 5:6], ALU.add, ALU.mult),
            reads=r_keys + ['st1', 'st5'], writes=[dst_key])

    astack = ExitStack()

    def asb(name, shape, dtype):
        return astack.enter_context(nc.sbuf_tensor(name, shape, dtype))

    cos_sb = asb('cos_sb', [128, NTC * D], F32)
    sin_sb = asb('sin_sb', [128, NTC * D], F32)
    ma_sb = asb('ma_sb', [128, 128], F32)
    mb_sb = asb('mb_sb', [128, 128], F32)
    gate_row = asb('gate_row', [1, T], F32)

    p.dma('sp', cos_sb[:, :].rearrange('p (tc d) -> p tc d', d=D),
          cos_in.ap().rearrange('(tc p) d -> p tc d', p=128), writes=['cos'],
          stream='cos')
    p.dma('sp', sin_sb[:, :].rearrange('p (tc d) -> p tc d', d=D),
          sin_in.ap().rearrange('(tc p) d -> p tc d', p=128), writes=['sin'],
          stream='sin')
    p.dma('sp', ma_sb[:, :], ma_in.ap(), writes=['ma'], stream='ma')
    p.dma('sp', mb_sb[:, :], mb_in.ap(), writes=['mb'], stream='mb')

    with nc.sbuf_tensor('tmp_f', [128, 128], F32) as tmp_f, \
            nc.sbuf_tensor('tmp_b', [128, 128], BF16) as tmp_b:
        for src, dst, key in ((pk, pkT_sb, 'pkT'), (pv, pvT_sb, 'pvT')):
            p.dma('sp', tmp_f[0:R, :], src.ap(), writes=['tmpf'], stream='tmpf')
            p.op('dve', lambda E: E.tensor_copy(tmp_b[0:R, :], tmp_f[0:R, :]),
                 reads=['tmpf'], writes=['tmpb'])
            transpose_tile(tmp_b[0:R, :], ['tmpb'], dst[:, :], [key],
                           kpart=R, mfree=128)
        p.dma('sp', tmp_f[:, 0:R], uo.ap(), writes=['tmpf'], stream='tmpf')
        p.op('dve', lambda E: E.tensor_copy(tmp_b[:, 0:R], tmp_f[:, 0:R]),
             reads=['tmpf'], writes=['tmpb'])
        transpose_tile(tmp_b[:, 0:R], ['tmpb'], uoT_sb[:, :], ['uoT'],
                       kpart=128, mfree=R)
    p.barrier()

    if 'a0' in PHASES:
        with ExitStack() as stack:
            zb = lambda n, s, d: stack.enter_context(nc.sbuf_tensor(n, s, d))
            xld = [zb(f'a0_xld{i}', [128, C], F32) for i in range(2)]
            xbf = [zb(f'a0_xbf{i}', [128, C], BF16) for i in range(2)]
            xtc = [zb(f'a0_xtc{i}', [128, C], BF16) for i in range(2)]
            iw1f = zb('a0_iw1f', [128, C], F32)
            iw1b = zb('a0_iw1b', [128, C], BF16)
            iw1T = zb('a0_iw1T', [128, NCC * 512], BF16)
            w2c = zb('a0_w2c', [128, 4], F32)
            w2b = zb('a0_w2b', [128, 4], BF16)
            hinfT = zb('a0_hinfT', [128, 4 * 128], BF16)
            sig = zb('a0_sig', [128, 1], F32)
            gbft = zb('a0_gbft', [128, 1], BF16)
            ps_in = PS1[:, 0:128]
            ps_sc = PS2[:, 0:1]

            for fc in range(4):
                p.dma('sp', iw1f[:, :],
                      info_w1.ap()[fc * 128:(fc + 1) * 128, :],
                      writes=['iw1f'], stream='iw1f')
                p.op('dve', lambda E: E.tensor_copy(iw1b[:, :], iw1f[:, :]),
                     reads=['iw1f'], writes=['iw1b'])
                for cc in range(NCC):
                    transpose_tile(iw1b[:, cc * 128:(cc + 1) * 128], ['iw1b'],
                                   iw1T[:, cc * 512 + fc * 128:
                                        cc * 512 + (fc + 1) * 128],
                                   [f'iw1T_{cc}_{fc}'])
            for fc in range(4):
                p.dma('sp', w2c[:, fc:fc + 1],
                      info_w2.ap()[:, fc * 128:(fc + 1) * 128]
                      .rearrange('one p -> p one'),
                      writes=['w2c'], stream='w2c')
            p.op('dve', lambda E: E.tensor_copy(w2b[:, :], w2c[:, :]),
                 reads=['w2c'], writes=['w2b'])

            for tc in range(NTC):
                b = tc % 2
                p.dma('pool', xld[b][:, :],
                      x_in.ap()[tc * 128:(tc + 1) * 128, :],
                      writes=[f'a0_xld{b}'], stream=f'xld{b}')
                p.op('dve', lambda E, b=b: E.tensor_copy(xbf[b][:, :],
                                                         xld[b][:, :]),
                     reads=[f'a0_xld{b}'], writes=[f'a0_xbf{b}'])
                for cc in range(NCC):
                    transpose_tile(xbf[b][:, cc * 128:(cc + 1) * 128],
                                   [f'a0_xbf{b}'],
                                   xtc[b][:, cc * 128:(cc + 1) * 128],
                                   [f'a0_xtc{b}_{cc}'])
                p.dma('pool', xT_scr.ap()[tc], xtc[b][:, :],
                      reads=[f'a0_xtc{b}_{cc}' for cc in range(NCC)],
                      writes=[f'xT_{tc}'], stream='xT_w')
                for fc in range(4):
                    for cc in range(NCC):
                        p.op('pe', lambda E, b=b, cc=cc, fc=fc: E.matmul(
                            ps_in,
                            iw1T[:, cc * 512 + fc * 128: cc * 512 + (fc + 1) * 128],
                            xtc[b][:, cc * 128:(cc + 1) * 128],
                            start=(cc == 0), stop=(cc == NCC - 1)),
                            reads=[f'iw1T_{cc}_{fc}', f'a0_xtc{b}_{cc}'],
                            writes=['ps_in'])
                    p.op('act', lambda E, fc=fc: E.activation(
                        hinfT[:, fc * 128:(fc + 1) * 128], ps_in, AF.Gelu),
                        reads=['ps_in'], writes=[f'hinfT{fc}'])
                for fc in range(4):
                    p.op('pe', lambda E, fc=fc: E.matmul(
                        ps_sc, hinfT[:, fc * 128:(fc + 1) * 128],
                        w2b[:, fc:fc + 1], start=(fc == 0), stop=(fc == 3)),
                        reads=[f'hinfT{fc}', 'w2b'], writes=['ps_sc'])
                p.op('act', lambda E: E.activation(sig[:, :], ps_sc,
                                                   AF.Sigmoid),
                     reads=['ps_sc'], writes=['a0_sig'])
                p.op('dve', lambda E, tc=tc: E.tensor_single_scalar(
                    gate_sb[:, tc:tc + 1], sig[:, :], DELTA_I, ALU.is_gt),
                    reads=['a0_sig'], writes=['gate'])
                p.op('dve', lambda E, tc=tc: E.tensor_copy(
                    gbft[:, :], gate_sb[:, tc:tc + 1]),
                    reads=['gate'], writes=['a0_gbft'])
                transpose_tile(gbft[:, :], ['a0_gbft'],
                               gate_row[:, tc * 128:(tc + 1) * 128],
                               [f'gate_row{tc}'], kpart=128, mfree=1)
            maybe_dump('gate_dbg', gate_sb[:, :], [128, NTC], ['gate'])
            p.barrier()

    if 'a1' in PHASES:
        with ExitStack() as stack:
            zb = lambda n, s, d: stack.enter_context(nc.sbuf_tensor(n, s, d))
            kT_sb = zb('kT_sb', [128, HH * T], BF16)
            v_sb = zb('v_sb', [128, NTC * HH * 128], BF16)
            wblk = zb('wblk', [128, NCC * 512], BF16)
            wld = [zb(f'a1_wld{i}', [128, C], F32) for i in range(2)]
            wbf = [zb(f'a1_wbf{i}', [128, C], BF16) for i in range(2)]
            xtc1 = [zb(f'a1_xtc{i}', [128, C], BF16) for i in range(2)]
            rst = zb('a1_rst', [128, 512], F32)
            rst2 = zb('a1_rst2', [128, 512], F32)
            ropeo = zb('a1_ropeo', [128, 512], BF16)
            qTst = zb('a1_qTst', [128, 512], BF16)
            qT_h = zb('qT_h', [128, T], BF16)
            krT = zb('krT', [64, T], BF16)
            qrT = zb('qrT', [64, T], BF16)
            vT_h = zb('vT_h', [128, T], BF16)
            vr = zb('vr', [128, NTC * 64], BF16)
            vrst = zb('vrst', [64, 512], BF16)
            ptg = zb('ptg', [128, NTC * 512], BF16)
            ptl = zb('ptl', [128, 8 * 512], BF16)
            cgr_sb = zb('cgr_sb', [64, 512], BF16)
            ctxst = zb('ctxst', [128, 512], BF16)
            lg_row = zb('lg_row', [1, 512], F32)
            ll_row = zb('ll_row', [1, 512], F32)
            rlg_row = zb('rlg_row', [1, 512], F32)
            rll_row = zb('rll_row', [1, 512], F32)
            f2_row = zb('f2_row', [1, 512], F32)
            bf2_sb = zb('bf2_sb', [128, 512], F32)
            brl_sb = zb('brl_sb', [128, 512], F32)
            ps_qkv = [PS0[:, 0:512], PS0[:, 512:1024]]
            ps_a = PS1[:, 0:512]
            psrow = PS1[0:1, 512:1024]
            psbc = PS1[:, 512:1024]
            ps_ctx = PS2

            def rope_evac(psum, pskey, tc):
                cs = cos_sb[:, tc * D:(tc + 1) * D]
                sn = sin_sb[:, tc * D:(tc + 1) * D]
                for j in range(4):
                    o = j * 128
                    p.op('dve', lambda E, o=o: E.tensor_tensor(
                        rst[:, o:o + 128], psum[:, o:o + 128], cs, ALU.mult),
                        reads=[pskey, 'cos'], writes=[f'a1_rst{j}'])
                    p.op('dve', lambda E, o=o: E.tensor_tensor(
                        rst2[:, o:o + 128:2], psum[:, o + 1:o + 128:2],
                        sn[:, 0:128:2], ALU.mult),
                        reads=[pskey, 'sin'], writes=[f'a1_rst2e{j}'])
                    p.op('dve', lambda E, o=o: E.tensor_tensor(
                        rst2[:, o + 1:o + 128:2], psum[:, o:o + 128:2],
                        sn[:, 1:128:2], ALU.mult),
                        reads=[pskey, 'sin'], writes=[f'a1_rst2o{j}'])
                p.op('dve', lambda E: E.tensor_tensor(
                    ropeo[:, :], rst[:, :], rst2[:, :], ALU.add),
                    reads=[k for j in range(4) for k in
                           (f'a1_rst{j}', f'a1_rst2e{j}', f'a1_rst2o{j}')],
                    writes=['a1_ropeo'])

            for X in range(2):
                for proj, off in ((('v', 0), ('v', 512), ('k', 0), ('k', 512),
                                   ('q', 0), ('q', 512)) if A1A else ()):
                    base = {'q': 0, 'k': C, 'v': 2 * C}[proj] + X * 1024 + off
                    for s4 in range(4):
                        b = s4 % 2
                        p.dma('sp', wld[b][:, :],
                              w_qkv.ap()[base + s4 * 128: base + (s4 + 1) * 128, :],
                              writes=[f'a1_wld{b}'], stream=f'wld{b}')
                        p.op('dve', lambda E, b=b: E.tensor_copy(
                            wbf[b][:, :], wld[b][:, :]),
                            reads=[f'a1_wld{b}'], writes=[f'a1_wbf{b}'])
                        for cc in range(NCC):
                            transpose_tile(wbf[b][:, cc * 128:(cc + 1) * 128],
                                           [f'a1_wbf{b}'],
                                           wblk[:, cc * 512 + s4 * 128:
                                                cc * 512 + (s4 + 1) * 128],
                                           [f'wblk_{cc}_{s4}'])
                    for tc in range(NTC):
                        b = tc % 2
                        p.dma('pool', xtc1[b][:, :], xT_scr.ap()[tc],
                              reads=[f'xT_{tc}'], writes=[f'a1_xtc{b}'],
                              stream=f'xtc{b}')
                        for cc in range(NCC):
                            p.op('pe', lambda E, b=b, cc=cc: E.matmul(
                                ps_qkv[b],
                                xtc1[b][:, cc * 128:(cc + 1) * 128],
                                wblk[:, cc * 512:(cc + 1) * 512],
                                start=(cc == 0), stop=(cc == NCC - 1)),
                                reads=[f'a1_xtc{b}'] +
                                [f'wblk_{cc}_{s4}' for s4 in range(4)],
                                writes=[f'ps_qkv{b}'])
                        if proj == 'v':
                            p.op('dve', lambda E, b=b, tc=tc, off=off:
                                 E.tensor_copy(
                                     v_sb[:, tc * HH * 128 + off:
                                          tc * HH * 128 + off + 512],
                                     ps_qkv[b]),
                                 reads=[f'ps_qkv{b}'], writes=[f'v_{tc}_{off}'])
                        else:
                            rope_evac(ps_qkv[b], f'ps_qkv{b}', tc)
                            for j in range(4):
                                hh = off // 128 + j
                                if proj == 'k':
                                    transpose_tile(
                                        ropeo[:, j * 128:(j + 1) * 128],
                                        ['a1_ropeo'],
                                        kT_sb[:, hh * T + tc * 128:
                                              hh * T + (tc + 1) * 128],
                                        [f'kT_{hh}_{tc}'])
                                else:
                                    transpose_tile(
                                        ropeo[:, j * 128:(j + 1) * 128],
                                        ['a1_ropeo'],
                                        qTst[:, j * 128:(j + 1) * 128],
                                        [f'a1_qTst{j}'])
                            if proj == 'q':
                                h0 = X * HH + off // 128
                                p.dma('pool',
                                      q_scr.ap()[h0:h0 + 4, :,
                                                 tc * 128:(tc + 1) * 128]
                                      .rearrange('h p t -> p h t'),
                                      qTst[:, :].rearrange(
                                          'p (h t) -> p h t', t=128),
                                      reads=[f'a1_qTst{j}'
                                             for j in range(4)],
                                      writes=[f'q_{h0}_{tc}'],
                                      stream='q_w')
                if X == 0:
                    maybe_dump('v_dbg', v_sb[:, :], [128, NTC * HH * 128],
                               [f'v_{tc}_{o}' for tc in range(NTC)
                                for o in (0, 512)], dtype=BF16)
                    maybe_dump('kT_dbg', kT_sb[:, :], [128, HH * T],
                               [f'kT_{hh}_{tc}' for hh in range(HH)
                                for tc in range(NTC)], dtype=BF16)
                p.barrier()

                for hh in (range(HH) if A1B else []):
                    h_abs = X * HH + hh
                    h0 = X * HH + (hh // 4) * 4
                    p.dma('pool', qT_h[:, :], q_scr.ap()[h_abs],
                          reads=[f'q_{h0}_{tc}' for tc in range(NTC)],
                          writes=['qT_h'], stream='qTh')
                    for sgp in (range(NG) if A1B_KR else []):
                        p.op('pe', lambda E, sgp=sgp, hh=hh: E.matmul(
                            ps_a[0:64, :], pkT_sb[:, :],
                            kT_sb[:, hh * T + sgp * 512:
                                  hh * T + (sgp + 1) * 512],
                            start=True, stop=True),
                            reads=['pkT'], writes=['ps_a'])
                        p.op('dve', lambda E, sgp=sgp: E.tensor_copy(
                            krT[:, sgp * 512:(sgp + 1) * 512], ps_a[0:64, :]),
                            reads=['ps_a'], writes=['krT'])
                        p.op('pe', lambda E, sgp=sgp: E.matmul(
                            ps_a[0:64, :], pkT_sb[:, :],
                            qT_h[:, sgp * 512:(sgp + 1) * 512],
                            start=True, stop=True),
                            reads=['pkT', 'qT_h'], writes=['ps_a'])
                        p.op('dve', lambda E, sgp=sgp: E.tensor_copy(
                            qrT[:, sgp * 512:(sgp + 1) * 512], ps_a[0:64, :]),
                            reads=['ps_a'], writes=['qrT'])
                    voff = (hh // 4) * 512
                    for tc in (range(NTC) if A1B_VR else []):
                        transpose_tile(
                            v_sb[:, tc * HH * 128 + hh * 128:
                                 tc * HH * 128 + (hh + 1) * 128],
                            [f'v_{tc}_{voff}'],
                            vT_h[:, tc * 128:(tc + 1) * 128], [f'vT_{tc}'])
                    for sgp in (range(NG) if (A1B_VR and A1B_VR2) else []):
                        p.op('pe', lambda E, sgp=sgp: E.matmul(
                            ps_a[0:64, :], pvT_sb[:, :],
                            vT_h[:, sgp * 512:(sgp + 1) * 512],
                            start=True, stop=True),
                            reads=['pvT'] + [f'vT_{sgp * 4 + j}'
                                             for j in range(4)],
                            writes=['ps_a'])
                        p.op('dve', lambda E: E.tensor_copy(
                            vrst[:, :], ps_a[0:64, :]),
                            reads=['ps_a'], writes=['vrst'])
                        for j in range(4):
                            transpose_tile(
                                vrst[:, j * 128:(j + 1) * 128], ['vrst'],
                                vr[:, (sgp * 4 + j) * 64:
                                   (sgp * 4 + j + 1) * 64],
                                [f'vr_{sgp * 4 + j}'], kpart=64, mfree=128)

                    for g in (range(NG) if A1B_ROWS else []):
                        G0 = g * 512
                        g4 = g * 4
                        nch = min(4, NTC - g4)
                        assert nch == 4
                        for k2 in (range(NTC // 2) if A1B_G else []):
                            for j in range(2):
                                sc = 2 * k2 + j
                                q = sc % 4
                                p.op('pe', lambda E, sc=sc, q=q, G0=G0:
                                     E.matmul(
                                         PS0[:, q * 512:(q + 1) * 512],
                                         krT[:, sc * 128:(sc + 1) * 128],
                                         qrT[:, G0:G0 + 512],
                                         start=True, stop=True),
                                     reads=['krT', 'qrT'],
                                     writes=[f'PS0q{q}'])
                            q0 = (2 * k2) % 4
                            p.op('act', lambda E, k2=k2, q0=q0: E.activation(
                                ptg[:, 2 * k2 * 512: 2 * (k2 + 1) * 512],
                                PS0[:, q0 * 512:(q0 + 2) * 512],
                                AF.Exp, scale=SCALE_G),
                                reads=[f'PS0q{q0}', f'PS0q{q0 + 1}'],
                                writes=[f'ptg_{2 * k2}', f'ptg_{2 * k2 + 1}'])
                        for sc in (range(NTC) if A1B_G else []):
                            p.op('pe', lambda E, sc=sc: E.matmul(
                                psrow, ones128[:, :],
                                ptg[:, sc * 512:(sc + 1) * 512],
                                start=(sc == 0), stop=(sc == NTC - 1)),
                                reads=[f'ptg_{sc}', 'ones128'],
                                writes=['ps1b'])
                        p.op('dve', lambda E: E.tensor_copy(lg_row[:, :],
                                                            psrow),
                             reads=['ps1b'], writes=['lg_row'])
                        lvalid = []
                        for sc_abs in range(max(0, g4 - 4), g4 + nch):
                            slot = sc_abs - (g4 - 4)
                            lo = max(0, sc_abs - g4)
                            hi = min(nch - 1, sc_abs - g4 + 4)
                            lvalid.append((sc_abs, slot, lo, hi))
                        lvalid.sort(key=lambda v: v[0] != g4)
                        for n_i, (sc_abs, slot, lo, hi) in (
                                enumerate(lvalid) if A1B_L else []):
                            w = (hi - lo + 1) * 128
                            q = slot % 4
                            p.op('pe', lambda E, sc_abs=sc_abs, lo=lo, w=w,
                                 q=q, hh=hh, g4=g4: E.matmul(
                                     PS0[:, q * 512: q * 512 + w],
                                     kT_sb[:, hh * T + sc_abs * 128:
                                           hh * T + (sc_abs + 1) * 128],
                                     qT_h[:, (g4 + lo) * 128:
                                          (g4 + lo) * 128 + w],
                                     start=True, stop=True),
                                 reads=['qT_h'], writes=[f'PS0q{q}'])
                            if g4 <= sc_abs:
                                col = (sc_abs - g4 - lo) * 128
                                p.op('dve', lambda E, q=q, col=col:
                                     E.tensor_tensor(
                                         PS0[:, q * 512 + col:
                                             q * 512 + col + 128],
                                         PS0[:, q * 512 + col:
                                             q * 512 + col + 128],
                                         mb_sb[:, :], ALU.add),
                                     reads=['mb', f'PS0q{q}'],
                                     writes=[f'PS0q{q}'])
                            else:
                                col = (sc_abs + 4 - g4 - lo) * 128
                                p.op('dve', lambda E, q=q, col=col:
                                     E.tensor_tensor(
                                         PS0[:, q * 512 + col:
                                             q * 512 + col + 128],
                                         PS0[:, q * 512 + col:
                                             q * 512 + col + 128],
                                         ma_sb[:, :], ALU.add),
                                     reads=['ma', f'PS0q{q}'],
                                     writes=[f'PS0q{q}'])
                            p.op('act', lambda E, slot=slot, lo=lo, w=w, q=q:
                                 E.activation(
                                     ptl[:, slot * 512 + lo * 128:
                                         slot * 512 + lo * 128 + w],
                                     PS0[:, q * 512: q * 512 + w],
                                     AF.Exp, scale=SCALE),
                                 reads=[f'PS0q{q}'], writes=[f'ptl_{slot}'])
                        for n_i, (sc_abs, slot, lo, hi) in (
                                enumerate(lvalid) if A1B_L else []):
                            w = (hi - lo + 1) * 128
                            p.op('pe', lambda E, slot=slot, lo=lo, w=w,
                                 st=(n_i == 0),
                                 sp=(n_i == len(lvalid) - 1): E.matmul(
                                     psrow[:, lo * 128: lo * 128 + w],
                                     ones128[:, :],
                                     ptl[:, slot * 512 + lo * 128:
                                         slot * 512 + lo * 128 + w],
                                     start=st, stop=sp),
                                 reads=[f'ptl_{slot}', 'ones128'],
                                 writes=['ps1b'])
                        p.op('dve', lambda E: E.tensor_copy(ll_row[:, :],
                                                            psrow),
                             reads=['ps1b'], writes=['ll_row'])
                        p.op('dve', lambda E: E.reciprocal(rlg_row[:, :],
                                                           lg_row[:, :]),
                             reads=['lg_row'], writes=['rlg_row'])
                        p.op('dve', lambda E: E.reciprocal(rll_row[:, :],
                                                           ll_row[:, :]),
                             reads=['ll_row'], writes=['rll_row'])
                        p.op('dve', lambda E, G0=G0: E.tensor_tensor(
                            f2_row[:, :], rlg_row[:, :],
                            gate_row[:, G0:G0 + 512], ALU.mult),
                            reads=['rlg_row'] + [f'gate_row{g4 + c}'
                                                 for c in range(4)],
                            writes=['f2_row'])
                        p.op('dve', lambda E: E.tensor_tensor(
                            f2_row[:, :], f2_row[:, :], ll_row[:, :],
                            ALU.mult),
                            reads=['f2_row', 'll_row'], writes=['f2_row'])
                        p.op('pe', lambda E: E.matmul(
                            psbc, ones1f[:, :], f2_row[:, :],
                            start=True, stop=True),
                            reads=['f2_row', 'ones1f'], writes=['ps1b'])
                        p.op('dve', lambda E: E.tensor_copy(bf2_sb[:, :],
                                                            psbc),
                             reads=['ps1b'], writes=['bf2'])
                        p.op('pe', lambda E: E.matmul(
                            psbc, ones1f[:, :], rll_row[:, :],
                            start=True, stop=True),
                            reads=['rll_row', 'ones1f'], writes=['ps1b'])
                        p.op('act', lambda E: E.copy(brl_sb[:, :], psbc),
                             reads=['ps1b'], writes=['brl'])
                        for sc in (range(NTC) if A1B_G else []):
                            p.op('pe', lambda E, sc=sc: E.matmul(
                                ps_a[0:64, :],
                                vr[:, sc * 64:(sc + 1) * 64],
                                ptg[:, sc * 512:(sc + 1) * 512],
                                start=(sc == 0), stop=(sc == NTC - 1)),
                                reads=[f'vr_{sc}', f'ptg_{sc}'],
                                writes=['ps_a'])
                        p.op('dve', lambda E: E.tensor_tensor(
                            cgr_sb[:, :], ps_a[0:64, :], bf2_sb[0:64, :],
                            ALU.mult),
                            reads=['ps_a', 'bf2'], writes=['cgr'])
                        p.op('pe', lambda E: E.matmul(
                            ps_ctx[:, :], uoT_sb[:, :], cgr_sb[:, :],
                            start=True, stop=not A1B_L),
                            reads=['uoT', 'cgr'], writes=['ps_ctx'])
                        for n_i, (sc_abs, slot, lo, hi) in (
                                enumerate(lvalid) if A1B_L else []):
                            p.op('pe', lambda E, sc_abs=sc_abs, slot=slot,
                                 lo=lo, hi=hi, hh=hh,
                                 last=(n_i == len(lvalid) - 1): E.matmul(
                                ps_ctx[:, lo * 128:(hi + 1) * 128],
                                v_sb[:, sc_abs * HH * 128 + hh * 128:
                                     sc_abs * HH * 128 + (hh + 1) * 128],
                                ptl[:, slot * 512 + lo * 128:
                                    slot * 512 + (hi + 1) * 128],
                                start=False, stop=last),
                                reads=[f'ptl_{slot}'], writes=['ps_ctx'])
                        p.op('dve', lambda E: E.tensor_tensor(
                            ctxst[:, :], ps_ctx[:, :], brl_sb[:, :], ALU.mult),
                            reads=['ps_ctx', 'brl'], writes=['ctxst'])
                        p.dma('pool', ctx_scr.ap()[h_abs][:, G0:G0 + 512],
                              ctxst[:, :], reads=['ctxst'],
                              writes=[f'ctx_{h_abs}_{g}'], stream='ctx_w')
                p.barrier()
    astack.close()

    if 'a2' in PHASES:
        with ExitStack() as stack:
            zb = lambda n, s, d: stack.enter_context(nc.sbuf_tensor(n, s, d))
            woT = zb('woT', [128, NCC * C], BF16)
            wld2 = [zb(f'a2_wld{i}', [128, C], F32) for i in range(2)]
            wbf2 = [zb(f'a2_wbf{i}', [128, C], BF16) for i in range(2)]
            ctxc = [zb(f'a2_ctxc{i}', [128, C], BF16) for i in range(2)]
            xrow = [zb(f'a2_xrow{i}', [128, C], F32) for i in range(2)]
            r_sb = zb('a2_r', [128, C], F32)
            x1f = zb('a2_x1f', [128, C], F32)
            x1b = zb('a2_x1b', [128, C], BF16)
            x1Tst = zb('a2_x1Tst', [128, C], BF16)
            junk = zb('a2_junk', [128, C], F32)

            for s in range(NCC):
                b = s % 2
                p.dma('sp', wld2[b][:, :], w_o.ap()[s * 128:(s + 1) * 128, :],
                      writes=[f'a2_wld{b}'], stream=f'wld{b}')
                p.op('dve', lambda E, b=b: E.tensor_copy(wbf2[b][:, :],
                                                         wld2[b][:, :]),
                     reads=[f'a2_wld{b}'], writes=[f'a2_wbf{b}'])
                for cc in range(NCC):
                    transpose_tile(wbf2[b][:, cc * 128:(cc + 1) * 128],
                                   [f'a2_wbf{b}'],
                                   woT[:, cc * C + s * 128:
                                       cc * C + (s + 1) * 128],
                                   [f'woT_{cc}_{s}'])

            for tc in range(NTC):
                b = tc % 2
                p.dma('pool', ctxc[b][:, :].rearrange('p (h t) -> p h t', t=128),
                      ctx_scr.ap()[:, :, tc * 128:(tc + 1) * 128]
                      .rearrange('h p t -> p h t'),
                      reads=[f'ctx_{h}_{tc // 4}' for h in range(H)],
                      writes=[f'a2_ctxc{b}'], stream=f'ctxc{b}')
                p.dma('sp', xrow[b][:, :], x_in.ap()[tc * 128:(tc + 1) * 128, :],
                      writes=[f'a2_xrow{b}'], stream=f'xrow{b}')
                for cog in range(4):
                    for cc in range(NCC):
                        p.op('pe', lambda E, b=b, cc=cc, cog=cog: E.matmul(
                            PS0[:, cog * 512:(cog + 1) * 512],
                            ctxc[b][:, cc * 128:(cc + 1) * 128],
                            woT[:, cc * C + cog * 512: cc * C + (cog + 1) * 512],
                            start=(cc == 0), stop=(cc == NCC - 1)),
                            reads=[f'a2_ctxc{b}'] +
                            [f'woT_{cc}_{s}' for s in range(cog * 4, cog * 4 + 4)],
                            writes=['PS0a'])
                p.op('dve', lambda E, b=b: E.tensor_tensor(
                    r_sb[:, :], PS0[:, :], xrow[b][:, :], ALU.add),
                    reads=['PS0a', f'a2_xrow{b}'], writes=['a2_r'])
                ln_epilogue(r_sb[:, :], ['a2_r'], x1f[:, :], 'a2_x1f',
                            junk[:, :], 'a2_junk')
                p.op('dve', lambda E: E.tensor_copy(x1b[:, :], x1f[:, :]),
                     reads=['a2_x1f'], writes=['a2_x1b'])
                p.dma('sp', x1_scr.ap()[tc * 128:(tc + 1) * 128, :], x1f[:, :],
                      reads=['a2_x1f'], writes=[f'x1_{tc}'], stream='x1_w')
                for cc in range(NCC):
                    transpose_tile(x1b[:, cc * 128:(cc + 1) * 128],
                                   ['a2_x1b'],
                                   x1Tst[:, cc * 128:(cc + 1) * 128],
                                   [f'a2_x1Tst{cc}'])
                p.dma('pool',
                      x1T_scr.ap()[:, :, tc * 128:(tc + 1) * 128]
                      .rearrange('cc p t -> p cc t'),
                      x1Tst[:, :].rearrange('p (cc t) -> p cc t', t=128),
                      reads=[f'a2_x1Tst{cc}' for cc in range(NCC)],
                      writes=[f'x1T_{tc}'], stream='x1T_w')
            p.barrier()

    if 'f' in PHASES:
        with ExitStack() as stack:
            zb = lambda n, s, d: stack.enter_context(nc.sbuf_tensor(n, s, d))
            TH = T // 2
            NTC_H = TH // 128
            NTT = TH // 512
            routT = zb('f_routT', [128, NCC * 16], BF16)
            shw = zb('f_shw', [128, NCC], BF16)
            with nc.sbuf_tensor('f_rTl', [16, C], F32) as rTl, \
                    nc.sbuf_tensor('f_rTb', [16, C], BF16) as rTb, \
                    nc.sbuf_tensor('f_shwf', [128, NCC], F32) as shwf:
                p.dma('sp', rTl[:, :], router_w.ap(), writes=['f_rTl'], stream='rTl')
                p.op('dve', lambda E: E.tensor_copy(rTb[:, :], rTl[:, :]),
                     reads=['f_rTl'], writes=['f_rTb'])
                for cc in range(NCC):
                    transpose_tile(rTb[0:16, cc * 128:(cc + 1) * 128],
                                   ['f_rTb'],
                                   routT[:, cc * 16:(cc + 1) * 16],
                                   [f'routT{cc}'], kpart=16, mfree=128)
                p.dma('sp', shwf[:, :].rearrange('p (cc one) -> p cc one', one=1),
                      sh_gate_w.ap().rearrange('(cc p) one -> p cc one', p=128),
                      writes=['f_shwf'], slow=True, stream='shwf')
                p.op('dve', lambda E: E.tensor_copy(shw[:, :], shwf[:, :]),
                     reads=['f_shwf'], writes=['f_shw'])
                p.barrier()

            x1Th = zb('x1Th', [128, NCC * TH], BF16)
            acc = zb('acc', [128, NTC_H * C], F32)
            aT = zb('aT', [128, NIC * TH], BF16)
            wgl = [zb(f'f_wgl{i}', [128, 1024], F32) for i in range(2)]
            wgb = [[zb(f'f_wgb{i}_{j}', [128, C], BF16) for j in range(2)]
                   for i in range(2)]
            wdb = zb('f_wdb', [128, NIC * C], BF16)
            wdl = [zb(f'f_wdl{i}', [128, 512], F32) for i in range(2)]
            prb = zb('f_prb', [128, 16], F32)
            prw = zb('f_prw', [128, 16], F32)
            j16 = zb('f_j16', [128, 16], F32)
            x1r = [zb('f_x1r0', [128, C], F32)]
            sgst = zb('f_sgst', [128, 512], BF16)
            uust = zb('f_uust', [128, 512], BF16)
            ps_g = [PS0[:, 0:512], PS0[:, 512:1024]]
            ps_u = [PS0[:, 1024:1536], PS0[:, 1536:2048]]
            ps_d = [PS1[:, 0:512], PS1[:, 512:1024]]
            ps_r = PS2[:, 0:16]
            ps_s = PS2[:, 128:129]

            for half in range(2):
                t0 = half * TH
                tc0 = t0 // 128
                for cc in range(NCC):
                    p.dma('pool', x1Th[:, cc * TH:(cc + 1) * TH],
                          x1T_scr.ap()[cc][:, t0:t0 + TH],
                          reads=[f'x1T_{tc0 + i}' for i in range(NTC_H)],
                          writes=[f'x1Th_{cc}'], stream='x1Th')
                x1Tkeys = [f'x1Th_{cc}' for cc in range(NCC)]
                for tl in range(NTC_H):
                    tc = tc0 + tl
                    for cc in range(NCC):
                        p.op('pe', lambda E, cc=cc, tl=tl: E.matmul(
                            ps_r, x1Th[:, cc * TH + tl * 128:
                                       cc * TH + (tl + 1) * 128],
                            routT[:, cc * 16:(cc + 1) * 16],
                            start=(cc == 0), stop=(cc == NCC - 1)),
                            reads=[f'x1Th_{cc}', f'routT{cc}'],
                            writes=['ps_r'])
                    for cc in range(NCC):
                        p.op('pe', lambda E, cc=cc, tl=tl: E.matmul(
                            ps_s, x1Th[:, cc * TH + tl * 128:
                                       cc * TH + (tl + 1) * 128],
                            shw[:, cc:cc + 1],
                            start=(cc == 0), stop=(cc == NCC - 1)),
                            reads=[f'x1Th_{cc}', 'f_shw'], writes=['ps_s'])
                    p.op('act', lambda E, tc=tc: E.activation(
                        sg_sb[:, tc:tc + 1], ps_s, AF.Sigmoid),
                        reads=['ps_s'], writes=['sg'])
                    p.op('dve', lambda E: E.reduce_max(
                        st_sb[:, 0:1], ps_r, axis=AX.X, negate=True),
                        reads=['ps_r'], writes=['st0'])
                    p.op('act', lambda E: E.activation(
                        prb[:, :], ps_r, AF.Exp, bias=st_sb[:, 0:1]),
                        reads=['ps_r', 'st0'], writes=['f_prb'])
                    p.op('dve', lambda E: E.tensor_copy(prw[:, :], prb[:, :]),
                         reads=['f_prb'], writes=['f_prw'])
                    for it in range(4):
                        p.op('dve', lambda E: E.reduce_max(
                            st_sb[:, 1:2], prw[:, :], axis=AX.X),
                            reads=['f_prw'], writes=['st1'])
                        if it < 3:
                            p.op('dve', lambda E: E.tensor_single_scalar(
                                j16[:, :], prw[:, :], st_sb[:, 1:2],
                                ALU.is_ge), reads=['f_prw', 'st1'],
                                writes=['f_j16'])
                            p.op('dve', lambda E: E.scalar_tensor_tensor(
                                prw[:, :], j16[:, :], NEG, prw[:, :],
                                ALU.mult, ALU.add),
                                reads=['f_j16', 'f_prw'], writes=['f_prw'])
                    p.op('dve', lambda E: E.tensor_single_scalar(
                        prw[:, :], prb[:, :], st_sb[:, 1:2], ALU.is_ge),
                        reads=['f_prb', 'st1'], writes=['f_prw'])
                    p.op('dve', lambda E: E.tensor_tensor(
                        prw[:, :], prw[:, :], prb[:, :], ALU.mult),
                        reads=['f_prw', 'f_prb'], writes=['f_prw'])
                    p.op('dve', lambda E: E.reduce_sum(
                        st_sb[:, 2:3], prw[:, :], axis=AX.X),
                        reads=['f_prw'], writes=['st2'])
                    p.op('dve', lambda E: E.reciprocal(
                        st_sb[:, 3:4], st_sb[:, 2:3]),
                        reads=['st2'], writes=['st3'])
                    p.op('dve', lambda E, tc=tc: E.tensor_scalar_mul(
                        comb_sb[:, tc * 16:(tc + 1) * 16], prw[:, :],
                        st_sb[:, 3:4]),
                        reads=['f_prw', 'st3'], writes=['comb'])

                experts = [('moe', e) for e in range(E_MOE)] + \
                          [('sh', v) for v in range(4)]
                for ei, (kind, e) in enumerate(experts):
                    for ic in range(NIC):
                        wp = ic % 2
                        for gi, (wt, sh_wt) in enumerate(
                                ((w_gate, sh_gate), (w_up, sh_up))):
                            if kind == 'moe':
                                wsrc = wt.ap()[e][ic].rearrange(
                                    'p cc i -> p (cc i)')
                            else:
                                wsrc = sh_wt.ap()[e * NIC + ic].rearrange(
                                    'p cc i -> p (cc i)')
                            for hf in range(2):
                                p.dma('sp', wgl[gi][:, :],
                                      wsrc[:, hf * 1024:(hf + 1) * 1024],
                                      writes=[f'f_wgl{gi}'],
                                      stream=f'wld{gi}')
                                p.op('dve', lambda E, gi=gi, wp=wp, hf=hf:
                                     E.tensor_copy(
                                         wgb[gi][wp][:, hf * 1024:
                                                     (hf + 1) * 1024],
                                         wgl[gi][:, :]),
                                     reads=[f'f_wgl{gi}'],
                                     writes=[f'f_wgb{gi}_{wp}'])
                        for tt in range(NTT):
                            pb = (ic * NTT + tt) % 2
                            for cc in range(NCC):
                                p.op('pe', lambda E, cc=cc, tt=tt, pb=pb,
                                     wp=wp: E.matmul(
                                         ps_g[pb],
                                         wgb[0][wp][:, cc * 128:(cc + 1) * 128],
                                         x1Th[:, cc * TH + tt * 512:
                                              cc * TH + (tt + 1) * 512],
                                         start=(cc == 0),
                                         stop=(cc == NCC - 1)),
                                     reads=[f'f_wgb0_{wp}', f'x1Th_{cc}'],
                                     writes=[f'ps_g{pb}'])
                            for cc in range(NCC):
                                p.op('pe', lambda E, cc=cc, tt=tt, pb=pb,
                                     wp=wp: E.matmul(
                                         ps_u[pb],
                                         wgb[1][wp][:, cc * 128:(cc + 1) * 128],
                                         x1Th[:, cc * TH + tt * 512:
                                              cc * TH + (tt + 1) * 512],
                                         start=(cc == 0),
                                         stop=(cc == NCC - 1)),
                                     reads=[f'f_wgb1_{wp}', f'x1Th_{cc}'],
                                     writes=[f'ps_u{pb}'])
                            p.op('act', lambda E, pb=pb: E.activation(
                                sgst[:, :], ps_g[pb], AF.Silu),
                                reads=[f'ps_g{pb}'], writes=['f_sgst'])
                            p.op('dve', lambda E, pb=pb: E.tensor_copy(
                                uust[:, :], ps_u[pb]),
                                reads=[f'ps_u{pb}'], writes=['f_uust'])
                            p.op('dve', lambda E, ic=ic, tt=tt: E.tensor_tensor(
                                aT[:, ic * TH + tt * 512:
                                   ic * TH + (tt + 1) * 512],
                                sgst[:, :], uust[:, :], ALU.mult),
                                reads=['f_sgst', 'f_uust'],
                                writes=[f'aT_{ic}_{tt}'])
                    for ic in range(NIC):
                        if kind == 'moe':
                            wsrc = w_down.ap()[e][ic * 128:(ic + 1) * 128, :]
                        else:
                            i0 = e * I_MOE + ic * 128
                            wsrc = sh_down.ap()[i0:i0 + 128, :]
                        for hf in range(4):
                            gi = hf % 2
                            p.dma('act', wdl[gi][:, :],
                                  wsrc[:, hf * 512:(hf + 1) * 512],
                                  writes=[f'f_wdl{gi}'], stream=f'wdn{gi}')
                            p.op('dve', lambda E, gi=gi, ic=ic, hf=hf:
                                 E.tensor_copy(
                                     wdb[:, ic * C + hf * 512:
                                         ic * C + (hf + 1) * 512],
                                     wdl[gi][:, :]),
                                 reads=[f'f_wdl{gi}'],
                                 writes=[f'f_wdb_{ic}_{hf}'])
                    for ts in range(NTC_H):
                        tc = tc0 + ts
                        if kind == 'moe':
                            scol = comb_sb[:, tc * 16 + e: tc * 16 + e + 1]
                            skey = 'comb'
                        else:
                            scol = sg_sb[:, tc:tc + 1]
                            skey = 'sg'
                        for cog in range(4):
                            pb = (ts * 4 + cog) % 2
                            for ic in range(NIC):
                                p.op('pe', lambda E, ic=ic, ts=ts, cog=cog,
                                     pb=pb: E.matmul(
                                         ps_d[pb],
                                         aT[:, ic * TH + ts * 128:
                                            ic * TH + (ts + 1) * 128],
                                         wdb[:, ic * C + cog * 512:
                                             ic * C + (cog + 1) * 512],
                                         start=(ic == 0),
                                         stop=(ic == NIC - 1)),
                                     reads=[f'f_wdb_{ic}_{cog}',
                                            f'aT_{ic}_{ts // 4}'],
                                     writes=[f'ps_d{pb}'])
                            if ei == 0:
                                p.op('dve', lambda E, ts=ts, cog=cog, pb=pb,
                                     scol=scol: E.tensor_scalar_mul(
                                         acc[:, ts * C + cog * 512:
                                             ts * C + (cog + 1) * 512],
                                         ps_d[pb], scol),
                                     reads=[f'ps_d{pb}', skey],
                                     writes=[f'acc_{ts}_{cog}'])
                            else:
                                p.op('dve', lambda E, ts=ts, cog=cog, pb=pb,
                                     scol=scol: E.scalar_tensor_tensor(
                                         acc[:, ts * C + cog * 512:
                                             ts * C + (cog + 1) * 512],
                                         ps_d[pb], scol,
                                         acc[:, ts * C + cog * 512:
                                             ts * C + (cog + 1) * 512],
                                         ALU.mult, ALU.add),
                                     reads=[f'ps_d{pb}', skey,
                                            f'acc_{ts}_{cog}'],
                                     writes=[f'acc_{ts}_{cog}'])

                for ts in range(NTC_H):
                    tc = tc0 + ts
                    b = 0
                    p.dma('pool', x1r[b][:, :],
                          x1_scr.ap()[tc * 128:(tc + 1) * 128, :],
                          reads=[f'x1_{tc}'], writes=[f'f_x1r{b}'],
                          stream='x1row')
                    acck = [f'acc_{ts}_{cog}' for cog in range(4)]
                    p.op('dve', lambda E, ts=ts, b=b: E.tensor_tensor(
                        acc[:, ts * C:(ts + 1) * C],
                        acc[:, ts * C:(ts + 1) * C],
                        x1r[b][:, :], ALU.add),
                        reads=acck + [f'f_x1r{b}'], writes=acck)
                    ln_epilogue(acc[:, ts * C:(ts + 1) * C], acck,
                                x1r[b][:, :], f'f_x1r{b}',
                                x1r[b][:, :], f'f_x1r{b}')
                    p.dma('pool', out.ap()[tc * 128:(tc + 1) * 128, :],
                          x1r[b][:, :],
                          reads=[f'f_x1r{b}'], writes=[f'out_{tc}'],
                          stream='out_w')
                p.barrier()

        maybe_dump('comb_dbg', comb_sb[:, :], [128, NTC * E_MOE], ['comb'])
        maybe_dump('sg_dbg', sg_sb[:, :], [128, NTC], ['sg'])

    p.finalize()
    p.emit()
    return nc


def _zero_init_acc():
    pass


def _tile_moe(w):
    E, Cx, Ix = w.shape
    return np.ascontiguousarray(
        w.reshape(E, NCC, 128, Ix // 128, 128).transpose(0, 3, 2, 1, 4))


def _tile_sh(w):
    Cx, Ix = w.shape
    return np.ascontiguousarray(
        w.reshape(NCC, 128, Ix // 128, 128).transpose(2, 1, 0, 3))


def kernel(L0_attn_x, L0_attn_w_qkv, L0_attn_w_o, L0_attn_pk, L0_attn_pv,
           L0_attn_uo, L0_attn_info_w1, L0_attn_info_w2, L0_ffn_router_w,
           L0_ffn_w_gate, L0_ffn_w_up, L0_ffn_w_down, L0_ffn_sh_gate,
           L0_ffn_sh_up, L0_ffn_sh_down, L0_ffn_sh_gate_w):
    xs = np.asarray(L0_attn_x)
    T = xs.shape[1]
    nc = build(T)
    consts = make_consts(T)
    common = dict(
        w_qkv=np.asarray(L0_attn_w_qkv), w_o=np.asarray(L0_attn_w_o),
        pk=np.asarray(L0_attn_pk), pv=np.asarray(L0_attn_pv),
        uo=np.asarray(L0_attn_uo), info_w1=np.asarray(L0_attn_info_w1),
        info_w2=np.asarray(L0_attn_info_w2),
        router_w=np.asarray(L0_ffn_router_w),
        w_gate=_tile_moe(np.asarray(L0_ffn_w_gate)),
        w_up=_tile_moe(np.asarray(L0_ffn_w_up)),
        w_down=np.asarray(L0_ffn_w_down),
        sh_gate=_tile_sh(np.asarray(L0_ffn_sh_gate)),
        sh_up=_tile_sh(np.asarray(L0_ffn_sh_up)),
        sh_down=np.asarray(L0_ffn_sh_down),
        sh_gate_w=np.asarray(L0_ffn_sh_gate_w), **consts)
    B = xs.shape[0]
    in_maps = [dict(common, x=np.ascontiguousarray(xs[b])) for b in range(B)]
    res = run_bass_kernel_spmd(nc, in_maps, core_ids=list(range(B)))
    return np.stack([res.results[b]['out'] for b in range(B)], axis=0)


# version 09
# speedup vs baseline: 1.1460x; 1.1460x over previous
import math
import bisect
from contextlib import ExitStack
import numpy as np
import ml_dtypes
import concourse.bass as bass
import concourse.mybir as mybir
from concourse.bass_utils import run_bass_kernel_spmd

F32 = mybir.dt.float32
BF16 = mybir.dt.bfloat16
AF = mybir.ActivationFunctionType
ALU = mybir.AluOpType
AX = mybir.AxisListType

ENG2ATTR = {'pe': 'tensor', 'dve': 'vector', 'act': 'scalar',
            'pool': 'gpsimd', 'sp': 'sync'}

C = 2048
H = 16
D = 128
R = 64
DELTA_I = 0.75
SCALE = 1.0 / math.sqrt(D)
SCALE_G = SCALE * D / R
NEG = -1.0e30
EPS = 1e-5
E_MOE = 16
I_MOE = 1408
I_SH = 5632
NIC = I_MOE // 128
NCC = C // 128

DUMP = set()
TSLOTS = 8
A1B = True
A1B_G = True
A1B_L = True
A1B_ROWS = True
A1A = True
A1B_KR = True
A1B_VR = True
A1B_VR2 = True
PHASES = {'a0', 'a1', 'a2', 'f'}


class Prog:

    def __init__(self, nc):
        self.nc = nc
        self.engines = list(ENG2ATTR)
        self.sem = {e: nc.alloc_semaphore(f'S_{e}') for e in self.engines}
        self.ops = {e: [] for e in self.engines}
        self.inc_idx = {e: [] for e in self.engines}
        self.inc_val = {e: [] for e in self.engines}
        self.count = {e: 0 for e in self.engines}
        self.clock = {e: {} for e in self.engines}
        self.res = {}
        self.pending = {e: [] for e in self.engines}
        self.streams = {}
        self.pending_tp = set()
        self.tp_flush = None

    def _stream(self, sname):
        sname = 'q_' + sname
        if sname not in self.streams:
            self.sem[sname] = self.nc.alloc_semaphore(f'S_{sname}')
            self.streams[sname] = 0
        return sname

    def _observe(self, peng, pidx):
        o = self.ops[peng][pidx]
        if o['dma']:
            return (o['stream'], self.streams[o['stream']])
        ii = self.inc_idx[peng]
        j = bisect.bisect_left(ii, pidx)
        if j < len(ii):
            return (peng, self.inc_val[peng][j])
        ops = self.ops[peng]
        k = len(ops) - 1
        while k >= 0 and (ops[k]['fn'] is None or ops[k]['dma']):
            k -= 1
        assert k >= 0 and k >= pidx, f'no op to observe {peng}:{pidx}'
        self.count[peng] += 1
        ops[k]['inc'] = 1
        self.inc_idx[peng].append(k)
        self.inc_val[peng].append(self.count[peng])
        return (peng, self.count[peng])

    def op(self, eng, fn, reads=(), writes=(), dma=False, stream=None):
        if self.pending_tp and ((set(reads) | set(writes)) & self.pending_tp):
            self.tp_flush()
        if dma:
            stream = self._stream(stream or eng)
        idx = len(self.ops[eng])
        waits = {}

        def need(tok):
            peng, pidx = tok
            if peng == eng and peng == 'pe':
                return
            sname, val = self._observe(peng, pidx)
            if self.clock[eng].get(sname, 0) >= val:
                return
            waits[sname] = max(waits.get(sname, 0), val)

        for tok in self.pending[eng]:
            need(tok)
        self.pending[eng] = []
        for r in reads:
            st = self.res.get(r)
            if st and st['w'] is not None:
                need(st['w'])
        for w in writes:
            st = self.res.get(w)
            if st:
                if st['w'] is not None:
                    need(st['w'])
                for t in st['r']:
                    need(t)
        tok = (eng, idx)
        for r in reads:
            self.res.setdefault(r, {'w': None, 'r': []})['r'].append(tok)
        for w in writes:
            self.res[w] = {'w': tok, 'r': []}
        for sname, val in waits.items():
            self.clock[eng][sname] = val
        o = {'fn': fn, 'waits': sorted(waits.items()), 'inc': None, 'dma': dma,
             'stream': stream}
        self.ops[eng].append(o)
        if dma:
            self.streams[stream] += 16
            o['inc'] = 16
        return tok

    def dma(self, q, out_ap, in_ap, reads=(), writes=(), slow=False,
            stream=None):
        return self.op(q, lambda E: E.dma_start(
            out_ap, in_ap, allow_slow_non_contiguous=slow),
            reads=reads, writes=writes, dma=True, stream=stream)

    def barrier(self):
        if self.pending_tp:
            self.tp_flush()
        toks = []
        for e in self.engines:
            ops = self.ops[e]
            k = len(ops) - 1
            while k >= 0 and ops[k]['fn'] is None:
                k -= 1
            if k >= 0:
                toks.append((e, k))
        for e in self.engines:
            self.pending[e] = list(self.pending[e]) + \
                [t for t in toks if t[0] != e]

    def wait_for(self, eng, tokens):
        waits = {}
        for (peng, pidx) in tokens:
            sname, val = self._observe(peng, pidx)
            if self.clock[eng].get(sname, 0) >= val:
                continue
            waits[sname] = max(waits.get(sname, 0), val)
        for sname, val in waits.items():
            self.clock[eng][sname] = val
        self.ops[eng].append({'fn': None, 'waits': sorted(waits.items()),
                              'inc': None, 'dma': False})

    def finalize(self):
        if self.pending_tp:
            self.tp_flush()
        waits = {}
        for sname, total in self.streams.items():
            if self.clock['sp'].get(sname, 0) < total:
                waits[sname] = total
                self.clock['sp'][sname] = total
        self.ops['sp'].append({'fn': None, 'waits': sorted(waits.items()),
                               'inc': None, 'dma': False, 'stream': None})

    def emit(self):
        semh = self.sem
        with self.nc.Block() as block:
            for eng in self.engines:
                ops = self.ops[eng]
                if not ops:
                    continue

                def make_body(ops=ops, eng=eng):
                    def body(E):
                        for o in ops:
                            for sname, val in o['waits']:
                                E.wait_ge(semh[sname], val)
                            if o['fn'] is not None:
                                ins = o['fn'](E)
                                if o['inc'] is not None:
                                    tgt = o['stream'] if o['dma'] else eng
                                    ins.then_inc(semh[tgt], o['inc'])
                    return body
                getattr(block, ENG2ATTR[eng])(make_body())


def make_consts(T):
    inv_freq = 1.0 / (10000.0 ** (np.arange(0, D, 2, dtype=np.float64) / D))
    t = np.arange(T, dtype=np.float64)
    freqs = np.outer(t, inv_freq)
    emb = np.concatenate([freqs, freqs], axis=-1)
    cos = np.cos(emb).astype(np.float32)
    sin = np.sin(emb).astype(np.float32)
    sin_s = sin.copy()
    sin_s[:, 0::2] = -sin_s[:, 0::2]
    j = np.arange(128)[:, None]
    i = np.arange(128)[None, :]
    mask_at = np.where(i >= j, NEG, 0.0).astype(np.float32)
    mask_bt = np.where(j > i, NEG, 0.0).astype(np.float32)
    ident = np.eye(128, dtype=ml_dtypes.bfloat16)
    return dict(cos_tab=cos, sin_tab=sin_s, mask_at=mask_at, mask_bt=mask_bt,
                ident=ident)


def build(T=2048):
    nc = bass.Bass('TRN2', target_bir_lowering=False)
    NTC = T // 128
    NG = T // 512
    HH = H // 2

    def dram(name, shape, dtype):
        kind = 'ExternalOutput' if name in DUMP else 'Internal'
        return nc.dram_tensor(name, shape, dtype, kind=kind)

    x_in = nc.dram_tensor('x', [T, C], F32, kind='ExternalInput')
    w_qkv = nc.dram_tensor('w_qkv', [3 * C, C], F32, kind='ExternalInput')
    w_o = nc.dram_tensor('w_o', [C, C], F32, kind='ExternalInput')
    pk = nc.dram_tensor('pk', [R, D], F32, kind='ExternalInput')
    pv = nc.dram_tensor('pv', [R, D], F32, kind='ExternalInput')
    uo = nc.dram_tensor('uo', [D, R], F32, kind='ExternalInput')
    info_w1 = nc.dram_tensor('info_w1', [512, C], F32, kind='ExternalInput')
    info_w2 = nc.dram_tensor('info_w2', [1, 512], F32, kind='ExternalInput')
    router_w = nc.dram_tensor('router_w', [E_MOE, C], F32, kind='ExternalInput')
    w_gate = nc.dram_tensor('w_gate', [E_MOE, NIC, 128, NCC, 128], F32, kind='ExternalInput')
    w_up = nc.dram_tensor('w_up', [E_MOE, NIC, 128, NCC, 128], F32, kind='ExternalInput')
    w_down = nc.dram_tensor('w_down', [E_MOE, I_MOE, C], F32, kind='ExternalInput')
    sh_gate = nc.dram_tensor('sh_gate', [4 * NIC, 128, NCC, 128], F32, kind='ExternalInput')
    sh_up = nc.dram_tensor('sh_up', [4 * NIC, 128, NCC, 128], F32, kind='ExternalInput')
    sh_down = nc.dram_tensor('sh_down', [I_SH, C], F32, kind='ExternalInput')
    sh_gate_w = nc.dram_tensor('sh_gate_w', [C, 1], F32, kind='ExternalInput')
    cos_in = nc.dram_tensor('cos_tab', [T, D], F32, kind='ExternalInput')
    sin_in = nc.dram_tensor('sin_tab', [T, D], F32, kind='ExternalInput')
    ma_in = nc.dram_tensor('mask_at', [128, 128], F32, kind='ExternalInput')
    mb_in = nc.dram_tensor('mask_bt', [128, 128], F32, kind='ExternalInput')
    id_in = nc.dram_tensor('ident', [128, 128], BF16, kind='ExternalInput')

    out = nc.dram_tensor('out', [T, C], F32, kind='ExternalOutput')

    xT_scr = dram('xT_scr', [NTC, 128, C], BF16)
    q_scr = dram('q_scr', [H, 128, T], BF16)
    ctx_scr = dram('ctx_scr', [H, 128, T], BF16)
    x1_scr = dram('x1_scr', [T, C], F32)
    x1T_scr = dram('x1T_scr', [NCC, 128, T], BF16)

    p = Prog(nc)

    PS0 = nc.alloc_psum_tensor('PS0', [128, 2048], F32)
    PS1 = nc.alloc_psum_tensor('PS1', [128, 1024], F32)
    PS2 = nc.alloc_psum_tensor('PS2', [128, 512], F32)
    PST = nc.alloc_psum_tensor('PST', [128, 1024], BF16)

    sb = nc.alloc_sbuf_tensor
    id_sb = sb('id_sb', [128, 128], BF16)
    pkT_sb = sb('pkT_sb', [128, R], BF16)
    pvT_sb = sb('pvT_sb', [128, R], BF16)
    uoT_sb = sb('uoT_sb', [64, 128], BF16)
    gate_sb = sb('gate_sb', [128, NTC], F32)
    ones128 = sb('ones128', [128, 1], BF16)
    ones1f = sb('ones1f', [1, 128], F32)
    sg_sb = sb('sg_sb', [128, NTC], F32)
    comb_sb = sb('comb_sb', [128, NTC * E_MOE], F32)
    st_sb = sb('st_sb', [128, 16], F32)

    p.dma('sp', id_sb[:, :], id_in.ap(), writes=['ident'], stream='ident')
    p.op('dve', lambda E: E.memset(ones128[:, :], 1.0), writes=['ones128'])
    p.op('dve', lambda E: E.memset(ones1f[:, :], 1.0), writes=['ones1f'])

    _tq = []
    _ectr = [0]
    _tflushing = [False]

    def tp_flush():
        if _tflushing[0] or not _tq:
            p.pending_tp.clear()
            return
        _tflushing[0] = True
        batch = list(_tq)
        del _tq[:]
        toks = []
        for (in_ap, in_keys, dst_ap, dst_keys, kpart, mfree, i) in batch:
            out_ap = PST[0:mfree, i * 128:i * 128 + kpart]
            tok = p.op('pe',
                       lambda E, out_ap=out_ap, in_ap=in_ap, kpart=kpart:
                       E.transpose(out_ap, in_ap, id_sb[0:kpart, 0:kpart]),
                       reads=list(in_keys) + ['ident'], writes=[f'PSTq{i}'])
            toks.append(tok)
        for (_, _, _, _, _, _, i) in batch:
            p.res[f'PSTq{i}'] = {'w': toks[-1], 'r': []}
        for (in_ap, in_keys, dst_ap, dst_keys, kpart, mfree, i) in batch:
            out_ap = PST[0:mfree, i * 128:i * 128 + kpart]
            _ectr[0] += 1
            if True:
                p.op('dve', lambda E, dst_ap=dst_ap, out_ap=out_ap:
                     E.tensor_copy(dst_ap, out_ap),
                     reads=[f'PSTq{i}'], writes=list(dst_keys))
            else:
                p.op('act', lambda E, dst_ap=dst_ap, out_ap=out_ap:
                     E.copy(dst_ap, out_ap),
                     reads=[f'PSTq{i}'], writes=list(dst_keys))
        p.pending_tp.clear()
        _tflushing[0] = False

    p.tp_flush = tp_flush

    def transpose_tile(in_ap, in_keys, dst_ap, dst_keys, kpart=128, mfree=128):
        if len(_tq) == 8:
            tp_flush()
        i = len(_tq)
        _tq.append((in_ap, list(in_keys), dst_ap, list(dst_keys), kpart,
                    mfree, i))
        p.pending_tp.update(in_keys)
        p.pending_tp.update(dst_keys)
        p.pending_tp.add(f'PSTq{i}')

    def maybe_dump(name, ap, shape, reads, dtype=F32):
        if name in DUMP:
            t = nc.dram_tensor(name, shape, dtype, kind='ExternalOutput')
            p.dma('sp', t.ap(), ap, reads=reads, stream='dump')

    def ln_epilogue(r_ap, r_keys, dst_ap, dst_key, junk_ap, junk_key):
        p.op('dve', lambda E: E.reduce_sum(st_sb[:, 0:1], r_ap, axis=AX.X),
             reads=r_keys, writes=['st0'])
        p.op('dve', lambda E: E.tensor_single_scalar(
            st_sb[:, 1:2], st_sb[:, 0:1], -1.0 / C, ALU.mult),
            reads=['st0'], writes=['st1'])
        p.op('dve', lambda E: E.scalar_tensor_tensor(
            junk_ap, r_ap, st_sb[:, 1:2], r_ap, ALU.add, ALU.mult,
            accum_out=st_sb[:, 2:3]),
            reads=r_keys + ['st1'], writes=[junk_key, 'st2'])
        p.op('dve', lambda E: E.tensor_scalar(
            st_sb[:, 3:4], st_sb[:, 2:3], 1.0 / C, EPS, ALU.mult, ALU.add),
            reads=['st2'], writes=['st3'])
        p.op('act', lambda E: E.activation(st_sb[:, 4:5], st_sb[:, 3:4],
                                           AF.Sqrt),
             reads=['st3'], writes=['st4'])
        p.op('dve', lambda E: E.reciprocal(st_sb[:, 5:6], st_sb[:, 4:5]),
             reads=['st4'], writes=['st5'])
        p.op('dve', lambda E: E.tensor_scalar(
            dst_ap, r_ap, st_sb[:, 1:2], st_sb[:, 5:6], ALU.add, ALU.mult),
            reads=r_keys + ['st1', 'st5'], writes=[dst_key])

    astack = ExitStack()

    def asb(name, shape, dtype):
        return astack.enter_context(nc.sbuf_tensor(name, shape, dtype))

    cos_sb = asb('cos_sb', [128, NTC * D], F32)
    sin_sb = asb('sin_sb', [128, NTC * D], F32)
    ma_sb = asb('ma_sb', [128, 128], F32)
    mb_sb = asb('mb_sb', [128, 128], F32)
    gate_row = asb('gate_row', [1, T], F32)

    p.dma('sp', cos_sb[:, :].rearrange('p (tc d) -> p tc d', d=D),
          cos_in.ap().rearrange('(tc p) d -> p tc d', p=128), writes=['cos'],
          stream='cos')
    p.dma('sp', sin_sb[:, :].rearrange('p (tc d) -> p tc d', d=D),
          sin_in.ap().rearrange('(tc p) d -> p tc d', p=128), writes=['sin'],
          stream='sin')
    p.dma('sp', ma_sb[:, :], ma_in.ap(), writes=['ma'], stream='ma')
    p.dma('sp', mb_sb[:, :], mb_in.ap(), writes=['mb'], stream='mb')

    with nc.sbuf_tensor('tmp_f', [128, 128], F32) as tmp_f, \
            nc.sbuf_tensor('tmp_b', [128, 128], BF16) as tmp_b:
        for src, dst, key in ((pk, pkT_sb, 'pkT'), (pv, pvT_sb, 'pvT')):
            p.dma('sp', tmp_f[0:R, :], src.ap(), writes=['tmpf'], stream='tmpf')
            p.op('dve', lambda E: E.tensor_copy(tmp_b[0:R, :], tmp_f[0:R, :]),
                 reads=['tmpf'], writes=['tmpb'])
            transpose_tile(tmp_b[0:R, :], ['tmpb'], dst[:, :], [key],
                           kpart=R, mfree=128)
        p.dma('sp', tmp_f[:, 0:R], uo.ap(), writes=['tmpf'], stream='tmpf')
        p.op('dve', lambda E: E.tensor_copy(tmp_b[:, 0:R], tmp_f[:, 0:R]),
             reads=['tmpf'], writes=['tmpb'])
        transpose_tile(tmp_b[:, 0:R], ['tmpb'], uoT_sb[:, :], ['uoT'],
                       kpart=128, mfree=R)
    p.barrier()

    if 'a0' in PHASES:
        with ExitStack() as stack:
            zb = lambda n, s, d: stack.enter_context(nc.sbuf_tensor(n, s, d))
            xld = [zb(f'a0_xld{i}', [128, C], F32) for i in range(2)]
            xbf = [zb(f'a0_xbf{i}', [128, C], BF16) for i in range(2)]
            xtc = [zb(f'a0_xtc{i}', [128, C], BF16) for i in range(2)]
            iw1f = zb('a0_iw1f', [128, C], F32)
            iw1b = zb('a0_iw1b', [128, C], BF16)
            iw1T = zb('a0_iw1T', [128, NCC * 512], BF16)
            w2c = zb('a0_w2c', [128, 4], F32)
            w2b = zb('a0_w2b', [128, 4], BF16)
            hinfT = zb('a0_hinfT', [128, 4 * 128], BF16)
            sig = zb('a0_sig', [128, 1], F32)
            gbft = zb('a0_gbft', [128, 1], BF16)
            ps_in = PS1[:, 0:128]
            ps_sc = PS2[:, 0:1]

            for fc in range(4):
                p.dma('sp', iw1f[:, :],
                      info_w1.ap()[fc * 128:(fc + 1) * 128, :],
                      writes=['iw1f'], stream='iw1f')
                p.op('dve', lambda E: E.tensor_copy(iw1b[:, :], iw1f[:, :]),
                     reads=['iw1f'], writes=['iw1b'])
                for cc in range(NCC):
                    transpose_tile(iw1b[:, cc * 128:(cc + 1) * 128], ['iw1b'],
                                   iw1T[:, cc * 512 + fc * 128:
                                        cc * 512 + (fc + 1) * 128],
                                   [f'iw1T_{cc}_{fc}'])
            for fc in range(4):
                p.dma('sp', w2c[:, fc:fc + 1],
                      info_w2.ap()[:, fc * 128:(fc + 1) * 128]
                      .rearrange('one p -> p one'),
                      writes=['w2c'], stream='w2c')
            p.op('dve', lambda E: E.tensor_copy(w2b[:, :], w2c[:, :]),
                 reads=['w2c'], writes=['w2b'])

            for tc in range(NTC):
                b = tc % 2
                p.dma('pool', xld[b][:, :],
                      x_in.ap()[tc * 128:(tc + 1) * 128, :],
                      writes=[f'a0_xld{b}'], stream=f'xld{b}')
                p.op('dve', lambda E, b=b: E.tensor_copy(xbf[b][:, :],
                                                         xld[b][:, :]),
                     reads=[f'a0_xld{b}'], writes=[f'a0_xbf{b}'])
                for cc in range(NCC):
                    transpose_tile(xbf[b][:, cc * 128:(cc + 1) * 128],
                                   [f'a0_xbf{b}'],
                                   xtc[b][:, cc * 128:(cc + 1) * 128],
                                   [f'a0_xtc{b}_{cc}'])
                p.dma('pool', xT_scr.ap()[tc], xtc[b][:, :],
                      reads=[f'a0_xtc{b}_{cc}' for cc in range(NCC)],
                      writes=[f'xT_{tc}'], stream='xT_w')
                for fc in range(4):
                    for cc in range(NCC):
                        p.op('pe', lambda E, b=b, cc=cc, fc=fc: E.matmul(
                            ps_in,
                            iw1T[:, cc * 512 + fc * 128: cc * 512 + (fc + 1) * 128],
                            xtc[b][:, cc * 128:(cc + 1) * 128],
                            start=(cc == 0), stop=(cc == NCC - 1)),
                            reads=[f'iw1T_{cc}_{fc}', f'a0_xtc{b}_{cc}'],
                            writes=['ps_in'])
                    p.op('act', lambda E, fc=fc: E.activation(
                        hinfT[:, fc * 128:(fc + 1) * 128], ps_in, AF.Gelu),
                        reads=['ps_in'], writes=[f'hinfT{fc}'])
                for fc in range(4):
                    p.op('pe', lambda E, fc=fc: E.matmul(
                        ps_sc, hinfT[:, fc * 128:(fc + 1) * 128],
                        w2b[:, fc:fc + 1], start=(fc == 0), stop=(fc == 3)),
                        reads=[f'hinfT{fc}', 'w2b'], writes=['ps_sc'])
                p.op('act', lambda E: E.activation(sig[:, :], ps_sc,
                                                   AF.Sigmoid),
                     reads=['ps_sc'], writes=['a0_sig'])
                p.op('dve', lambda E, tc=tc: E.tensor_single_scalar(
                    gate_sb[:, tc:tc + 1], sig[:, :], DELTA_I, ALU.is_gt),
                    reads=['a0_sig'], writes=['gate'])
                p.op('dve', lambda E, tc=tc: E.tensor_copy(
                    gbft[:, :], gate_sb[:, tc:tc + 1]),
                    reads=['gate'], writes=['a0_gbft'])
                transpose_tile(gbft[:, :], ['a0_gbft'],
                               gate_row[:, tc * 128:(tc + 1) * 128],
                               [f'gate_row{tc}'], kpart=128, mfree=1)
            maybe_dump('gate_dbg', gate_sb[:, :], [128, NTC], ['gate'])
            p.barrier()

    if 'a1' in PHASES:
        with ExitStack() as stack:
            zb = lambda n, s, d: stack.enter_context(nc.sbuf_tensor(n, s, d))
            kT_sb = zb('kT_sb', [128, HH * T], BF16)
            v_sb = zb('v_sb', [128, NTC * HH * 128], BF16)
            wblk = zb('wblk', [128, NCC * 512], BF16)
            wld = [zb(f'a1_wld{i}', [128, C], F32) for i in range(2)]
            wbf = [zb(f'a1_wbf{i}', [128, C], BF16) for i in range(2)]
            xtc1 = [zb(f'a1_xtc{i}', [128, C], BF16) for i in range(2)]
            rst = zb('a1_rst', [128, 512], F32)
            rst2 = zb('a1_rst2', [128, 512], F32)
            ropeo = zb('a1_ropeo', [128, 512], BF16)
            qTst = zb('a1_qTst', [128, 512], BF16)
            qT_h = zb('qT_h', [128, T], BF16)
            krT = zb('krT', [64, T], BF16)
            qrT = zb('qrT', [64, T], BF16)
            vT_h = zb('vT_h', [128, T], BF16)
            vr = zb('vr', [128, NTC * 64], BF16)
            vrst = zb('vrst', [64, 512], BF16)
            ptg = zb('ptg', [128, NTC * 512], BF16)
            ptl = zb('ptl', [128, 8 * 512], BF16)
            cgr_sb = zb('cgr_sb', [64, 512], BF16)
            ctxst = zb('ctxst', [128, 512], BF16)
            lg_row = zb('lg_row', [1, 512], F32)
            ll_row = zb('ll_row', [1, 512], F32)
            rlg_row = zb('rlg_row', [1, 512], F32)
            rll_row = zb('rll_row', [1, 512], F32)
            f2_row = zb('f2_row', [1, 512], F32)
            bf2_sb = zb('bf2_sb', [128, 512], F32)
            brl_sb = zb('brl_sb', [128, 512], F32)
            ps_qkv = [PS0[:, 0:512], PS0[:, 512:1024]]
            ps_a = PS1[:, 0:512]
            psrow = PS1[0:1, 512:1024]
            psbc = PS1[:, 512:1024]
            ps_ctx = PS2

            def rope_evac(psum, pskey, tc):
                cs = cos_sb[:, tc * D:(tc + 1) * D]
                sn = sin_sb[:, tc * D:(tc + 1) * D]
                for j in range(4):
                    o = j * 128
                    p.op('dve', lambda E, o=o: E.tensor_tensor(
                        rst[:, o:o + 128], psum[:, o:o + 128], cs, ALU.mult),
                        reads=[pskey, 'cos'], writes=[f'a1_rst{j}'])
                    p.op('dve', lambda E, o=o: E.tensor_tensor(
                        rst2[:, o:o + 128:2], psum[:, o + 1:o + 128:2],
                        sn[:, 0:128:2], ALU.mult),
                        reads=[pskey, 'sin'], writes=[f'a1_rst2e{j}'])
                    p.op('dve', lambda E, o=o: E.tensor_tensor(
                        rst2[:, o + 1:o + 128:2], psum[:, o:o + 128:2],
                        sn[:, 1:128:2], ALU.mult),
                        reads=[pskey, 'sin'], writes=[f'a1_rst2o{j}'])
                p.op('dve', lambda E: E.tensor_tensor(
                    ropeo[:, :], rst[:, :], rst2[:, :], ALU.add),
                    reads=[k for j in range(4) for k in
                           (f'a1_rst{j}', f'a1_rst2e{j}', f'a1_rst2o{j}')],
                    writes=['a1_ropeo'])

            for X in range(2):
                for proj, off in ((('v', 0), ('v', 512), ('k', 0), ('k', 512),
                                   ('q', 0), ('q', 512)) if A1A else ()):
                    base = {'q': 0, 'k': C, 'v': 2 * C}[proj] + X * 1024 + off
                    for s4 in range(4):
                        b = s4 % 2
                        p.dma('sp', wld[b][:, :],
                              w_qkv.ap()[base + s4 * 128: base + (s4 + 1) * 128, :],
                              writes=[f'a1_wld{b}'], stream=f'wld{b}')
                        p.op('dve', lambda E, b=b: E.tensor_copy(
                            wbf[b][:, :], wld[b][:, :]),
                            reads=[f'a1_wld{b}'], writes=[f'a1_wbf{b}'])
                        for cc in range(NCC):
                            transpose_tile(wbf[b][:, cc * 128:(cc + 1) * 128],
                                           [f'a1_wbf{b}'],
                                           wblk[:, cc * 512 + s4 * 128:
                                                cc * 512 + (s4 + 1) * 128],
                                           [f'wblk_{cc}_{s4}'])
                    for tc in range(NTC):
                        b = tc % 2
                        p.dma('pool', xtc1[b][:, :], xT_scr.ap()[tc],
                              reads=[f'xT_{tc}'], writes=[f'a1_xtc{b}'],
                              stream=f'xtc{b}')
                        for cc in range(NCC):
                            p.op('pe', lambda E, b=b, cc=cc: E.matmul(
                                ps_qkv[b],
                                xtc1[b][:, cc * 128:(cc + 1) * 128],
                                wblk[:, cc * 512:(cc + 1) * 512],
                                start=(cc == 0), stop=(cc == NCC - 1)),
                                reads=[f'a1_xtc{b}'] +
                                [f'wblk_{cc}_{s4}' for s4 in range(4)],
                                writes=[f'ps_qkv{b}'])
                        if proj == 'v':
                            p.op('dve', lambda E, b=b, tc=tc, off=off:
                                 E.tensor_copy(
                                     v_sb[:, tc * HH * 128 + off:
                                          tc * HH * 128 + off + 512],
                                     ps_qkv[b]),
                                 reads=[f'ps_qkv{b}'], writes=[f'v_{tc}_{off}'])
                        else:
                            rope_evac(ps_qkv[b], f'ps_qkv{b}', tc)
                            for j in range(4):
                                hh = off // 128 + j
                                if proj == 'k':
                                    transpose_tile(
                                        ropeo[:, j * 128:(j + 1) * 128],
                                        ['a1_ropeo'],
                                        kT_sb[:, hh * T + tc * 128:
                                              hh * T + (tc + 1) * 128],
                                        [f'kT_{hh}_{tc}'])
                                else:
                                    transpose_tile(
                                        ropeo[:, j * 128:(j + 1) * 128],
                                        ['a1_ropeo'],
                                        qTst[:, j * 128:(j + 1) * 128],
                                        [f'a1_qTst{j}'])
                            if proj == 'q':
                                h0 = X * HH + off // 128
                                p.dma('pool',
                                      q_scr.ap()[h0:h0 + 4, :,
                                                 tc * 128:(tc + 1) * 128]
                                      .rearrange('h p t -> p h t'),
                                      qTst[:, :].rearrange(
                                          'p (h t) -> p h t', t=128),
                                      reads=[f'a1_qTst{j}'
                                             for j in range(4)],
                                      writes=[f'q_{h0}_{tc}'],
                                      stream='q_w')
                if X == 0:
                    maybe_dump('v_dbg', v_sb[:, :], [128, NTC * HH * 128],
                               [f'v_{tc}_{o}' for tc in range(NTC)
                                for o in (0, 512)], dtype=BF16)
                    maybe_dump('kT_dbg', kT_sb[:, :], [128, HH * T],
                               [f'kT_{hh}_{tc}' for hh in range(HH)
                                for tc in range(NTC)], dtype=BF16)
                p.barrier()

                for hh in (range(HH) if A1B else []):
                    h_abs = X * HH + hh
                    h0 = X * HH + (hh // 4) * 4
                    p.dma('pool', qT_h[:, :], q_scr.ap()[h_abs],
                          reads=[f'q_{h0}_{tc}' for tc in range(NTC)],
                          writes=['qT_h'], stream='qTh')
                    for sgp in (range(NG) if A1B_KR else []):
                        p.op('pe', lambda E, sgp=sgp, hh=hh: E.matmul(
                            ps_a[0:64, :], pkT_sb[:, :],
                            kT_sb[:, hh * T + sgp * 512:
                                  hh * T + (sgp + 1) * 512],
                            start=True, stop=True),
                            reads=['pkT'], writes=['ps_a'])
                        p.op('dve', lambda E, sgp=sgp: E.tensor_copy(
                            krT[:, sgp * 512:(sgp + 1) * 512], ps_a[0:64, :]),
                            reads=['ps_a'], writes=['krT'])
                        p.op('pe', lambda E, sgp=sgp: E.matmul(
                            ps_a[0:64, :], pkT_sb[:, :],
                            qT_h[:, sgp * 512:(sgp + 1) * 512],
                            start=True, stop=True),
                            reads=['pkT', 'qT_h'], writes=['ps_a'])
                        p.op('dve', lambda E, sgp=sgp: E.tensor_copy(
                            qrT[:, sgp * 512:(sgp + 1) * 512], ps_a[0:64, :]),
                            reads=['ps_a'], writes=['qrT'])
                    voff = (hh // 4) * 512
                    for tc in (range(NTC) if A1B_VR else []):
                        transpose_tile(
                            v_sb[:, tc * HH * 128 + hh * 128:
                                 tc * HH * 128 + (hh + 1) * 128],
                            [f'v_{tc}_{voff}'],
                            vT_h[:, tc * 128:(tc + 1) * 128], [f'vT_{tc}'])
                    for sgp in (range(NG) if (A1B_VR and A1B_VR2) else []):
                        p.op('pe', lambda E, sgp=sgp: E.matmul(
                            ps_a[0:64, :], pvT_sb[:, :],
                            vT_h[:, sgp * 512:(sgp + 1) * 512],
                            start=True, stop=True),
                            reads=['pvT'] + [f'vT_{sgp * 4 + j}'
                                             for j in range(4)],
                            writes=['ps_a'])
                        p.op('dve', lambda E: E.tensor_copy(
                            vrst[:, :], ps_a[0:64, :]),
                            reads=['ps_a'], writes=['vrst'])
                        for j in range(4):
                            transpose_tile(
                                vrst[:, j * 128:(j + 1) * 128], ['vrst'],
                                vr[:, (sgp * 4 + j) * 64:
                                   (sgp * 4 + j + 1) * 64],
                                [f'vr_{sgp * 4 + j}'], kpart=64, mfree=128)

                    for g in (range(NG) if A1B_ROWS else []):
                        G0 = g * 512
                        g4 = g * 4
                        nch = min(4, NTC - g4)
                        assert nch == 4
                        for k2 in (range(NTC // 2) if A1B_G else []):
                            for j in range(2):
                                sc = 2 * k2 + j
                                q = sc % 4
                                p.op('pe', lambda E, sc=sc, q=q, G0=G0:
                                     E.matmul(
                                         PS0[:, q * 512:(q + 1) * 512],
                                         krT[:, sc * 128:(sc + 1) * 128],
                                         qrT[:, G0:G0 + 512],
                                         start=True, stop=True),
                                     reads=['krT', 'qrT'],
                                     writes=[f'PS0q{q}'])
                            q0 = (2 * k2) % 4
                            p.op('act', lambda E, k2=k2, q0=q0: E.activation(
                                ptg[:, 2 * k2 * 512: 2 * (k2 + 1) * 512],
                                PS0[:, q0 * 512:(q0 + 2) * 512],
                                AF.Exp, scale=SCALE_G),
                                reads=[f'PS0q{q0}', f'PS0q{q0 + 1}'],
                                writes=[f'ptg_{2 * k2}', f'ptg_{2 * k2 + 1}'])
                        for sc in (range(NTC) if A1B_G else []):
                            p.op('pe', lambda E, sc=sc: E.matmul(
                                psrow, ones128[:, :],
                                ptg[:, sc * 512:(sc + 1) * 512],
                                start=(sc == 0), stop=(sc == NTC - 1)),
                                reads=[f'ptg_{sc}', 'ones128'],
                                writes=['ps1b'])
                        p.op('dve', lambda E: E.tensor_copy(lg_row[:, :],
                                                            psrow),
                             reads=['ps1b'], writes=['lg_row'])
                        lvalid = []
                        for sc_abs in range(max(0, g4 - 4), g4 + nch):
                            slot = sc_abs - (g4 - 4)
                            lo = max(0, sc_abs - g4)
                            hi = min(nch - 1, sc_abs - g4 + 4)
                            lvalid.append((sc_abs, slot, lo, hi))
                        lvalid.sort(key=lambda v: v[0] != g4)
                        for n_i, (sc_abs, slot, lo, hi) in (
                                enumerate(lvalid) if A1B_L else []):
                            w = (hi - lo + 1) * 128
                            q = slot % 4
                            p.op('pe', lambda E, sc_abs=sc_abs, lo=lo, w=w,
                                 q=q, hh=hh, g4=g4: E.matmul(
                                     PS0[:, q * 512: q * 512 + w],
                                     kT_sb[:, hh * T + sc_abs * 128:
                                           hh * T + (sc_abs + 1) * 128],
                                     qT_h[:, (g4 + lo) * 128:
                                          (g4 + lo) * 128 + w],
                                     start=True, stop=True),
                                 reads=['qT_h'], writes=[f'PS0q{q}'])
                            if g4 <= sc_abs:
                                col = (sc_abs - g4 - lo) * 128
                                p.op('dve', lambda E, q=q, col=col:
                                     E.tensor_tensor(
                                         PS0[:, q * 512 + col:
                                             q * 512 + col + 128],
                                         PS0[:, q * 512 + col:
                                             q * 512 + col + 128],
                                         mb_sb[:, :], ALU.add),
                                     reads=['mb', f'PS0q{q}'],
                                     writes=[f'PS0q{q}'])
                            else:
                                col = (sc_abs + 4 - g4 - lo) * 128
                                p.op('dve', lambda E, q=q, col=col:
                                     E.tensor_tensor(
                                         PS0[:, q * 512 + col:
                                             q * 512 + col + 128],
                                         PS0[:, q * 512 + col:
                                             q * 512 + col + 128],
                                         ma_sb[:, :], ALU.add),
                                     reads=['ma', f'PS0q{q}'],
                                     writes=[f'PS0q{q}'])
                            p.op('act', lambda E, slot=slot, lo=lo, w=w, q=q:
                                 E.activation(
                                     ptl[:, slot * 512 + lo * 128:
                                         slot * 512 + lo * 128 + w],
                                     PS0[:, q * 512: q * 512 + w],
                                     AF.Exp, scale=SCALE),
                                 reads=[f'PS0q{q}'], writes=[f'ptl_{slot}'])
                        for n_i, (sc_abs, slot, lo, hi) in (
                                enumerate(lvalid) if A1B_L else []):
                            w = (hi - lo + 1) * 128
                            p.op('pe', lambda E, slot=slot, lo=lo, w=w,
                                 st=(n_i == 0),
                                 sp=(n_i == len(lvalid) - 1): E.matmul(
                                     psrow[:, lo * 128: lo * 128 + w],
                                     ones128[:, :],
                                     ptl[:, slot * 512 + lo * 128:
                                         slot * 512 + lo * 128 + w],
                                     start=st, stop=sp),
                                 reads=[f'ptl_{slot}', 'ones128'],
                                 writes=['ps1b'])
                        p.op('dve', lambda E: E.tensor_copy(ll_row[:, :],
                                                            psrow),
                             reads=['ps1b'], writes=['ll_row'])
                        p.op('dve', lambda E: E.reciprocal(rlg_row[:, :],
                                                           lg_row[:, :]),
                             reads=['lg_row'], writes=['rlg_row'])
                        p.op('dve', lambda E: E.reciprocal(rll_row[:, :],
                                                           ll_row[:, :]),
                             reads=['ll_row'], writes=['rll_row'])
                        p.op('dve', lambda E, G0=G0: E.tensor_tensor(
                            f2_row[:, :], rlg_row[:, :],
                            gate_row[:, G0:G0 + 512], ALU.mult),
                            reads=['rlg_row'] + [f'gate_row{g4 + c}'
                                                 for c in range(4)],
                            writes=['f2_row'])
                        p.op('dve', lambda E: E.tensor_tensor(
                            f2_row[:, :], f2_row[:, :], ll_row[:, :],
                            ALU.mult),
                            reads=['f2_row', 'll_row'], writes=['f2_row'])
                        p.op('pe', lambda E: E.matmul(
                            psbc, ones1f[:, :], f2_row[:, :],
                            start=True, stop=True),
                            reads=['f2_row', 'ones1f'], writes=['ps1b'])
                        p.op('dve', lambda E: E.tensor_copy(bf2_sb[:, :],
                                                            psbc),
                             reads=['ps1b'], writes=['bf2'])
                        p.op('pe', lambda E: E.matmul(
                            psbc, ones1f[:, :], rll_row[:, :],
                            start=True, stop=True),
                            reads=['rll_row', 'ones1f'], writes=['ps1b'])
                        p.op('act', lambda E: E.copy(brl_sb[:, :], psbc),
                             reads=['ps1b'], writes=['brl'])
                        for sc in (range(NTC) if A1B_G else []):
                            p.op('pe', lambda E, sc=sc: E.matmul(
                                ps_a[0:64, :],
                                vr[:, sc * 64:(sc + 1) * 64],
                                ptg[:, sc * 512:(sc + 1) * 512],
                                start=(sc == 0), stop=(sc == NTC - 1)),
                                reads=[f'vr_{sc}', f'ptg_{sc}'],
                                writes=['ps_a'])
                        p.op('dve', lambda E: E.tensor_tensor(
                            cgr_sb[:, :], ps_a[0:64, :], bf2_sb[0:64, :],
                            ALU.mult),
                            reads=['ps_a', 'bf2'], writes=['cgr'])
                        p.op('pe', lambda E: E.matmul(
                            ps_ctx[:, :], uoT_sb[:, :], cgr_sb[:, :],
                            start=True, stop=not A1B_L),
                            reads=['uoT', 'cgr'], writes=['ps_ctx'])
                        for n_i, (sc_abs, slot, lo, hi) in (
                                enumerate(lvalid) if A1B_L else []):
                            p.op('pe', lambda E, sc_abs=sc_abs, slot=slot,
                                 lo=lo, hi=hi, hh=hh,
                                 last=(n_i == len(lvalid) - 1): E.matmul(
                                ps_ctx[:, lo * 128:(hi + 1) * 128],
                                v_sb[:, sc_abs * HH * 128 + hh * 128:
                                     sc_abs * HH * 128 + (hh + 1) * 128],
                                ptl[:, slot * 512 + lo * 128:
                                    slot * 512 + (hi + 1) * 128],
                                start=False, stop=last),
                                reads=[f'ptl_{slot}'], writes=['ps_ctx'])
                        p.op('dve', lambda E: E.tensor_tensor(
                            ctxst[:, :], ps_ctx[:, :], brl_sb[:, :], ALU.mult),
                            reads=['ps_ctx', 'brl'], writes=['ctxst'])
                        p.dma('pool', ctx_scr.ap()[h_abs][:, G0:G0 + 512],
                              ctxst[:, :], reads=['ctxst'],
                              writes=[f'ctx_{h_abs}_{g}'], stream='ctx_w')
                p.barrier()
    astack.close()

    if 'a2' in PHASES:
        with ExitStack() as stack:
            zb = lambda n, s, d: stack.enter_context(nc.sbuf_tensor(n, s, d))
            woT = zb('woT', [128, NCC * C], BF16)
            wld2 = [zb(f'a2_wld{i}', [128, C], F32) for i in range(2)]
            wbf2 = [zb(f'a2_wbf{i}', [128, C], BF16) for i in range(2)]
            ctxc = [zb(f'a2_ctxc{i}', [128, C], BF16) for i in range(2)]
            xrow = [zb(f'a2_xrow{i}', [128, C], F32) for i in range(2)]
            r_sb = zb('a2_r', [128, C], F32)
            x1f = zb('a2_x1f', [128, C], F32)
            x1b = zb('a2_x1b', [128, C], BF16)
            x1Tst = zb('a2_x1Tst', [128, C], BF16)
            junk = zb('a2_junk', [128, C], F32)

            for s in range(NCC):
                b = s % 2
                p.dma('sp', wld2[b][:, :], w_o.ap()[s * 128:(s + 1) * 128, :],
                      writes=[f'a2_wld{b}'], stream=f'wld{b}')
                p.op('dve', lambda E, b=b: E.tensor_copy(wbf2[b][:, :],
                                                         wld2[b][:, :]),
                     reads=[f'a2_wld{b}'], writes=[f'a2_wbf{b}'])
                for cc in range(NCC):
                    transpose_tile(wbf2[b][:, cc * 128:(cc + 1) * 128],
                                   [f'a2_wbf{b}'],
                                   woT[:, cc * C + s * 128:
                                       cc * C + (s + 1) * 128],
                                   [f'woT_{cc}_{s}'])

            for tc in range(NTC):
                b = tc % 2
                p.dma('pool', ctxc[b][:, :].rearrange('p (h t) -> p h t', t=128),
                      ctx_scr.ap()[:, :, tc * 128:(tc + 1) * 128]
                      .rearrange('h p t -> p h t'),
                      reads=[f'ctx_{h}_{tc // 4}' for h in range(H)],
                      writes=[f'a2_ctxc{b}'], stream=f'ctxc{b}')
                p.dma('sp', xrow[b][:, :], x_in.ap()[tc * 128:(tc + 1) * 128, :],
                      writes=[f'a2_xrow{b}'], stream=f'xrow{b}')
                for cog in range(4):
                    for cc in range(NCC):
                        p.op('pe', lambda E, b=b, cc=cc, cog=cog: E.matmul(
                            PS0[:, cog * 512:(cog + 1) * 512],
                            ctxc[b][:, cc * 128:(cc + 1) * 128],
                            woT[:, cc * C + cog * 512: cc * C + (cog + 1) * 512],
                            start=(cc == 0), stop=(cc == NCC - 1)),
                            reads=[f'a2_ctxc{b}'] +
                            [f'woT_{cc}_{s}' for s in range(cog * 4, cog * 4 + 4)],
                            writes=['PS0a'])
                p.op('dve', lambda E, b=b: E.tensor_tensor(
                    r_sb[:, :], PS0[:, :], xrow[b][:, :], ALU.add),
                    reads=['PS0a', f'a2_xrow{b}'], writes=['a2_r'])
                ln_epilogue(r_sb[:, :], ['a2_r'], x1f[:, :], 'a2_x1f',
                            junk[:, :], 'a2_junk')
                p.op('dve', lambda E: E.tensor_copy(x1b[:, :], x1f[:, :]),
                     reads=['a2_x1f'], writes=['a2_x1b'])
                p.dma('sp', x1_scr.ap()[tc * 128:(tc + 1) * 128, :], x1f[:, :],
                      reads=['a2_x1f'], writes=[f'x1_{tc}'], stream='x1_w')
                for cc in range(NCC):
                    transpose_tile(x1b[:, cc * 128:(cc + 1) * 128],
                                   ['a2_x1b'],
                                   x1Tst[:, cc * 128:(cc + 1) * 128],
                                   [f'a2_x1Tst{cc}'])
                p.dma('pool',
                      x1T_scr.ap()[:, :, tc * 128:(tc + 1) * 128]
                      .rearrange('cc p t -> p cc t'),
                      x1Tst[:, :].rearrange('p (cc t) -> p cc t', t=128),
                      reads=[f'a2_x1Tst{cc}' for cc in range(NCC)],
                      writes=[f'x1T_{tc}'], stream='x1T_w')
            p.barrier()

    if 'f' in PHASES:
        with ExitStack() as stack:
            zb = lambda n, s, d: stack.enter_context(nc.sbuf_tensor(n, s, d))
            TH = T // 2
            NTC_H = TH // 128
            NTT = TH // 512
            routT = zb('f_routT', [128, NCC * 16], BF16)
            shw = zb('f_shw', [128, NCC], BF16)
            with nc.sbuf_tensor('f_rTl', [16, C], F32) as rTl, \
                    nc.sbuf_tensor('f_rTb', [16, C], BF16) as rTb, \
                    nc.sbuf_tensor('f_shwf', [128, NCC], F32) as shwf:
                p.dma('sp', rTl[:, :], router_w.ap(), writes=['f_rTl'], stream='rTl')
                p.op('dve', lambda E: E.tensor_copy(rTb[:, :], rTl[:, :]),
                     reads=['f_rTl'], writes=['f_rTb'])
                for cc in range(NCC):
                    transpose_tile(rTb[0:16, cc * 128:(cc + 1) * 128],
                                   ['f_rTb'],
                                   routT[:, cc * 16:(cc + 1) * 16],
                                   [f'routT{cc}'], kpart=16, mfree=128)
                p.dma('sp', shwf[:, :].rearrange('p (cc one) -> p cc one', one=1),
                      sh_gate_w.ap().rearrange('(cc p) one -> p cc one', p=128),
                      writes=['f_shwf'], slow=True, stream='shwf')
                p.op('dve', lambda E: E.tensor_copy(shw[:, :], shwf[:, :]),
                     reads=['f_shwf'], writes=['f_shw'])
                p.barrier()

            x1Th = zb('x1Th', [128, NCC * TH], BF16)
            acc = zb('acc', [128, NTC_H * C], F32)
            aT = zb('aT', [128, NIC * TH], BF16)
            wgl = [zb(f'f_wgl{i}', [128, C], F32) for i in range(2)]
            wgb = [[zb(f'f_wgb{i}_{j}', [128, C], BF16) for j in range(2)]
                   for i in range(2)]
            wdb = zb('f_wdb', [128, NIC * C], BF16)
            wdl = [zb(f'f_wdl{i}', [128, 1024], F32) for i in range(2)]
            prb = zb('f_prb', [128, 16], F32)
            prw = zb('f_prw', [128, 16], F32)
            j16 = zb('f_j16', [128, 16], F32)
            sgst = zb('f_sgst', [128, 512], BF16)
            x1r = [wgl[0]]
            ps_g = [PS0[:, 0:512], PS0[:, 512:1024]]
            ps_u = [PS0[:, 1024:1536], PS0[:, 1536:2048]]
            ps_d = [PS1[:, 0:512], PS1[:, 512:1024]]
            ps_r = PS2[:, 0:16]
            ps_s = PS2[:, 128:129]

            for half in range(2):
                t0 = half * TH
                tc0 = t0 // 128
                for cc in range(NCC):
                    p.dma('pool', x1Th[:, cc * TH:(cc + 1) * TH],
                          x1T_scr.ap()[cc][:, t0:t0 + TH],
                          reads=[f'x1T_{tc0 + i}' for i in range(NTC_H)],
                          writes=[f'x1Th_{cc}'], stream='x1Th')
                x1Tkeys = [f'x1Th_{cc}' for cc in range(NCC)]
                for tl in range(NTC_H):
                    tc = tc0 + tl
                    for cc in range(NCC):
                        p.op('pe', lambda E, cc=cc, tl=tl: E.matmul(
                            ps_r, x1Th[:, cc * TH + tl * 128:
                                       cc * TH + (tl + 1) * 128],
                            routT[:, cc * 16:(cc + 1) * 16],
                            start=(cc == 0), stop=(cc == NCC - 1)),
                            reads=[f'x1Th_{cc}', f'routT{cc}'],
                            writes=['ps_r'])
                    for cc in range(NCC):
                        p.op('pe', lambda E, cc=cc, tl=tl: E.matmul(
                            ps_s, x1Th[:, cc * TH + tl * 128:
                                       cc * TH + (tl + 1) * 128],
                            shw[:, cc:cc + 1],
                            start=(cc == 0), stop=(cc == NCC - 1)),
                            reads=[f'x1Th_{cc}', 'f_shw'], writes=['ps_s'])
                    p.op('act', lambda E, tc=tc: E.activation(
                        sg_sb[:, tc:tc + 1], ps_s, AF.Sigmoid),
                        reads=['ps_s'], writes=['sg'])
                    p.op('dve', lambda E: E.reduce_max(
                        st_sb[:, 0:1], ps_r, axis=AX.X, negate=True),
                        reads=['ps_r'], writes=['st0'])
                    p.op('act', lambda E: E.activation(
                        prb[:, :], ps_r, AF.Exp, bias=st_sb[:, 0:1]),
                        reads=['ps_r', 'st0'], writes=['f_prb'])
                    p.op('dve', lambda E: E.tensor_copy(prw[:, :], prb[:, :]),
                         reads=['f_prb'], writes=['f_prw'])
                    for it in range(4):
                        p.op('dve', lambda E: E.reduce_max(
                            st_sb[:, 1:2], prw[:, :], axis=AX.X),
                            reads=['f_prw'], writes=['st1'])
                        if it < 3:
                            p.op('dve', lambda E: E.tensor_single_scalar(
                                j16[:, :], prw[:, :], st_sb[:, 1:2],
                                ALU.is_ge), reads=['f_prw', 'st1'],
                                writes=['f_j16'])
                            p.op('dve', lambda E: E.scalar_tensor_tensor(
                                prw[:, :], j16[:, :], NEG, prw[:, :],
                                ALU.mult, ALU.add),
                                reads=['f_j16', 'f_prw'], writes=['f_prw'])
                    p.op('dve', lambda E: E.tensor_single_scalar(
                        prw[:, :], prb[:, :], st_sb[:, 1:2], ALU.is_ge),
                        reads=['f_prb', 'st1'], writes=['f_prw'])
                    p.op('dve', lambda E: E.tensor_tensor(
                        prw[:, :], prw[:, :], prb[:, :], ALU.mult),
                        reads=['f_prw', 'f_prb'], writes=['f_prw'])
                    p.op('dve', lambda E: E.reduce_sum(
                        st_sb[:, 2:3], prw[:, :], axis=AX.X),
                        reads=['f_prw'], writes=['st2'])
                    p.op('dve', lambda E: E.reciprocal(
                        st_sb[:, 3:4], st_sb[:, 2:3]),
                        reads=['st2'], writes=['st3'])
                    p.op('dve', lambda E, tc=tc: E.tensor_scalar_mul(
                        comb_sb[:, tc * 16:(tc + 1) * 16], prw[:, :],
                        st_sb[:, 3:4]),
                        reads=['f_prw', 'st3'], writes=['comb'])

                experts = [('moe', e) for e in range(E_MOE)] + \
                          [('sh', v) for v in range(4)]
                for ei, (kind, e) in enumerate(experts):
                    for ic in range(NIC):
                        wp = ic % 2
                        for gi, (wt, sh_wt) in enumerate(
                                ((w_gate, sh_gate), (w_up, sh_up))):
                            if kind == 'moe':
                                wsrc = wt.ap()[e][ic].rearrange(
                                    'p cc i -> p (cc i)')
                            else:
                                wsrc = sh_wt.ap()[e * NIC + ic].rearrange(
                                    'p cc i -> p (cc i)')
                            p.dma('sp', wgl[gi][:, :], wsrc,
                                  writes=[f'f_wgl{gi}'], stream=f'wld{gi}')
                            p.op('dve', lambda E, gi=gi, wp=wp: E.tensor_copy(
                                wgb[gi][wp][:, :], wgl[gi][:, :]),
                                reads=[f'f_wgl{gi}'], writes=[f'f_wgb{gi}_{wp}'])
                        for tt in range(NTT):
                            pb = (ic * NTT + tt) % 2
                            for cc in range(NCC):
                                p.op('pe', lambda E, cc=cc, tt=tt, pb=pb,
                                     wp=wp: E.matmul(
                                         ps_g[pb],
                                         wgb[0][wp][:, cc * 128:(cc + 1) * 128],
                                         x1Th[:, cc * TH + tt * 512:
                                              cc * TH + (tt + 1) * 512],
                                         start=(cc == 0),
                                         stop=(cc == NCC - 1)),
                                     reads=[f'f_wgb0_{wp}', f'x1Th_{cc}'],
                                     writes=[f'ps_g{pb}'])
                            for cc in range(NCC):
                                p.op('pe', lambda E, cc=cc, tt=tt, pb=pb,
                                     wp=wp: E.matmul(
                                         ps_u[pb],
                                         wgb[1][wp][:, cc * 128:(cc + 1) * 128],
                                         x1Th[:, cc * TH + tt * 512:
                                              cc * TH + (tt + 1) * 512],
                                         start=(cc == 0),
                                         stop=(cc == NCC - 1)),
                                     reads=[f'f_wgb1_{wp}', f'x1Th_{cc}'],
                                     writes=[f'ps_u{pb}'])
                            p.op('act', lambda E, pb=pb: E.activation(
                                sgst[:, :], ps_g[pb], AF.Silu),
                                reads=[f'ps_g{pb}'], writes=['f_sgst'])
                            p.op('dve', lambda E, ic=ic, tt=tt, pb=pb:
                                 E.tensor_tensor(
                                     aT[:, ic * TH + tt * 512:
                                        ic * TH + (tt + 1) * 512],
                                     sgst[:, :], ps_u[pb], ALU.mult),
                                 reads=['f_sgst', f'ps_u{pb}'],
                                 writes=[f'aT_{ic}_{tt}'])
                        if kind == 'moe':
                            wsrc = w_down.ap()[e][ic * 128:(ic + 1) * 128, :]
                        else:
                            i0 = e * I_MOE + ic * 128
                            wsrc = sh_down.ap()[i0:i0 + 128, :]
                        for hf in range(2):
                            gi = hf
                            p.dma('act', wdl[gi][:, :],
                                  wsrc[:, hf * 1024:(hf + 1) * 1024],
                                  writes=[f'f_wdl{gi}'], stream=f'wdn{gi}')
                            p.op('dve', lambda E, gi=gi, ic=ic, hf=hf:
                                 E.tensor_copy(
                                     wdb[:, ic * C + hf * 1024:
                                         ic * C + (hf + 1) * 1024],
                                     wdl[gi][:, :]),
                                 reads=[f'f_wdl{gi}'],
                                 writes=[f'f_wdb_{ic}_{hf}'])
                    for ts in range(NTC_H):
                        tc = tc0 + ts
                        if kind == 'moe':
                            scol = comb_sb[:, tc * 16 + e: tc * 16 + e + 1]
                            skey = 'comb'
                        else:
                            scol = sg_sb[:, tc:tc + 1]
                            skey = 'sg'
                        for cog in range(4):
                            pb = (ts * 4 + cog) % 2
                            for ic in range(NIC):
                                p.op('pe', lambda E, ic=ic, ts=ts, cog=cog,
                                     pb=pb: E.matmul(
                                         ps_d[pb],
                                         aT[:, ic * TH + ts * 128:
                                            ic * TH + (ts + 1) * 128],
                                         wdb[:, ic * C + cog * 512:
                                             ic * C + (cog + 1) * 512],
                                         start=(ic == 0),
                                         stop=(ic == NIC - 1)),
                                     reads=[f'f_wdb_{ic}_{cog // 2}',
                                            f'aT_{ic}_{ts // 4}'],
                                     writes=[f'ps_d{pb}'])
                            if ei == 0:
                                p.op('dve', lambda E, ts=ts, cog=cog, pb=pb,
                                     scol=scol: E.tensor_scalar_mul(
                                         acc[:, ts * C + cog * 512:
                                             ts * C + (cog + 1) * 512],
                                         ps_d[pb], scol),
                                     reads=[f'ps_d{pb}', skey],
                                     writes=[f'acc_{ts}_{cog}'])
                            else:
                                p.op('dve', lambda E, ts=ts, cog=cog, pb=pb,
                                     scol=scol: E.scalar_tensor_tensor(
                                         acc[:, ts * C + cog * 512:
                                             ts * C + (cog + 1) * 512],
                                         ps_d[pb], scol,
                                         acc[:, ts * C + cog * 512:
                                             ts * C + (cog + 1) * 512],
                                         ALU.mult, ALU.add),
                                     reads=[f'ps_d{pb}', skey,
                                            f'acc_{ts}_{cog}'],
                                     writes=[f'acc_{ts}_{cog}'])

                for ts in range(NTC_H):
                    tc = tc0 + ts
                    b = 0
                    p.dma('pool', x1r[b][:, :],
                          x1_scr.ap()[tc * 128:(tc + 1) * 128, :],
                          reads=[f'x1_{tc}'], writes=['f_wgl0'],
                          stream='x1row')
                    acck = [f'acc_{ts}_{cog}' for cog in range(4)]
                    p.op('dve', lambda E, ts=ts, b=b: E.tensor_tensor(
                        acc[:, ts * C:(ts + 1) * C],
                        acc[:, ts * C:(ts + 1) * C],
                        x1r[b][:, :], ALU.add),
                        reads=acck + ['f_wgl0'], writes=acck)
                    ln_epilogue(acc[:, ts * C:(ts + 1) * C], acck,
                                x1r[b][:, :], 'f_wgl0',
                                x1r[b][:, :], 'f_wgl0')
                    p.dma('pool', out.ap()[tc * 128:(tc + 1) * 128, :],
                          x1r[b][:, :],
                          reads=['f_wgl0'], writes=[f'out_{tc}'],
                          stream='out_w')
                p.barrier()

        maybe_dump('comb_dbg', comb_sb[:, :], [128, NTC * E_MOE], ['comb'])
        maybe_dump('sg_dbg', sg_sb[:, :], [128, NTC], ['sg'])

    p.finalize()
    p.emit()
    return nc


def _zero_init_acc():
    pass


def _tile_moe(w):
    E, Cx, Ix = w.shape
    return np.ascontiguousarray(
        w.reshape(E, NCC, 128, Ix // 128, 128).transpose(0, 3, 2, 1, 4))


def _tile_sh(w):
    Cx, Ix = w.shape
    return np.ascontiguousarray(
        w.reshape(NCC, 128, Ix // 128, 128).transpose(2, 1, 0, 3))


def kernel(L0_attn_x, L0_attn_w_qkv, L0_attn_w_o, L0_attn_pk, L0_attn_pv,
           L0_attn_uo, L0_attn_info_w1, L0_attn_info_w2, L0_ffn_router_w,
           L0_ffn_w_gate, L0_ffn_w_up, L0_ffn_w_down, L0_ffn_sh_gate,
           L0_ffn_sh_up, L0_ffn_sh_down, L0_ffn_sh_gate_w):
    xs = np.asarray(L0_attn_x)
    T = xs.shape[1]
    nc = build(T)
    consts = make_consts(T)
    common = dict(
        w_qkv=np.asarray(L0_attn_w_qkv), w_o=np.asarray(L0_attn_w_o),
        pk=np.asarray(L0_attn_pk), pv=np.asarray(L0_attn_pv),
        uo=np.asarray(L0_attn_uo), info_w1=np.asarray(L0_attn_info_w1),
        info_w2=np.asarray(L0_attn_info_w2),
        router_w=np.asarray(L0_ffn_router_w),
        w_gate=_tile_moe(np.asarray(L0_ffn_w_gate)),
        w_up=_tile_moe(np.asarray(L0_ffn_w_up)),
        w_down=np.asarray(L0_ffn_w_down),
        sh_gate=_tile_sh(np.asarray(L0_ffn_sh_gate)),
        sh_up=_tile_sh(np.asarray(L0_ffn_sh_up)),
        sh_down=np.asarray(L0_ffn_sh_down),
        sh_gate_w=np.asarray(L0_ffn_sh_gate_w), **consts)
    B = xs.shape[0]
    in_maps = [dict(common, x=np.ascontiguousarray(xs[b])) for b in range(B)]
    res = run_bass_kernel_spmd(nc, in_maps, core_ids=list(range(B)))
    return np.stack([res.results[b]['out'] for b in range(B)], axis=0)


# version 10 (final)
# speedup vs baseline: 1.2665x; 1.1052x over previous
import math
import bisect
from contextlib import ExitStack
import numpy as np
import ml_dtypes
import concourse.bass as bass
import concourse.mybir as mybir
from concourse.bass_utils import run_bass_kernel_spmd

F32 = mybir.dt.float32
BF16 = mybir.dt.bfloat16
AF = mybir.ActivationFunctionType
ALU = mybir.AluOpType
AX = mybir.AxisListType

ENG2ATTR = {'pe': 'tensor', 'dve': 'vector', 'act': 'scalar',
            'pool': 'gpsimd', 'sp': 'sync'}

C = 2048
H = 16
D = 128
R = 64
DELTA_I = 0.75
SCALE = 1.0 / math.sqrt(D)
SCALE_G = SCALE * D / R
NEG = -1.0e30
EPS = 1e-5
E_MOE = 16
I_MOE = 1408
I_SH = 5632
NIC = I_MOE // 128
NCC = C // 128

DUMP = set()
TSLOTS = 8
A1B = True
A1B_G = True
A1B_L = True
A1B_ROWS = True
A1A = True
A1B_KR = True
A1B_VR = True
A1B_VR2 = True
PHASES = {'a0', 'a1', 'a2', 'f'}


class Prog:

    def __init__(self, nc):
        self.nc = nc
        self.engines = list(ENG2ATTR)
        self.sem = {e: nc.alloc_semaphore(f'S_{e}') for e in self.engines}
        self.ops = {e: [] for e in self.engines}
        self.inc_idx = {e: [] for e in self.engines}
        self.inc_val = {e: [] for e in self.engines}
        self.count = {e: 0 for e in self.engines}
        self.clock = {e: {} for e in self.engines}
        self.res = {}
        self.pending = {e: [] for e in self.engines}
        self.streams = {}
        self.pending_tp = set()
        self.tp_flush = None

    def _stream(self, sname):
        sname = 'q_' + sname
        if sname not in self.streams:
            self.sem[sname] = self.nc.alloc_semaphore(f'S_{sname}')
            self.streams[sname] = 0
        return sname

    def _observe(self, peng, pidx):
        o = self.ops[peng][pidx]
        if o['dma']:
            return (o['stream'], self.streams[o['stream']])
        ii = self.inc_idx[peng]
        j = bisect.bisect_left(ii, pidx)
        if j < len(ii):
            return (peng, self.inc_val[peng][j])
        ops = self.ops[peng]
        k = len(ops) - 1
        while k >= 0 and (ops[k]['fn'] is None or ops[k]['dma']):
            k -= 1
        assert k >= 0 and k >= pidx, f'no op to observe {peng}:{pidx}'
        self.count[peng] += 1
        ops[k]['inc'] = 1
        self.inc_idx[peng].append(k)
        self.inc_val[peng].append(self.count[peng])
        return (peng, self.count[peng])

    def op(self, eng, fn, reads=(), writes=(), dma=False, stream=None):
        if self.pending_tp and ((set(reads) | set(writes)) & self.pending_tp):
            self.tp_flush()
        if dma:
            stream = self._stream(stream or eng)
        idx = len(self.ops[eng])
        waits = {}

        def need(tok):
            peng, pidx = tok
            if peng == eng and peng == 'pe':
                return
            sname, val = self._observe(peng, pidx)
            if self.clock[eng].get(sname, 0) >= val:
                return
            waits[sname] = max(waits.get(sname, 0), val)

        for tok in self.pending[eng]:
            need(tok)
        self.pending[eng] = []
        for r in reads:
            st = self.res.get(r)
            if st and st['w'] is not None:
                need(st['w'])
        for w in writes:
            st = self.res.get(w)
            if st:
                if st['w'] is not None:
                    need(st['w'])
                for t in st['r']:
                    need(t)
        tok = (eng, idx)
        for r in reads:
            self.res.setdefault(r, {'w': None, 'r': []})['r'].append(tok)
        for w in writes:
            self.res[w] = {'w': tok, 'r': []}
        for sname, val in waits.items():
            self.clock[eng][sname] = val
        o = {'fn': fn, 'waits': sorted(waits.items()), 'inc': None, 'dma': dma,
             'stream': stream}
        self.ops[eng].append(o)
        if dma:
            self.streams[stream] += 16
            o['inc'] = 16
        return tok

    def dma(self, q, out_ap, in_ap, reads=(), writes=(), slow=False,
            stream=None):
        return self.op(q, lambda E: E.dma_start(
            out_ap, in_ap, allow_slow_non_contiguous=slow),
            reads=reads, writes=writes, dma=True, stream=stream)

    def barrier(self):
        if self.pending_tp:
            self.tp_flush()
        toks = []
        for e in self.engines:
            ops = self.ops[e]
            k = len(ops) - 1
            while k >= 0 and ops[k]['fn'] is None:
                k -= 1
            if k >= 0:
                toks.append((e, k))
        for e in self.engines:
            self.pending[e] = list(self.pending[e]) + \
                [t for t in toks if t[0] != e]

    def wait_for(self, eng, tokens):
        waits = {}
        for (peng, pidx) in tokens:
            sname, val = self._observe(peng, pidx)
            if self.clock[eng].get(sname, 0) >= val:
                continue
            waits[sname] = max(waits.get(sname, 0), val)
        for sname, val in waits.items():
            self.clock[eng][sname] = val
        self.ops[eng].append({'fn': None, 'waits': sorted(waits.items()),
                              'inc': None, 'dma': False})

    def finalize(self):
        if self.pending_tp:
            self.tp_flush()
        waits = {}
        for sname, total in self.streams.items():
            if self.clock['sp'].get(sname, 0) < total:
                waits[sname] = total
                self.clock['sp'][sname] = total
        self.ops['sp'].append({'fn': None, 'waits': sorted(waits.items()),
                               'inc': None, 'dma': False, 'stream': None})

    def emit(self):
        semh = self.sem
        with self.nc.Block() as block:
            for eng in self.engines:
                ops = self.ops[eng]
                if not ops:
                    continue

                def make_body(ops=ops, eng=eng):
                    def body(E):
                        for o in ops:
                            for sname, val in o['waits']:
                                E.wait_ge(semh[sname], val)
                            if o['fn'] is not None:
                                ins = o['fn'](E)
                                if o['inc'] is not None:
                                    tgt = o['stream'] if o['dma'] else eng
                                    ins.then_inc(semh[tgt], o['inc'])
                    return body
                getattr(block, ENG2ATTR[eng])(make_body())


def make_consts(T):
    inv_freq = 1.0 / (10000.0 ** (np.arange(0, D, 2, dtype=np.float64) / D))
    t = np.arange(T, dtype=np.float64)
    freqs = np.outer(t, inv_freq)
    emb = np.concatenate([freqs, freqs], axis=-1)
    cos = np.cos(emb).astype(np.float32)
    sin = np.sin(emb).astype(np.float32)
    sin_s = sin.copy()
    sin_s[:, 0::2] = -sin_s[:, 0::2]
    j = np.arange(128)[:, None]
    i = np.arange(128)[None, :]
    mask_at = np.where(i >= j, NEG, 0.0).astype(np.float32)
    mask_bt = np.where(j > i, NEG, 0.0).astype(np.float32)
    ident = np.eye(128, dtype=ml_dtypes.bfloat16)
    return dict(cos_tab=cos, sin_tab=sin_s, mask_at=mask_at, mask_bt=mask_bt,
                ident=ident)


def build(T=2048):
    nc = bass.Bass('TRN2', target_bir_lowering=False)
    NTC = T // 128
    NG = T // 512
    HH = H // 2

    def dram(name, shape, dtype):
        kind = 'ExternalOutput' if name in DUMP else 'Internal'
        return nc.dram_tensor(name, shape, dtype, kind=kind)

    x_in = nc.dram_tensor('x', [T, C], F32, kind='ExternalInput')
    w_qkv = nc.dram_tensor('w_qkv', [3 * C, C], F32, kind='ExternalInput')
    w_o = nc.dram_tensor('w_o', [C, C], F32, kind='ExternalInput')
    pk = nc.dram_tensor('pk', [R, D], F32, kind='ExternalInput')
    pv = nc.dram_tensor('pv', [R, D], F32, kind='ExternalInput')
    uo = nc.dram_tensor('uo', [D, R], F32, kind='ExternalInput')
    info_w1 = nc.dram_tensor('info_w1', [512, C], F32, kind='ExternalInput')
    info_w2 = nc.dram_tensor('info_w2', [1, 512], F32, kind='ExternalInput')
    router_w = nc.dram_tensor('router_w', [E_MOE, C], F32, kind='ExternalInput')
    w_gate = nc.dram_tensor('w_gate', [E_MOE, NIC, 128, NCC, 128], F32, kind='ExternalInput')
    w_up = nc.dram_tensor('w_up', [E_MOE, NIC, 128, NCC, 128], F32, kind='ExternalInput')
    w_down = nc.dram_tensor('w_down', [E_MOE, I_MOE, C], F32, kind='ExternalInput')
    sh_gate = nc.dram_tensor('sh_gate', [4 * NIC, 128, NCC, 128], F32, kind='ExternalInput')
    sh_up = nc.dram_tensor('sh_up', [4 * NIC, 128, NCC, 128], F32, kind='ExternalInput')
    sh_down = nc.dram_tensor('sh_down', [I_SH, C], F32, kind='ExternalInput')
    sh_gate_w = nc.dram_tensor('sh_gate_w', [C, 1], F32, kind='ExternalInput')
    cos_in = nc.dram_tensor('cos_tab', [T, D], F32, kind='ExternalInput')
    sin_in = nc.dram_tensor('sin_tab', [T, D], F32, kind='ExternalInput')
    ma_in = nc.dram_tensor('mask_at', [128, 128], F32, kind='ExternalInput')
    mb_in = nc.dram_tensor('mask_bt', [128, 128], F32, kind='ExternalInput')
    id_in = nc.dram_tensor('ident', [128, 128], BF16, kind='ExternalInput')

    out = nc.dram_tensor('out', [T, C], F32, kind='ExternalOutput')

    xT_scr = dram('xT_scr', [NTC, 128, C], BF16)
    q_scr = dram('q_scr', [H, 128, T], BF16)
    ctx_scr = dram('ctx_scr', [H, 128, T], BF16)
    x1_scr = dram('x1_scr', [T, C], F32)
    x1T_scr = dram('x1T_scr', [NCC, 128, T], BF16)

    p = Prog(nc)

    PS0 = nc.alloc_psum_tensor('PS0', [128, 2048], F32)
    PS1 = nc.alloc_psum_tensor('PS1', [128, 1024], F32)
    PS2 = nc.alloc_psum_tensor('PS2', [128, 512], F32)
    PST = nc.alloc_psum_tensor('PST', [128, 1024], BF16)

    sb = nc.alloc_sbuf_tensor
    id_sb = sb('id_sb', [128, 128], BF16)
    pkT_sb = sb('pkT_sb', [128, R], BF16)
    pvT_sb = sb('pvT_sb', [128, R], BF16)
    uoT_sb = sb('uoT_sb', [64, 128], BF16)
    gate_sb = sb('gate_sb', [128, NTC], F32)
    ones128 = sb('ones128', [128, 1], BF16)
    ones1f = sb('ones1f', [1, 128], F32)
    sg_sb = sb('sg_sb', [128, NTC], F32)
    comb_sb = sb('comb_sb', [128, NTC * E_MOE], F32)
    st_sb = sb('st_sb', [128, 16], F32)

    p.dma('sp', id_sb[:, :], id_in.ap(), writes=['ident'], stream='ident')
    p.op('dve', lambda E: E.memset(ones128[:, :], 1.0), writes=['ones128'])
    p.op('dve', lambda E: E.memset(ones1f[:, :], 1.0), writes=['ones1f'])

    _tq = []
    _ectr = [0]
    _tflushing = [False]

    def tp_flush():
        if _tflushing[0] or not _tq:
            p.pending_tp.clear()
            return
        _tflushing[0] = True
        batch = list(_tq)
        del _tq[:]
        toks = []
        for (in_ap, in_keys, dst_ap, dst_keys, kpart, mfree, i) in batch:
            out_ap = PST[0:mfree, i * 128:i * 128 + kpart]
            tok = p.op('pe',
                       lambda E, out_ap=out_ap, in_ap=in_ap, kpart=kpart:
                       E.transpose(out_ap, in_ap, id_sb[0:kpart, 0:kpart]),
                       reads=list(in_keys) + ['ident'], writes=[f'PSTq{i}'])
            toks.append(tok)
        for (_, _, _, _, _, _, i) in batch:
            p.res[f'PSTq{i}'] = {'w': toks[-1], 'r': []}
        for (in_ap, in_keys, dst_ap, dst_keys, kpart, mfree, i) in batch:
            out_ap = PST[0:mfree, i * 128:i * 128 + kpart]
            _ectr[0] += 1
            if True:
                p.op('dve', lambda E, dst_ap=dst_ap, out_ap=out_ap:
                     E.tensor_copy(dst_ap, out_ap),
                     reads=[f'PSTq{i}'], writes=list(dst_keys))
            else:
                p.op('act', lambda E, dst_ap=dst_ap, out_ap=out_ap:
                     E.copy(dst_ap, out_ap),
                     reads=[f'PSTq{i}'], writes=list(dst_keys))
        p.pending_tp.clear()
        _tflushing[0] = False

    p.tp_flush = tp_flush

    def transpose_tile(in_ap, in_keys, dst_ap, dst_keys, kpart=128, mfree=128):
        if len(_tq) == 8:
            tp_flush()
        i = len(_tq)
        _tq.append((in_ap, list(in_keys), dst_ap, list(dst_keys), kpart,
                    mfree, i))
        p.pending_tp.update(in_keys)
        p.pending_tp.update(dst_keys)
        p.pending_tp.add(f'PSTq{i}')

    def maybe_dump(name, ap, shape, reads, dtype=F32):
        if name in DUMP:
            t = nc.dram_tensor(name, shape, dtype, kind='ExternalOutput')
            p.dma('sp', t.ap(), ap, reads=reads, stream='dump')

    def ln_epilogue(r_ap, r_keys, dst_ap, dst_key, junk_ap, junk_key):
        p.op('dve', lambda E: E.reduce_sum(st_sb[:, 0:1], r_ap, axis=AX.X),
             reads=r_keys, writes=['st0'])
        p.op('dve', lambda E: E.tensor_single_scalar(
            st_sb[:, 1:2], st_sb[:, 0:1], -1.0 / C, ALU.mult),
            reads=['st0'], writes=['st1'])
        p.op('dve', lambda E: E.scalar_tensor_tensor(
            junk_ap, r_ap, st_sb[:, 1:2], r_ap, ALU.add, ALU.mult,
            accum_out=st_sb[:, 2:3]),
            reads=r_keys + ['st1'], writes=[junk_key, 'st2'])
        p.op('dve', lambda E: E.tensor_scalar(
            st_sb[:, 3:4], st_sb[:, 2:3], 1.0 / C, EPS, ALU.mult, ALU.add),
            reads=['st2'], writes=['st3'])
        p.op('act', lambda E: E.activation(st_sb[:, 4:5], st_sb[:, 3:4],
                                           AF.Sqrt),
             reads=['st3'], writes=['st4'])
        p.op('dve', lambda E: E.reciprocal(st_sb[:, 5:6], st_sb[:, 4:5]),
             reads=['st4'], writes=['st5'])
        p.op('dve', lambda E: E.tensor_scalar(
            dst_ap, r_ap, st_sb[:, 1:2], st_sb[:, 5:6], ALU.add, ALU.mult),
            reads=r_keys + ['st1', 'st5'], writes=[dst_key])

    astack = ExitStack()

    def asb(name, shape, dtype):
        return astack.enter_context(nc.sbuf_tensor(name, shape, dtype))

    cos_sb = asb('cos_sb', [128, NTC * D], F32)
    sin_sb = asb('sin_sb', [128, NTC * D], F32)
    ma_sb = asb('ma_sb', [128, 128], F32)
    mb_sb = asb('mb_sb', [128, 128], F32)
    gate_row = asb('gate_row', [1, T], F32)

    p.dma('sp', cos_sb[:, :].rearrange('p (tc d) -> p tc d', d=D),
          cos_in.ap().rearrange('(tc p) d -> p tc d', p=128), writes=['cos'],
          stream='cos')
    p.dma('sp', sin_sb[:, :].rearrange('p (tc d) -> p tc d', d=D),
          sin_in.ap().rearrange('(tc p) d -> p tc d', p=128), writes=['sin'],
          stream='sin')
    p.dma('sp', ma_sb[:, :], ma_in.ap(), writes=['ma'], stream='ma')
    p.dma('sp', mb_sb[:, :], mb_in.ap(), writes=['mb'], stream='mb')

    with nc.sbuf_tensor('tmp_f', [128, 128], F32) as tmp_f, \
            nc.sbuf_tensor('tmp_b', [128, 128], BF16) as tmp_b:
        for src, dst, key in ((pk, pkT_sb, 'pkT'), (pv, pvT_sb, 'pvT')):
            p.dma('sp', tmp_f[0:R, :], src.ap(), writes=['tmpf'], stream='tmpf')
            p.op('dve', lambda E: E.tensor_copy(tmp_b[0:R, :], tmp_f[0:R, :]),
                 reads=['tmpf'], writes=['tmpb'])
            transpose_tile(tmp_b[0:R, :], ['tmpb'], dst[:, :], [key],
                           kpart=R, mfree=128)
        p.dma('sp', tmp_f[:, 0:R], uo.ap(), writes=['tmpf'], stream='tmpf')
        p.op('dve', lambda E: E.tensor_copy(tmp_b[:, 0:R], tmp_f[:, 0:R]),
             reads=['tmpf'], writes=['tmpb'])
        transpose_tile(tmp_b[:, 0:R], ['tmpb'], uoT_sb[:, :], ['uoT'],
                       kpart=128, mfree=R)
    p.barrier()

    if 'a0' in PHASES:
        with ExitStack() as stack:
            zb = lambda n, s, d: stack.enter_context(nc.sbuf_tensor(n, s, d))
            xld = [zb(f'a0_xld{i}', [128, C], F32) for i in range(2)]
            xbf = [zb(f'a0_xbf{i}', [128, C], BF16) for i in range(2)]
            xtc = [zb(f'a0_xtc{i}', [128, C], BF16) for i in range(2)]
            iw1f = zb('a0_iw1f', [128, C], F32)
            iw1b = zb('a0_iw1b', [128, C], BF16)
            iw1T = zb('a0_iw1T', [128, NCC * 512], BF16)
            w2c = zb('a0_w2c', [128, 4], F32)
            w2b = zb('a0_w2b', [128, 4], BF16)
            hinfT = zb('a0_hinfT', [128, 4 * 128], BF16)
            sig = zb('a0_sig', [128, 1], F32)
            gbft = zb('a0_gbft', [128, 1], BF16)
            ps_in = PS1[:, 0:128]
            ps_sc = PS2[:, 0:1]

            for fc in range(4):
                p.dma('sp', iw1f[:, :],
                      info_w1.ap()[fc * 128:(fc + 1) * 128, :],
                      writes=['iw1f'], stream='iw1f')
                p.op('dve', lambda E: E.tensor_copy(iw1b[:, :], iw1f[:, :]),
                     reads=['iw1f'], writes=['iw1b'])
                for cc in range(NCC):
                    transpose_tile(iw1b[:, cc * 128:(cc + 1) * 128], ['iw1b'],
                                   iw1T[:, cc * 512 + fc * 128:
                                        cc * 512 + (fc + 1) * 128],
                                   [f'iw1T_{cc}_{fc}'])
            for fc in range(4):
                p.dma('sp', w2c[:, fc:fc + 1],
                      info_w2.ap()[:, fc * 128:(fc + 1) * 128]
                      .rearrange('one p -> p one'),
                      writes=['w2c'], stream='w2c')
            p.op('dve', lambda E: E.tensor_copy(w2b[:, :], w2c[:, :]),
                 reads=['w2c'], writes=['w2b'])

            for tc in range(NTC):
                b = tc % 2
                p.dma('pool', xld[b][:, :],
                      x_in.ap()[tc * 128:(tc + 1) * 128, :],
                      writes=[f'a0_xld{b}'], stream=f'xld{b}')
                p.op('dve', lambda E, b=b: E.tensor_copy(xbf[b][:, :],
                                                         xld[b][:, :]),
                     reads=[f'a0_xld{b}'], writes=[f'a0_xbf{b}'])
                for cc in range(NCC):
                    transpose_tile(xbf[b][:, cc * 128:(cc + 1) * 128],
                                   [f'a0_xbf{b}'],
                                   xtc[b][:, cc * 128:(cc + 1) * 128],
                                   [f'a0_xtc{b}_{cc}'])
                p.dma('pool', xT_scr.ap()[tc], xtc[b][:, :],
                      reads=[f'a0_xtc{b}_{cc}' for cc in range(NCC)],
                      writes=[f'xT_{tc}'], stream='xT_w')
                for fc in range(4):
                    for cc in range(NCC):
                        p.op('pe', lambda E, b=b, cc=cc, fc=fc: E.matmul(
                            ps_in,
                            iw1T[:, cc * 512 + fc * 128: cc * 512 + (fc + 1) * 128],
                            xtc[b][:, cc * 128:(cc + 1) * 128],
                            start=(cc == 0), stop=(cc == NCC - 1)),
                            reads=[f'iw1T_{cc}_{fc}', f'a0_xtc{b}_{cc}'],
                            writes=['ps_in'])
                    p.op('act', lambda E, fc=fc: E.activation(
                        hinfT[:, fc * 128:(fc + 1) * 128], ps_in, AF.Gelu),
                        reads=['ps_in'], writes=[f'hinfT{fc}'])
                for fc in range(4):
                    p.op('pe', lambda E, fc=fc: E.matmul(
                        ps_sc, hinfT[:, fc * 128:(fc + 1) * 128],
                        w2b[:, fc:fc + 1], start=(fc == 0), stop=(fc == 3)),
                        reads=[f'hinfT{fc}', 'w2b'], writes=['ps_sc'])
                p.op('act', lambda E: E.activation(sig[:, :], ps_sc,
                                                   AF.Sigmoid),
                     reads=['ps_sc'], writes=['a0_sig'])
                p.op('dve', lambda E, tc=tc: E.tensor_single_scalar(
                    gate_sb[:, tc:tc + 1], sig[:, :], DELTA_I, ALU.is_gt),
                    reads=['a0_sig'], writes=['gate'])
                p.op('dve', lambda E, tc=tc: E.tensor_copy(
                    gbft[:, :], gate_sb[:, tc:tc + 1]),
                    reads=['gate'], writes=['a0_gbft'])
                transpose_tile(gbft[:, :], ['a0_gbft'],
                               gate_row[:, tc * 128:(tc + 1) * 128],
                               [f'gate_row{tc}'], kpart=128, mfree=1)
            maybe_dump('gate_dbg', gate_sb[:, :], [128, NTC], ['gate'])
            p.barrier()

    if 'a1' in PHASES:
        with ExitStack() as stack:
            zb = lambda n, s, d: stack.enter_context(nc.sbuf_tensor(n, s, d))
            kT_sb = zb('kT_sb', [128, HH * T], BF16)
            v_sb = zb('v_sb', [128, NTC * HH * 128], BF16)
            wblk = zb('wblk', [128, NCC * 512], BF16)
            wld = [zb(f'a1_wld{i}', [128, C], F32) for i in range(2)]
            wbf = [zb(f'a1_wbf{i}', [128, C], BF16) for i in range(2)]
            xtc1 = [zb(f'a1_xtc{i}', [128, C], BF16) for i in range(2)]
            rst = zb('a1_rst', [128, 512], F32)
            rst2 = zb('a1_rst2', [128, 512], F32)
            ropeo = zb('a1_ropeo', [128, 512], BF16)
            qTst = zb('a1_qTst', [128, 512], BF16)
            qT_h = zb('qT_h', [128, T], BF16)
            krT = zb('krT', [64, T], BF16)
            qrT = zb('qrT', [64, T], BF16)
            vT_h = zb('vT_h', [128, T], BF16)
            vr = zb('vr', [128, NTC * 64], BF16)
            vrst = zb('vrst', [64, 512], BF16)
            ptg = zb('ptg', [128, NTC * 512], BF16)
            ptl = zb('ptl', [128, 8 * 512], BF16)
            cgr_sb = zb('cgr_sb', [64, 512], BF16)
            ctxst = zb('ctxst', [128, 512], BF16)
            lg_row = zb('lg_row', [1, 512], F32)
            ll_row = zb('ll_row', [1, 512], F32)
            rlg_row = zb('rlg_row', [1, 512], F32)
            rll_row = zb('rll_row', [1, 512], F32)
            f2_row = zb('f2_row', [1, 512], F32)
            bf2_sb = zb('bf2_sb', [128, 512], F32)
            brl_sb = zb('brl_sb', [128, 512], F32)
            ps_qkv = [PS0[:, 0:512], PS0[:, 512:1024]]
            ps_a = PS1[:, 0:512]
            psrow = PS1[0:1, 512:1024]
            psbc = PS1[:, 512:1024]
            ps_ctx = PS2

            def rope_evac(psum, pskey, tc):
                cs = cos_sb[:, tc * D:(tc + 1) * D]
                sn = sin_sb[:, tc * D:(tc + 1) * D]
                for j in range(4):
                    o = j * 128
                    p.op('dve', lambda E, o=o: E.tensor_tensor(
                        rst[:, o:o + 128], psum[:, o:o + 128], cs, ALU.mult),
                        reads=[pskey, 'cos'], writes=[f'a1_rst{j}'])
                    p.op('dve', lambda E, o=o: E.tensor_tensor(
                        rst2[:, o:o + 128:2], psum[:, o + 1:o + 128:2],
                        sn[:, 0:128:2], ALU.mult),
                        reads=[pskey, 'sin'], writes=[f'a1_rst2e{j}'])
                    p.op('dve', lambda E, o=o: E.tensor_tensor(
                        rst2[:, o + 1:o + 128:2], psum[:, o:o + 128:2],
                        sn[:, 1:128:2], ALU.mult),
                        reads=[pskey, 'sin'], writes=[f'a1_rst2o{j}'])
                p.op('dve', lambda E: E.tensor_tensor(
                    ropeo[:, :], rst[:, :], rst2[:, :], ALU.add),
                    reads=[k for j in range(4) for k in
                           (f'a1_rst{j}', f'a1_rst2e{j}', f'a1_rst2o{j}')],
                    writes=['a1_ropeo'])

            for X in range(2):
                for proj, off in ((('v', 0), ('v', 512), ('k', 0), ('k', 512),
                                   ('q', 0), ('q', 512)) if A1A else ()):
                    base = {'q': 0, 'k': C, 'v': 2 * C}[proj] + X * 1024 + off
                    for s4 in range(4):
                        b = s4 % 2
                        p.dma('sp', wld[b][:, :],
                              w_qkv.ap()[base + s4 * 128: base + (s4 + 1) * 128, :],
                              writes=[f'a1_wld{b}'], stream=f'wld{b}')
                        p.op('dve', lambda E, b=b: E.tensor_copy(
                            wbf[b][:, :], wld[b][:, :]),
                            reads=[f'a1_wld{b}'], writes=[f'a1_wbf{b}'])
                        for cc in range(NCC):
                            transpose_tile(wbf[b][:, cc * 128:(cc + 1) * 128],
                                           [f'a1_wbf{b}'],
                                           wblk[:, cc * 512 + s4 * 128:
                                                cc * 512 + (s4 + 1) * 128],
                                           [f'wblk_{cc}_{s4}'])
                    for tc in range(NTC):
                        b = tc % 2
                        p.dma('pool', xtc1[b][:, :], xT_scr.ap()[tc],
                              reads=[f'xT_{tc}'], writes=[f'a1_xtc{b}'],
                              stream=f'xtc{b}')
                        for cc in range(NCC):
                            p.op('pe', lambda E, b=b, cc=cc: E.matmul(
                                ps_qkv[b],
                                xtc1[b][:, cc * 128:(cc + 1) * 128],
                                wblk[:, cc * 512:(cc + 1) * 512],
                                start=(cc == 0), stop=(cc == NCC - 1)),
                                reads=[f'a1_xtc{b}'] +
                                [f'wblk_{cc}_{s4}' for s4 in range(4)],
                                writes=[f'ps_qkv{b}'])
                        if proj == 'v':
                            p.op('dve', lambda E, b=b, tc=tc, off=off:
                                 E.tensor_copy(
                                     v_sb[:, tc * HH * 128 + off:
                                          tc * HH * 128 + off + 512],
                                     ps_qkv[b]),
                                 reads=[f'ps_qkv{b}'], writes=[f'v_{tc}_{off}'])
                        else:
                            rope_evac(ps_qkv[b], f'ps_qkv{b}', tc)
                            for j in range(4):
                                hh = off // 128 + j
                                if proj == 'k':
                                    transpose_tile(
                                        ropeo[:, j * 128:(j + 1) * 128],
                                        ['a1_ropeo'],
                                        kT_sb[:, hh * T + tc * 128:
                                              hh * T + (tc + 1) * 128],
                                        [f'kT_{hh}_{tc}'])
                                else:
                                    transpose_tile(
                                        ropeo[:, j * 128:(j + 1) * 128],
                                        ['a1_ropeo'],
                                        qTst[:, j * 128:(j + 1) * 128],
                                        [f'a1_qTst{j}'])
                            if proj == 'q':
                                h0 = X * HH + off // 128
                                p.dma('pool',
                                      q_scr.ap()[h0:h0 + 4, :,
                                                 tc * 128:(tc + 1) * 128]
                                      .rearrange('h p t -> p h t'),
                                      qTst[:, :].rearrange(
                                          'p (h t) -> p h t', t=128),
                                      reads=[f'a1_qTst{j}'
                                             for j in range(4)],
                                      writes=[f'q_{h0}_{tc}'],
                                      stream='q_w')
                if X == 0:
                    maybe_dump('v_dbg', v_sb[:, :], [128, NTC * HH * 128],
                               [f'v_{tc}_{o}' for tc in range(NTC)
                                for o in (0, 512)], dtype=BF16)
                    maybe_dump('kT_dbg', kT_sb[:, :], [128, HH * T],
                               [f'kT_{hh}_{tc}' for hh in range(HH)
                                for tc in range(NTC)], dtype=BF16)
                p.barrier()

                for hh in (range(HH) if A1B else []):
                    h_abs = X * HH + hh
                    h0 = X * HH + (hh // 4) * 4
                    p.dma('pool', qT_h[:, :], q_scr.ap()[h_abs],
                          reads=[f'q_{h0}_{tc}' for tc in range(NTC)],
                          writes=['qT_h'], stream='qTh')
                    for sgp in (range(NG) if A1B_KR else []):
                        p.op('pe', lambda E, sgp=sgp, hh=hh: E.matmul(
                            ps_a[0:64, :], pkT_sb[:, :],
                            kT_sb[:, hh * T + sgp * 512:
                                  hh * T + (sgp + 1) * 512],
                            start=True, stop=True),
                            reads=['pkT'], writes=['ps_a'])
                        p.op('dve', lambda E, sgp=sgp: E.tensor_copy(
                            krT[:, sgp * 512:(sgp + 1) * 512], ps_a[0:64, :]),
                            reads=['ps_a'], writes=['krT'])
                        p.op('pe', lambda E, sgp=sgp: E.matmul(
                            ps_a[0:64, :], pkT_sb[:, :],
                            qT_h[:, sgp * 512:(sgp + 1) * 512],
                            start=True, stop=True),
                            reads=['pkT', 'qT_h'], writes=['ps_a'])
                        p.op('dve', lambda E, sgp=sgp: E.tensor_copy(
                            qrT[:, sgp * 512:(sgp + 1) * 512], ps_a[0:64, :]),
                            reads=['ps_a'], writes=['qrT'])
                    voff = (hh // 4) * 512
                    for tc in (range(NTC) if A1B_VR else []):
                        transpose_tile(
                            v_sb[:, tc * HH * 128 + hh * 128:
                                 tc * HH * 128 + (hh + 1) * 128],
                            [f'v_{tc}_{voff}'],
                            vT_h[:, tc * 128:(tc + 1) * 128], [f'vT_{tc}'])
                    for sgp in (range(NG) if (A1B_VR and A1B_VR2) else []):
                        p.op('pe', lambda E, sgp=sgp: E.matmul(
                            ps_a[0:64, :], pvT_sb[:, :],
                            vT_h[:, sgp * 512:(sgp + 1) * 512],
                            start=True, stop=True),
                            reads=['pvT'] + [f'vT_{sgp * 4 + j}'
                                             for j in range(4)],
                            writes=['ps_a'])
                        p.op('dve', lambda E: E.tensor_copy(
                            vrst[:, :], ps_a[0:64, :]),
                            reads=['ps_a'], writes=['vrst'])
                        for j in range(4):
                            transpose_tile(
                                vrst[:, j * 128:(j + 1) * 128], ['vrst'],
                                vr[:, (sgp * 4 + j) * 64:
                                   (sgp * 4 + j + 1) * 64],
                                [f'vr_{sgp * 4 + j}'], kpart=64, mfree=128)

                    for g in (range(NG) if A1B_ROWS else []):
                        G0 = g * 512
                        g4 = g * 4
                        nch = min(4, NTC - g4)
                        assert nch == 4
                        for k2 in (range(NTC // 2) if A1B_G else []):
                            for j in range(2):
                                sc = 2 * k2 + j
                                q = sc % 4
                                p.op('pe', lambda E, sc=sc, q=q, G0=G0:
                                     E.matmul(
                                         PS0[:, q * 512:(q + 1) * 512],
                                         krT[:, sc * 128:(sc + 1) * 128],
                                         qrT[:, G0:G0 + 512],
                                         start=True, stop=True),
                                     reads=['krT', 'qrT'],
                                     writes=[f'PS0q{q}'])
                            q0 = (2 * k2) % 4
                            p.op('act', lambda E, k2=k2, q0=q0: E.activation(
                                ptg[:, 2 * k2 * 512: 2 * (k2 + 1) * 512],
                                PS0[:, q0 * 512:(q0 + 2) * 512],
                                AF.Exp, scale=SCALE_G),
                                reads=[f'PS0q{q0}', f'PS0q{q0 + 1}'],
                                writes=[f'ptg_{2 * k2}', f'ptg_{2 * k2 + 1}'])
                        for sc in (range(NTC) if A1B_G else []):
                            p.op('pe', lambda E, sc=sc: E.matmul(
                                psrow, ones128[:, :],
                                ptg[:, sc * 512:(sc + 1) * 512],
                                start=(sc == 0), stop=(sc == NTC - 1)),
                                reads=[f'ptg_{sc}', 'ones128'],
                                writes=['ps1b'])
                        p.op('dve', lambda E: E.tensor_copy(lg_row[:, :],
                                                            psrow),
                             reads=['ps1b'], writes=['lg_row'])
                        lvalid = []
                        for sc_abs in range(max(0, g4 - 4), g4 + nch):
                            slot = sc_abs - (g4 - 4)
                            lo = max(0, sc_abs - g4)
                            hi = min(nch - 1, sc_abs - g4 + 4)
                            lvalid.append((sc_abs, slot, lo, hi))
                        lvalid.sort(key=lambda v: v[0] != g4)
                        for n_i, (sc_abs, slot, lo, hi) in (
                                enumerate(lvalid) if A1B_L else []):
                            w = (hi - lo + 1) * 128
                            q = slot % 4
                            p.op('pe', lambda E, sc_abs=sc_abs, lo=lo, w=w,
                                 q=q, hh=hh, g4=g4: E.matmul(
                                     PS0[:, q * 512: q * 512 + w],
                                     kT_sb[:, hh * T + sc_abs * 128:
                                           hh * T + (sc_abs + 1) * 128],
                                     qT_h[:, (g4 + lo) * 128:
                                          (g4 + lo) * 128 + w],
                                     start=True, stop=True),
                                 reads=['qT_h'], writes=[f'PS0q{q}'])
                            if g4 <= sc_abs:
                                col = (sc_abs - g4 - lo) * 128
                                p.op('dve', lambda E, q=q, col=col:
                                     E.tensor_tensor(
                                         PS0[:, q * 512 + col:
                                             q * 512 + col + 128],
                                         PS0[:, q * 512 + col:
                                             q * 512 + col + 128],
                                         mb_sb[:, :], ALU.add),
                                     reads=['mb', f'PS0q{q}'],
                                     writes=[f'PS0q{q}'])
                            else:
                                col = (sc_abs + 4 - g4 - lo) * 128
                                p.op('dve', lambda E, q=q, col=col:
                                     E.tensor_tensor(
                                         PS0[:, q * 512 + col:
                                             q * 512 + col + 128],
                                         PS0[:, q * 512 + col:
                                             q * 512 + col + 128],
                                         ma_sb[:, :], ALU.add),
                                     reads=['ma', f'PS0q{q}'],
                                     writes=[f'PS0q{q}'])
                            p.op('act', lambda E, slot=slot, lo=lo, w=w, q=q:
                                 E.activation(
                                     ptl[:, slot * 512 + lo * 128:
                                         slot * 512 + lo * 128 + w],
                                     PS0[:, q * 512: q * 512 + w],
                                     AF.Exp, scale=SCALE),
                                 reads=[f'PS0q{q}'], writes=[f'ptl_{slot}'])
                        for n_i, (sc_abs, slot, lo, hi) in (
                                enumerate(lvalid) if A1B_L else []):
                            w = (hi - lo + 1) * 128
                            p.op('pe', lambda E, slot=slot, lo=lo, w=w,
                                 st=(n_i == 0),
                                 sp=(n_i == len(lvalid) - 1): E.matmul(
                                     psrow[:, lo * 128: lo * 128 + w],
                                     ones128[:, :],
                                     ptl[:, slot * 512 + lo * 128:
                                         slot * 512 + lo * 128 + w],
                                     start=st, stop=sp),
                                 reads=[f'ptl_{slot}', 'ones128'],
                                 writes=['ps1b'])
                        p.op('dve', lambda E: E.tensor_copy(ll_row[:, :],
                                                            psrow),
                             reads=['ps1b'], writes=['ll_row'])
                        p.op('dve', lambda E: E.reciprocal(rlg_row[:, :],
                                                           lg_row[:, :]),
                             reads=['lg_row'], writes=['rlg_row'])
                        p.op('dve', lambda E: E.reciprocal(rll_row[:, :],
                                                           ll_row[:, :]),
                             reads=['ll_row'], writes=['rll_row'])
                        p.op('dve', lambda E, G0=G0: E.tensor_tensor(
                            f2_row[:, :], rlg_row[:, :],
                            gate_row[:, G0:G0 + 512], ALU.mult),
                            reads=['rlg_row'] + [f'gate_row{g4 + c}'
                                                 for c in range(4)],
                            writes=['f2_row'])
                        p.op('dve', lambda E: E.tensor_tensor(
                            f2_row[:, :], f2_row[:, :], ll_row[:, :],
                            ALU.mult),
                            reads=['f2_row', 'll_row'], writes=['f2_row'])
                        p.op('pe', lambda E: E.matmul(
                            psbc, ones1f[:, :], f2_row[:, :],
                            start=True, stop=True),
                            reads=['f2_row', 'ones1f'], writes=['ps1b'])
                        p.op('dve', lambda E: E.tensor_copy(bf2_sb[:, :],
                                                            psbc),
                             reads=['ps1b'], writes=['bf2'])
                        p.op('pe', lambda E: E.matmul(
                            psbc, ones1f[:, :], rll_row[:, :],
                            start=True, stop=True),
                            reads=['rll_row', 'ones1f'], writes=['ps1b'])
                        p.op('act', lambda E: E.copy(brl_sb[:, :], psbc),
                             reads=['ps1b'], writes=['brl'])
                        for sc in (range(NTC) if A1B_G else []):
                            p.op('pe', lambda E, sc=sc: E.matmul(
                                ps_a[0:64, :],
                                vr[:, sc * 64:(sc + 1) * 64],
                                ptg[:, sc * 512:(sc + 1) * 512],
                                start=(sc == 0), stop=(sc == NTC - 1)),
                                reads=[f'vr_{sc}', f'ptg_{sc}'],
                                writes=['ps_a'])
                        p.op('dve', lambda E: E.tensor_tensor(
                            cgr_sb[:, :], ps_a[0:64, :], bf2_sb[0:64, :],
                            ALU.mult),
                            reads=['ps_a', 'bf2'], writes=['cgr'])
                        p.op('pe', lambda E: E.matmul(
                            ps_ctx[:, :], uoT_sb[:, :], cgr_sb[:, :],
                            start=True, stop=not A1B_L),
                            reads=['uoT', 'cgr'], writes=['ps_ctx'])
                        for n_i, (sc_abs, slot, lo, hi) in (
                                enumerate(lvalid) if A1B_L else []):
                            p.op('pe', lambda E, sc_abs=sc_abs, slot=slot,
                                 lo=lo, hi=hi, hh=hh,
                                 last=(n_i == len(lvalid) - 1): E.matmul(
                                ps_ctx[:, lo * 128:(hi + 1) * 128],
                                v_sb[:, sc_abs * HH * 128 + hh * 128:
                                     sc_abs * HH * 128 + (hh + 1) * 128],
                                ptl[:, slot * 512 + lo * 128:
                                    slot * 512 + (hi + 1) * 128],
                                start=False, stop=last),
                                reads=[f'ptl_{slot}'], writes=['ps_ctx'])
                        p.op('dve', lambda E: E.tensor_tensor(
                            ctxst[:, :], ps_ctx[:, :], brl_sb[:, :], ALU.mult),
                            reads=['ps_ctx', 'brl'], writes=['ctxst'])
                        p.dma('pool', ctx_scr.ap()[h_abs][:, G0:G0 + 512],
                              ctxst[:, :], reads=['ctxst'],
                              writes=[f'ctx_{h_abs}_{g}'], stream='ctx_w')
                p.barrier()
    astack.close()

    if 'a2' in PHASES:
        with ExitStack() as stack:
            zb = lambda n, s, d: stack.enter_context(nc.sbuf_tensor(n, s, d))
            woT = zb('woT', [128, NCC * C], BF16)
            wld2 = [zb(f'a2_wld{i}', [128, C], F32) for i in range(2)]
            wbf2 = [zb(f'a2_wbf{i}', [128, C], BF16) for i in range(2)]
            ctxc = [zb(f'a2_ctxc{i}', [128, C], BF16) for i in range(2)]
            xrow = [zb(f'a2_xrow{i}', [128, C], F32) for i in range(2)]
            r_sb = zb('a2_r', [128, C], F32)
            x1f = zb('a2_x1f', [128, C], F32)
            x1b = zb('a2_x1b', [128, C], BF16)
            x1Tst = zb('a2_x1Tst', [128, C], BF16)
            junk = zb('a2_junk', [128, C], F32)

            for s in range(NCC):
                b = s % 2
                p.dma('sp', wld2[b][:, :], w_o.ap()[s * 128:(s + 1) * 128, :],
                      writes=[f'a2_wld{b}'], stream=f'wld{b}')
                p.op('dve', lambda E, b=b: E.tensor_copy(wbf2[b][:, :],
                                                         wld2[b][:, :]),
                     reads=[f'a2_wld{b}'], writes=[f'a2_wbf{b}'])
                for cc in range(NCC):
                    transpose_tile(wbf2[b][:, cc * 128:(cc + 1) * 128],
                                   [f'a2_wbf{b}'],
                                   woT[:, cc * C + s * 128:
                                       cc * C + (s + 1) * 128],
                                   [f'woT_{cc}_{s}'])

            for tc in range(NTC):
                b = tc % 2
                p.dma('pool', ctxc[b][:, :].rearrange('p (h t) -> p h t', t=128),
                      ctx_scr.ap()[:, :, tc * 128:(tc + 1) * 128]
                      .rearrange('h p t -> p h t'),
                      reads=[f'ctx_{h}_{tc // 4}' for h in range(H)],
                      writes=[f'a2_ctxc{b}'], stream=f'ctxc{b}')
                p.dma('sp', xrow[b][:, :], x_in.ap()[tc * 128:(tc + 1) * 128, :],
                      writes=[f'a2_xrow{b}'], stream=f'xrow{b}')
                for cog in range(4):
                    for cc in range(NCC):
                        p.op('pe', lambda E, b=b, cc=cc, cog=cog: E.matmul(
                            PS0[:, cog * 512:(cog + 1) * 512],
                            ctxc[b][:, cc * 128:(cc + 1) * 128],
                            woT[:, cc * C + cog * 512: cc * C + (cog + 1) * 512],
                            start=(cc == 0), stop=(cc == NCC - 1)),
                            reads=[f'a2_ctxc{b}'] +
                            [f'woT_{cc}_{s}' for s in range(cog * 4, cog * 4 + 4)],
                            writes=['PS0a'])
                p.op('dve', lambda E, b=b: E.tensor_tensor(
                    r_sb[:, :], PS0[:, :], xrow[b][:, :], ALU.add),
                    reads=['PS0a', f'a2_xrow{b}'], writes=['a2_r'])
                ln_epilogue(r_sb[:, :], ['a2_r'], x1f[:, :], 'a2_x1f',
                            junk[:, :], 'a2_junk')
                p.op('dve', lambda E: E.tensor_copy(x1b[:, :], x1f[:, :]),
                     reads=['a2_x1f'], writes=['a2_x1b'])
                p.dma('sp', x1_scr.ap()[tc * 128:(tc + 1) * 128, :], x1f[:, :],
                      reads=['a2_x1f'], writes=[f'x1_{tc}'], stream='x1_w')
                for cc in range(NCC):
                    transpose_tile(x1b[:, cc * 128:(cc + 1) * 128],
                                   ['a2_x1b'],
                                   x1Tst[:, cc * 128:(cc + 1) * 128],
                                   [f'a2_x1Tst{cc}'])
                p.dma('pool',
                      x1T_scr.ap()[:, :, tc * 128:(tc + 1) * 128]
                      .rearrange('cc p t -> p cc t'),
                      x1Tst[:, :].rearrange('p (cc t) -> p cc t', t=128),
                      reads=[f'a2_x1Tst{cc}' for cc in range(NCC)],
                      writes=[f'x1T_{tc}'], stream='x1T_w')
            p.barrier()

    if 'f' in PHASES:
        with ExitStack() as stack:
            zb = lambda n, s, d: stack.enter_context(nc.sbuf_tensor(n, s, d))
            TH = T // 2
            NTC_H = TH // 128
            NTT = TH // 512
            routT = zb('f_routT', [128, NCC * 16], BF16)
            shw = zb('f_shw', [128, NCC], BF16)
            with nc.sbuf_tensor('f_rTl', [16, C], F32) as rTl, \
                    nc.sbuf_tensor('f_rTb', [16, C], BF16) as rTb, \
                    nc.sbuf_tensor('f_shwf', [128, NCC], F32) as shwf:
                p.dma('sp', rTl[:, :], router_w.ap(), writes=['f_rTl'], stream='rTl')
                p.op('dve', lambda E: E.tensor_copy(rTb[:, :], rTl[:, :]),
                     reads=['f_rTl'], writes=['f_rTb'])
                for cc in range(NCC):
                    transpose_tile(rTb[0:16, cc * 128:(cc + 1) * 128],
                                   ['f_rTb'],
                                   routT[:, cc * 16:(cc + 1) * 16],
                                   [f'routT{cc}'], kpart=16, mfree=128)
                p.dma('sp', shwf[:, :].rearrange('p (cc one) -> p cc one', one=1),
                      sh_gate_w.ap().rearrange('(cc p) one -> p cc one', p=128),
                      writes=['f_shwf'], slow=True, stream='shwf')
                p.op('dve', lambda E: E.tensor_copy(shw[:, :], shwf[:, :]),
                     reads=['f_shwf'], writes=['f_shw'])
                p.barrier()

            x1Th = zb('x1Th', [128, NCC * TH], BF16)
            acc = zb('acc', [128, NTC_H * C], F32)
            aT = zb('aT', [128, NIC * TH], BF16)
            wgl = [zb(f'f_wgl{i}', [128, C], F32) for i in range(2)]
            wgb = [[zb(f'f_wgb{i}_{j}', [128, C], BF16) for j in range(2)]
                   for i in range(2)]
            wdb = zb('f_wdb', [128, NIC * C], BF16)
            wdl = [zb(f'f_wdl{i}', [128, 1024], F32) for i in range(2)]
            prb = zb('f_prb', [128, 16], F32)
            prw = zb('f_prw', [128, 16], F32)
            j16 = zb('f_j16', [128, 16], F32)
            sgst = zb('f_sgst', [128, 512], BF16)
            x1r = [wgl[0]]
            ps_g = [PS0[:, 0:512], PS0[:, 512:1024]]
            ps_u = [PS0[:, 1024:1536], PS0[:, 1536:2048]]
            ps_d = [PS1[:, 0:512], PS1[:, 512:1024]]
            ps_r = PS2[:, 0:16]
            ps_s = PS2[:, 128:129]

            for half in range(2):
                t0 = half * TH
                tc0 = t0 // 128
                for cc in range(NCC):
                    p.dma('pool', x1Th[:, cc * TH:(cc + 1) * TH],
                          x1T_scr.ap()[cc][:, t0:t0 + TH],
                          reads=[f'x1T_{tc0 + i}' for i in range(NTC_H)],
                          writes=[f'x1Th_{cc}'], stream='x1Th')
                x1Tkeys = [f'x1Th_{cc}' for cc in range(NCC)]
                for tl in range(NTC_H):
                    tc = tc0 + tl
                    for cc in range(NCC):
                        p.op('pe', lambda E, cc=cc, tl=tl: E.matmul(
                            ps_r, x1Th[:, cc * TH + tl * 128:
                                       cc * TH + (tl + 1) * 128],
                            routT[:, cc * 16:(cc + 1) * 16],
                            start=(cc == 0), stop=(cc == NCC - 1)),
                            reads=[f'x1Th_{cc}', f'routT{cc}'],
                            writes=['ps_r'])
                    for cc in range(NCC):
                        p.op('pe', lambda E, cc=cc, tl=tl: E.matmul(
                            ps_s, x1Th[:, cc * TH + tl * 128:
                                       cc * TH + (tl + 1) * 128],
                            shw[:, cc:cc + 1],
                            start=(cc == 0), stop=(cc == NCC - 1)),
                            reads=[f'x1Th_{cc}', 'f_shw'], writes=['ps_s'])
                    p.op('act', lambda E, tc=tc: E.activation(
                        sg_sb[:, tc:tc + 1], ps_s, AF.Sigmoid),
                        reads=['ps_s'], writes=['sg'])
                    p.op('dve', lambda E: E.reduce_max(
                        st_sb[:, 0:1], ps_r, axis=AX.X, negate=True),
                        reads=['ps_r'], writes=['st0'])
                    p.op('act', lambda E: E.activation(
                        prb[:, :], ps_r, AF.Exp, bias=st_sb[:, 0:1]),
                        reads=['ps_r', 'st0'], writes=['f_prb'])
                    p.op('dve', lambda E: E.tensor_copy(prw[:, :], prb[:, :]),
                         reads=['f_prb'], writes=['f_prw'])
                    for it in range(4):
                        p.op('dve', lambda E: E.reduce_max(
                            st_sb[:, 1:2], prw[:, :], axis=AX.X),
                            reads=['f_prw'], writes=['st1'])
                        if it < 3:
                            p.op('dve', lambda E: E.tensor_single_scalar(
                                j16[:, :], prw[:, :], st_sb[:, 1:2],
                                ALU.is_ge), reads=['f_prw', 'st1'],
                                writes=['f_j16'])
                            p.op('dve', lambda E: E.scalar_tensor_tensor(
                                prw[:, :], j16[:, :], NEG, prw[:, :],
                                ALU.mult, ALU.add),
                                reads=['f_j16', 'f_prw'], writes=['f_prw'])
                    p.op('dve', lambda E: E.tensor_single_scalar(
                        prw[:, :], prb[:, :], st_sb[:, 1:2], ALU.is_ge),
                        reads=['f_prb', 'st1'], writes=['f_prw'])
                    p.op('dve', lambda E: E.tensor_tensor(
                        prw[:, :], prw[:, :], prb[:, :], ALU.mult),
                        reads=['f_prw', 'f_prb'], writes=['f_prw'])
                    p.op('dve', lambda E: E.reduce_sum(
                        st_sb[:, 2:3], prw[:, :], axis=AX.X),
                        reads=['f_prw'], writes=['st2'])
                    p.op('dve', lambda E: E.reciprocal(
                        st_sb[:, 3:4], st_sb[:, 2:3]),
                        reads=['st2'], writes=['st3'])
                    p.op('dve', lambda E, tc=tc: E.tensor_scalar_mul(
                        comb_sb[:, tc * 16:(tc + 1) * 16], prw[:, :],
                        st_sb[:, 3:4]),
                        reads=['f_prw', 'st3'], writes=['comb'])

                experts = [('moe', e) for e in range(E_MOE)] + \
                          [('sh', v) for v in range(4)]
                for ei, (kind, e) in enumerate(experts):
                    for ic in range(NIC):
                        wp = ic % 2
                        for gi, (wt, sh_wt) in enumerate(
                                ((w_gate, sh_gate), (w_up, sh_up))):
                            if kind == 'moe':
                                wsrc = wt.ap()[e][ic].rearrange(
                                    'p cc i -> p (cc i)')
                            else:
                                wsrc = sh_wt.ap()[e * NIC + ic].rearrange(
                                    'p cc i -> p (cc i)')
                            p.dma('sp', wgl[gi][:, :], wsrc,
                                  writes=[f'f_wgl{gi}'], stream=f'wld{gi}')
                            p.op('dve', lambda E, gi=gi, wp=wp: E.tensor_copy(
                                wgb[gi][wp][:, :], wgl[gi][:, :]),
                                reads=[f'f_wgl{gi}'], writes=[f'f_wgb{gi}_{wp}'])
                        for tt in range(NTT):
                            pb = (ic * NTT + tt) % 2
                            for cc in range(NCC):
                                p.op('pe', lambda E, cc=cc, tt=tt, pb=pb,
                                     wp=wp: E.matmul(
                                         ps_g[pb],
                                         wgb[0][wp][:, cc * 128:(cc + 1) * 128],
                                         x1Th[:, cc * TH + tt * 512:
                                              cc * TH + (tt + 1) * 512],
                                         start=(cc == 0),
                                         stop=(cc == NCC - 1)),
                                     reads=[f'f_wgb0_{wp}', f'x1Th_{cc}'],
                                     writes=[f'ps_g{pb}'])
                            for cc in range(NCC):
                                p.op('pe', lambda E, cc=cc, tt=tt, pb=pb,
                                     wp=wp: E.matmul(
                                         ps_u[pb],
                                         wgb[1][wp][:, cc * 128:(cc + 1) * 128],
                                         x1Th[:, cc * TH + tt * 512:
                                              cc * TH + (tt + 1) * 512],
                                         start=(cc == 0),
                                         stop=(cc == NCC - 1)),
                                     reads=[f'f_wgb1_{wp}', f'x1Th_{cc}'],
                                     writes=[f'ps_u{pb}'])
                            p.op('act', lambda E, pb=pb: E.activation(
                                sgst[:, :], ps_g[pb], AF.Silu),
                                reads=[f'ps_g{pb}'], writes=['f_sgst'])
                            p.op('dve', lambda E, ic=ic, tt=tt, pb=pb:
                                 E.tensor_tensor(
                                     aT[:, ic * TH + tt * 512:
                                        ic * TH + (tt + 1) * 512],
                                     sgst[:, :], ps_u[pb], ALU.mult),
                                 reads=['f_sgst', f'ps_u{pb}'],
                                 writes=[f'aT_{ic}_{tt}'])
                        if kind == 'moe':
                            wsrc = w_down.ap()[e][ic * 128:(ic + 1) * 128, :]
                        else:
                            i0 = e * I_MOE + ic * 128
                            wsrc = sh_down.ap()[i0:i0 + 128, :]
                        for hf in range(2):
                            gi = hf
                            p.dma('pool', wdl[gi][:, :],
                                  wsrc[:, hf * 1024:(hf + 1) * 1024],
                                  writes=[f'f_wdl{gi}'], stream=f'wdn{gi}')
                            p.op('dve', lambda E, gi=gi, ic=ic, hf=hf:
                                 E.tensor_copy(
                                     wdb[:, ic * C + hf * 1024:
                                         ic * C + (hf + 1) * 1024],
                                     wdl[gi][:, :]),
                                 reads=[f'f_wdl{gi}'],
                                 writes=[f'f_wdb_{ic}_{hf}'])
                    for ts in range(NTC_H):
                        tc = tc0 + ts
                        if kind == 'moe':
                            scol = comb_sb[:, tc * 16 + e: tc * 16 + e + 1]
                            skey = 'comb'
                        else:
                            scol = sg_sb[:, tc:tc + 1]
                            skey = 'sg'
                        for cog in range(4):
                            pb = (ts * 4 + cog) % 2
                            for ic in range(NIC):
                                p.op('pe', lambda E, ic=ic, ts=ts, cog=cog,
                                     pb=pb: E.matmul(
                                         ps_d[pb],
                                         aT[:, ic * TH + ts * 128:
                                            ic * TH + (ts + 1) * 128],
                                         wdb[:, ic * C + cog * 512:
                                             ic * C + (cog + 1) * 512],
                                         start=(ic == 0),
                                         stop=(ic == NIC - 1)),
                                     reads=[f'f_wdb_{ic}_{cog // 2}',
                                            f'aT_{ic}_{ts // 4}'],
                                     writes=[f'ps_d{pb}'])
                            if ei == 0:
                                p.op('dve', lambda E, ts=ts, cog=cog, pb=pb,
                                     scol=scol: E.tensor_scalar_mul(
                                         acc[:, ts * C + cog * 512:
                                             ts * C + (cog + 1) * 512],
                                         ps_d[pb], scol),
                                     reads=[f'ps_d{pb}', skey],
                                     writes=[f'acc_{ts}_{cog}'])
                            else:
                                p.op('dve', lambda E, ts=ts, cog=cog, pb=pb,
                                     scol=scol: E.scalar_tensor_tensor(
                                         acc[:, ts * C + cog * 512:
                                             ts * C + (cog + 1) * 512],
                                         ps_d[pb], scol,
                                         acc[:, ts * C + cog * 512:
                                             ts * C + (cog + 1) * 512],
                                         ALU.mult, ALU.add),
                                     reads=[f'ps_d{pb}', skey,
                                            f'acc_{ts}_{cog}'],
                                     writes=[f'acc_{ts}_{cog}'])

                for ts in range(NTC_H):
                    tc = tc0 + ts
                    b = 0
                    p.dma('pool', x1r[b][:, :],
                          x1_scr.ap()[tc * 128:(tc + 1) * 128, :],
                          reads=[f'x1_{tc}'], writes=['f_wgl0'],
                          stream='x1row')
                    acck = [f'acc_{ts}_{cog}' for cog in range(4)]
                    p.op('dve', lambda E, ts=ts, b=b: E.tensor_tensor(
                        acc[:, ts * C:(ts + 1) * C],
                        acc[:, ts * C:(ts + 1) * C],
                        x1r[b][:, :], ALU.add),
                        reads=acck + ['f_wgl0'], writes=acck)
                    ln_epilogue(acc[:, ts * C:(ts + 1) * C], acck,
                                x1r[b][:, :], 'f_wgl0',
                                x1r[b][:, :], 'f_wgl0')
                    p.dma('pool', out.ap()[tc * 128:(tc + 1) * 128, :],
                          x1r[b][:, :],
                          reads=['f_wgl0'], writes=[f'out_{tc}'],
                          stream='out_w')
                p.barrier()

        maybe_dump('comb_dbg', comb_sb[:, :], [128, NTC * E_MOE], ['comb'])
        maybe_dump('sg_dbg', sg_sb[:, :], [128, NTC], ['sg'])

    p.finalize()
    p.emit()
    return nc


def _zero_init_acc():
    pass


def _tile_moe(w):
    E, Cx, Ix = w.shape
    return np.ascontiguousarray(
        w.reshape(E, NCC, 128, Ix // 128, 128).transpose(0, 3, 2, 1, 4))


def _tile_sh(w):
    Cx, Ix = w.shape
    return np.ascontiguousarray(
        w.reshape(NCC, 128, Ix // 128, 128).transpose(2, 1, 0, 3))


def kernel(L0_attn_x, L0_attn_w_qkv, L0_attn_w_o, L0_attn_pk, L0_attn_pv,
           L0_attn_uo, L0_attn_info_w1, L0_attn_info_w2, L0_ffn_router_w,
           L0_ffn_w_gate, L0_ffn_w_up, L0_ffn_w_down, L0_ffn_sh_gate,
           L0_ffn_sh_up, L0_ffn_sh_down, L0_ffn_sh_gate_w):
    xs = np.asarray(L0_attn_x)
    T = xs.shape[1]
    nc = build(T)
    consts = make_consts(T)
    common = dict(
        w_qkv=np.asarray(L0_attn_w_qkv), w_o=np.asarray(L0_attn_w_o),
        pk=np.asarray(L0_attn_pk), pv=np.asarray(L0_attn_pv),
        uo=np.asarray(L0_attn_uo), info_w1=np.asarray(L0_attn_info_w1),
        info_w2=np.asarray(L0_attn_info_w2),
        router_w=np.asarray(L0_ffn_router_w),
        w_gate=_tile_moe(np.asarray(L0_ffn_w_gate)),
        w_up=_tile_moe(np.asarray(L0_ffn_w_up)),
        w_down=np.asarray(L0_ffn_w_down),
        sh_gate=_tile_sh(np.asarray(L0_ffn_sh_gate)),
        sh_up=_tile_sh(np.asarray(L0_ffn_sh_up)),
        sh_down=np.asarray(L0_ffn_sh_down),
        sh_gate_w=np.asarray(L0_ffn_sh_gate_w), **consts)
    B = xs.shape[0]
    in_maps = [dict(common, x=np.ascontiguousarray(xs[b])) for b in range(B)]
    res = run_bass_kernel_spmd(nc, in_maps, core_ids=list(range(B)))
    return np.stack([res.results[b]['out'] for b in range(B)], axis=0)


# version 13
# speedup vs baseline: 1.3000x; 1.0264x over previous
import math
import bisect
from contextlib import ExitStack
import numpy as np
import ml_dtypes
import concourse.bass as bass
import concourse.mybir as mybir
from concourse.bass_utils import run_bass_kernel_spmd

F32 = mybir.dt.float32
BF16 = mybir.dt.bfloat16
AF = mybir.ActivationFunctionType
ALU = mybir.AluOpType
AX = mybir.AxisListType

ENG2ATTR = {'pe': 'tensor', 'dve': 'vector', 'act': 'scalar',
            'pool': 'gpsimd', 'sp': 'sync'}

C = 2048
H = 16
D = 128
R = 64
DELTA_I = 0.75
SCALE = 1.0 / math.sqrt(D)
SCALE_G = SCALE * D / R
NEG = -1.0e30
EPS = 1e-5
E_MOE = 16
I_MOE = 1408
I_SH = 5632
NIC = I_MOE // 128
NCC = C // 128

DUMP = set()
TSLOTS = 8
A1B = True
A1B_G = True
A1B_L = True
A1B_ROWS = True
A1A = True
A1B_KR = True
A1B_VR = True
A1B_VR2 = True
PHASES = {'a0', 'a1', 'a2', 'f'}


class Prog:

    def __init__(self, nc):
        self.nc = nc
        self.engines = list(ENG2ATTR)
        self.sem = {e: nc.alloc_semaphore(f'S_{e}') for e in self.engines}
        self.ops = {e: [] for e in self.engines}
        self.inc_idx = {e: [] for e in self.engines}
        self.inc_val = {e: [] for e in self.engines}
        self.count = {e: 0 for e in self.engines}
        self.clock = {e: {} for e in self.engines}
        self.res = {}
        self.pending = {e: [] for e in self.engines}
        self.streams = {}
        self.pending_tp = set()
        self.tp_flush = None

    def _stream(self, sname):
        sname = 'q_' + sname
        if sname not in self.streams:
            self.sem[sname] = self.nc.alloc_semaphore(f'S_{sname}')
            self.streams[sname] = 0
        return sname

    def _observe(self, peng, pidx):
        o = self.ops[peng][pidx]
        if o['dma']:
            return (o['stream'], self.streams[o['stream']])
        ii = self.inc_idx[peng]
        j = bisect.bisect_left(ii, pidx)
        if j < len(ii):
            return (peng, self.inc_val[peng][j])
        ops = self.ops[peng]
        k = len(ops) - 1
        while k >= 0 and (ops[k]['fn'] is None or ops[k]['dma']):
            k -= 1
        assert k >= 0 and k >= pidx, f'no op to observe {peng}:{pidx}'
        self.count[peng] += 1
        ops[k]['inc'] = 1
        self.inc_idx[peng].append(k)
        self.inc_val[peng].append(self.count[peng])
        return (peng, self.count[peng])

    def op(self, eng, fn, reads=(), writes=(), dma=False, stream=None):
        if self.pending_tp and ((set(reads) | set(writes)) & self.pending_tp):
            self.tp_flush()
        if dma:
            stream = self._stream(stream or eng)
        idx = len(self.ops[eng])
        waits = {}

        def need(tok):
            peng, pidx = tok
            if peng == eng and peng == 'pe':
                return
            sname, val = self._observe(peng, pidx)
            if self.clock[eng].get(sname, 0) >= val:
                return
            waits[sname] = max(waits.get(sname, 0), val)

        for tok in self.pending[eng]:
            need(tok)
        self.pending[eng] = []
        for r in reads:
            st = self.res.get(r)
            if st and st['w'] is not None:
                need(st['w'])
        for w in writes:
            st = self.res.get(w)
            if st:
                if st['w'] is not None:
                    need(st['w'])
                for t in st['r']:
                    need(t)
        tok = (eng, idx)
        for r in reads:
            self.res.setdefault(r, {'w': None, 'r': []})['r'].append(tok)
        for w in writes:
            self.res[w] = {'w': tok, 'r': []}
        for sname, val in waits.items():
            self.clock[eng][sname] = val
        o = {'fn': fn, 'waits': sorted(waits.items()), 'inc': None, 'dma': dma,
             'stream': stream}
        self.ops[eng].append(o)
        if dma:
            self.streams[stream] += 16
            o['inc'] = 16
        return tok

    def dma(self, q, out_ap, in_ap, reads=(), writes=(), slow=False,
            stream=None):
        return self.op(q, lambda E: E.dma_start(
            out_ap, in_ap, allow_slow_non_contiguous=slow),
            reads=reads, writes=writes, dma=True, stream=stream)

    def barrier(self):
        if self.pending_tp:
            self.tp_flush()
        toks = []
        for e in self.engines:
            ops = self.ops[e]
            k = len(ops) - 1
            while k >= 0 and ops[k]['fn'] is None:
                k -= 1
            if k >= 0:
                toks.append((e, k))
        for e in self.engines:
            self.pending[e] = list(self.pending[e]) + \
                [t for t in toks if t[0] != e]

    def wait_for(self, eng, tokens):
        waits = {}
        for (peng, pidx) in tokens:
            sname, val = self._observe(peng, pidx)
            if self.clock[eng].get(sname, 0) >= val:
                continue
            waits[sname] = max(waits.get(sname, 0), val)
        for sname, val in waits.items():
            self.clock[eng][sname] = val
        self.ops[eng].append({'fn': None, 'waits': sorted(waits.items()),
                              'inc': None, 'dma': False})

    def finalize(self):
        if self.pending_tp:
            self.tp_flush()
        waits = {}
        for sname, total in self.streams.items():
            if self.clock['sp'].get(sname, 0) < total:
                waits[sname] = total
                self.clock['sp'][sname] = total
        self.ops['sp'].append({'fn': None, 'waits': sorted(waits.items()),
                               'inc': None, 'dma': False, 'stream': None})

    def emit(self):
        semh = self.sem
        with self.nc.Block() as block:
            for eng in self.engines:
                ops = self.ops[eng]
                if not ops:
                    continue

                def make_body(ops=ops, eng=eng):
                    def body(E):
                        for o in ops:
                            for sname, val in o['waits']:
                                E.wait_ge(semh[sname], val)
                            if o['fn'] is not None:
                                ins = o['fn'](E)
                                if o['inc'] is not None:
                                    tgt = o['stream'] if o['dma'] else eng
                                    ins.then_inc(semh[tgt], o['inc'])
                    return body
                getattr(block, ENG2ATTR[eng])(make_body())


def make_consts(T):
    inv_freq = 1.0 / (10000.0 ** (np.arange(0, D, 2, dtype=np.float64) / D))
    t = np.arange(T, dtype=np.float64)
    freqs = np.outer(t, inv_freq)
    emb = np.concatenate([freqs, freqs], axis=-1)
    cos = np.cos(emb).astype(np.float32)
    sin = np.sin(emb).astype(np.float32)
    sin_s = sin.copy()
    sin_s[:, 0::2] = -sin_s[:, 0::2]
    j = np.arange(128)[:, None]
    i = np.arange(128)[None, :]
    mask_at = np.where(i >= j, NEG, 0.0).astype(np.float32)
    mask_bt = np.where(j > i, NEG, 0.0).astype(np.float32)
    ident = np.eye(128, dtype=ml_dtypes.bfloat16)
    return dict(cos_tab=cos, sin_tab=sin_s, mask_at=mask_at, mask_bt=mask_bt,
                ident=ident)


def build(T=2048):
    nc = bass.Bass('TRN2', target_bir_lowering=False)
    NTC = T // 128
    NG = T // 512
    HH = H // 2

    def dram(name, shape, dtype):
        kind = 'ExternalOutput' if name in DUMP else 'Internal'
        return nc.dram_tensor(name, shape, dtype, kind=kind)

    x_in = nc.dram_tensor('x', [T, C], F32, kind='ExternalInput')
    w_qkv = nc.dram_tensor('w_qkv', [3 * C, C], F32, kind='ExternalInput')
    w_o = nc.dram_tensor('w_o', [C, C], F32, kind='ExternalInput')
    pk = nc.dram_tensor('pk', [R, D], F32, kind='ExternalInput')
    pv = nc.dram_tensor('pv', [R, D], F32, kind='ExternalInput')
    uo = nc.dram_tensor('uo', [D, R], F32, kind='ExternalInput')
    info_w1 = nc.dram_tensor('info_w1', [512, C], F32, kind='ExternalInput')
    info_w2 = nc.dram_tensor('info_w2', [1, 512], F32, kind='ExternalInput')
    router_w = nc.dram_tensor('router_w', [E_MOE, C], F32, kind='ExternalInput')
    w_gate = nc.dram_tensor('w_gate', [E_MOE, NIC, 128, NCC, 128], F32, kind='ExternalInput')
    w_up = nc.dram_tensor('w_up', [E_MOE, NIC, 128, NCC, 128], F32, kind='ExternalInput')
    w_down = nc.dram_tensor('w_down', [E_MOE, I_MOE, C], F32, kind='ExternalInput')
    sh_gate = nc.dram_tensor('sh_gate', [4 * NIC, 128, NCC, 128], F32, kind='ExternalInput')
    sh_up = nc.dram_tensor('sh_up', [4 * NIC, 128, NCC, 128], F32, kind='ExternalInput')
    sh_down = nc.dram_tensor('sh_down', [I_SH, C], F32, kind='ExternalInput')
    sh_gate_w = nc.dram_tensor('sh_gate_w', [C, 1], F32, kind='ExternalInput')
    cos_in = nc.dram_tensor('cos_tab', [T, D], F32, kind='ExternalInput')
    sin_in = nc.dram_tensor('sin_tab', [T, D], F32, kind='ExternalInput')
    ma_in = nc.dram_tensor('mask_at', [128, 128], F32, kind='ExternalInput')
    mb_in = nc.dram_tensor('mask_bt', [128, 128], F32, kind='ExternalInput')
    id_in = nc.dram_tensor('ident', [128, 128], BF16, kind='ExternalInput')

    out = nc.dram_tensor('out', [T, C], F32, kind='ExternalOutput')

    xT_scr = dram('xT_scr', [NTC, 128, C], BF16)
    q_scr = dram('q_scr', [H, 128, T], BF16)
    ctx_scr = dram('ctx_scr', [H, 128, T], BF16)
    x1_scr = dram('x1_scr', [T, C], F32)
    x1T_scr = dram('x1T_scr', [NCC, 128, T], BF16)

    p = Prog(nc)

    PS0 = nc.alloc_psum_tensor('PS0', [128, 2048], F32)
    PS1 = nc.alloc_psum_tensor('PS1', [128, 1024], F32)
    PS2 = nc.alloc_psum_tensor('PS2', [128, 512], F32)
    PST = nc.alloc_psum_tensor('PST', [128, 1024], BF16)

    sb = nc.alloc_sbuf_tensor
    id_sb = sb('id_sb', [128, 128], BF16)
    pkT_sb = sb('pkT_sb', [128, R], BF16)
    pvT_sb = sb('pvT_sb', [128, R], BF16)
    uoT_sb = sb('uoT_sb', [64, 128], BF16)
    gate_sb = sb('gate_sb', [128, NTC], F32)
    ones128 = sb('ones128', [128, 1], BF16)
    ones1f = sb('ones1f', [1, 128], F32)
    sg_sb = sb('sg_sb', [128, NTC], F32)
    comb_sb = sb('comb_sb', [128, NTC * E_MOE], F32)
    st_sb = sb('st_sb', [128, 16], F32)

    p.dma('sp', id_sb[:, :], id_in.ap(), writes=['ident'], stream='ident')
    p.op('dve', lambda E: E.memset(ones128[:, :], 1.0), writes=['ones128'])
    p.op('dve', lambda E: E.memset(ones1f[:, :], 1.0), writes=['ones1f'])

    _tq = []
    _ectr = [0]
    _tflushing = [False]

    def tp_flush():
        if _tflushing[0] or not _tq:
            p.pending_tp.clear()
            return
        _tflushing[0] = True
        batch = list(_tq)
        del _tq[:]
        toks = []
        for (in_ap, in_keys, dst_ap, dst_keys, kpart, mfree, i) in batch:
            out_ap = PST[0:mfree, i * 128:i * 128 + kpart]
            tok = p.op('pe',
                       lambda E, out_ap=out_ap, in_ap=in_ap, kpart=kpart:
                       E.transpose(out_ap, in_ap, id_sb[0:kpart, 0:kpart]),
                       reads=list(in_keys) + ['ident'], writes=[f'PSTq{i}'])
            toks.append(tok)
        for (_, _, _, _, _, _, i) in batch:
            p.res[f'PSTq{i}'] = {'w': toks[-1], 'r': []}
        for (in_ap, in_keys, dst_ap, dst_keys, kpart, mfree, i) in batch:
            out_ap = PST[0:mfree, i * 128:i * 128 + kpart]
            _ectr[0] += 1
            if True:
                p.op('dve', lambda E, dst_ap=dst_ap, out_ap=out_ap:
                     E.tensor_copy(dst_ap, out_ap),
                     reads=[f'PSTq{i}'], writes=list(dst_keys))
            else:
                p.op('act', lambda E, dst_ap=dst_ap, out_ap=out_ap:
                     E.copy(dst_ap, out_ap),
                     reads=[f'PSTq{i}'], writes=list(dst_keys))
        p.pending_tp.clear()
        _tflushing[0] = False

    p.tp_flush = tp_flush

    def transpose_tile(in_ap, in_keys, dst_ap, dst_keys, kpart=128, mfree=128):
        if len(_tq) == 8:
            tp_flush()
        i = len(_tq)
        _tq.append((in_ap, list(in_keys), dst_ap, list(dst_keys), kpart,
                    mfree, i))
        p.pending_tp.update(in_keys)
        p.pending_tp.update(dst_keys)
        p.pending_tp.add(f'PSTq{i}')

    def maybe_dump(name, ap, shape, reads, dtype=F32):
        if name in DUMP:
            t = nc.dram_tensor(name, shape, dtype, kind='ExternalOutput')
            p.dma('sp', t.ap(), ap, reads=reads, stream='dump')

    def ln_epilogue(r_ap, r_keys, dst_ap, dst_key, junk_ap, junk_key):
        p.op('dve', lambda E: E.reduce_sum(st_sb[:, 0:1], r_ap, axis=AX.X),
             reads=r_keys, writes=['st0'])
        p.op('dve', lambda E: E.tensor_single_scalar(
            st_sb[:, 1:2], st_sb[:, 0:1], -1.0 / C, ALU.mult),
            reads=['st0'], writes=['st1'])
        p.op('dve', lambda E: E.scalar_tensor_tensor(
            junk_ap, r_ap, st_sb[:, 1:2], r_ap, ALU.add, ALU.mult,
            accum_out=st_sb[:, 2:3]),
            reads=r_keys + ['st1'], writes=[junk_key, 'st2'])
        p.op('dve', lambda E: E.tensor_scalar(
            st_sb[:, 3:4], st_sb[:, 2:3], 1.0 / C, EPS, ALU.mult, ALU.add),
            reads=['st2'], writes=['st3'])
        p.op('act', lambda E: E.activation(st_sb[:, 4:5], st_sb[:, 3:4],
                                           AF.Sqrt),
             reads=['st3'], writes=['st4'])
        p.op('dve', lambda E: E.reciprocal(st_sb[:, 5:6], st_sb[:, 4:5]),
             reads=['st4'], writes=['st5'])
        p.op('dve', lambda E: E.tensor_scalar(
            dst_ap, r_ap, st_sb[:, 1:2], st_sb[:, 5:6], ALU.add, ALU.mult),
            reads=r_keys + ['st1', 'st5'], writes=[dst_key])

    astack = ExitStack()

    def asb(name, shape, dtype):
        return astack.enter_context(nc.sbuf_tensor(name, shape, dtype))

    cos_sb = asb('cos_sb', [128, NTC * D], F32)
    sin_sb = asb('sin_sb', [128, NTC * D], F32)
    ma_sb = asb('ma_sb', [128, 128], F32)
    mb_sb = asb('mb_sb', [128, 128], F32)
    gate_row = asb('gate_row', [1, T], F32)

    p.dma('sp', cos_sb[:, :].rearrange('p (tc d) -> p tc d', d=D),
          cos_in.ap().rearrange('(tc p) d -> p tc d', p=128), writes=['cos'],
          stream='cos')
    p.dma('sp', sin_sb[:, :].rearrange('p (tc d) -> p tc d', d=D),
          sin_in.ap().rearrange('(tc p) d -> p tc d', p=128), writes=['sin'],
          stream='sin')
    p.dma('sp', ma_sb[:, :], ma_in.ap(), writes=['ma'], stream='ma')
    p.dma('sp', mb_sb[:, :], mb_in.ap(), writes=['mb'], stream='mb')

    with nc.sbuf_tensor('tmp_f', [128, 128], F32) as tmp_f, \
            nc.sbuf_tensor('tmp_b', [128, 128], BF16) as tmp_b:
        for src, dst, key in ((pk, pkT_sb, 'pkT'), (pv, pvT_sb, 'pvT')):
            p.dma('sp', tmp_f[0:R, :], src.ap(), writes=['tmpf'], stream='tmpf')
            p.op('dve', lambda E: E.tensor_copy(tmp_b[0:R, :], tmp_f[0:R, :]),
                 reads=['tmpf'], writes=['tmpb'])
            transpose_tile(tmp_b[0:R, :], ['tmpb'], dst[:, :], [key],
                           kpart=R, mfree=128)
        p.dma('sp', tmp_f[:, 0:R], uo.ap(), writes=['tmpf'], stream='tmpf')
        p.op('dve', lambda E: E.tensor_copy(tmp_b[:, 0:R], tmp_f[:, 0:R]),
             reads=['tmpf'], writes=['tmpb'])
        transpose_tile(tmp_b[:, 0:R], ['tmpb'], uoT_sb[:, :], ['uoT'],
                       kpart=128, mfree=R)
    p.barrier()

    if 'a0' in PHASES:
        with ExitStack() as stack:
            zb = lambda n, s, d: stack.enter_context(nc.sbuf_tensor(n, s, d))
            xld = [zb(f'a0_xld{i}', [128, C], F32) for i in range(2)]
            xbf = [zb(f'a0_xbf{i}', [128, C], BF16) for i in range(2)]
            xtc = [zb(f'a0_xtc{i}', [128, C], BF16) for i in range(2)]
            iw1f = zb('a0_iw1f', [128, C], F32)
            iw1b = zb('a0_iw1b', [128, C], BF16)
            iw1T = zb('a0_iw1T', [128, NCC * 512], BF16)
            w2c = zb('a0_w2c', [128, 4], F32)
            w2b = zb('a0_w2b', [128, 4], BF16)
            hinfT = zb('a0_hinfT', [128, 4 * 128], BF16)
            sig = zb('a0_sig', [128, 1], F32)
            gbft = zb('a0_gbft', [128, 1], BF16)
            ps_in = PS1[:, 0:128]
            ps_sc = PS2[:, 0:1]

            for fc in range(4):
                p.dma('sp', iw1f[:, :],
                      info_w1.ap()[fc * 128:(fc + 1) * 128, :],
                      writes=['iw1f'], stream='iw1f')
                p.op('dve', lambda E: E.tensor_copy(iw1b[:, :], iw1f[:, :]),
                     reads=['iw1f'], writes=['iw1b'])
                for cc in range(NCC):
                    transpose_tile(iw1b[:, cc * 128:(cc + 1) * 128], ['iw1b'],
                                   iw1T[:, cc * 512 + fc * 128:
                                        cc * 512 + (fc + 1) * 128],
                                   [f'iw1T_{cc}_{fc}'])
            for fc in range(4):
                p.dma('sp', w2c[:, fc:fc + 1],
                      info_w2.ap()[:, fc * 128:(fc + 1) * 128]
                      .rearrange('one p -> p one'),
                      writes=['w2c'], stream='w2c')
            p.op('dve', lambda E: E.tensor_copy(w2b[:, :], w2c[:, :]),
                 reads=['w2c'], writes=['w2b'])

            for tc in range(NTC):
                b = tc % 2
                p.dma('pool', xld[b][:, :],
                      x_in.ap()[tc * 128:(tc + 1) * 128, :],
                      writes=[f'a0_xld{b}'], stream=f'xld{b}')
                p.op('dve', lambda E, b=b: E.tensor_copy(xbf[b][:, :],
                                                         xld[b][:, :]),
                     reads=[f'a0_xld{b}'], writes=[f'a0_xbf{b}'])
                for cc in range(NCC):
                    transpose_tile(xbf[b][:, cc * 128:(cc + 1) * 128],
                                   [f'a0_xbf{b}'],
                                   xtc[b][:, cc * 128:(cc + 1) * 128],
                                   [f'a0_xtc{b}_{cc}'])
                p.dma('pool', xT_scr.ap()[tc], xtc[b][:, :],
                      reads=[f'a0_xtc{b}_{cc}' for cc in range(NCC)],
                      writes=[f'xT_{tc}'], stream='xT_w')
                for fc in range(4):
                    for cc in range(NCC):
                        p.op('pe', lambda E, b=b, cc=cc, fc=fc: E.matmul(
                            ps_in,
                            iw1T[:, cc * 512 + fc * 128: cc * 512 + (fc + 1) * 128],
                            xtc[b][:, cc * 128:(cc + 1) * 128],
                            start=(cc == 0), stop=(cc == NCC - 1)),
                            reads=[f'iw1T_{cc}_{fc}', f'a0_xtc{b}_{cc}'],
                            writes=['ps_in'])
                    p.op('act', lambda E, fc=fc: E.activation(
                        hinfT[:, fc * 128:(fc + 1) * 128], ps_in, AF.Gelu),
                        reads=['ps_in'], writes=[f'hinfT{fc}'])
                for fc in range(4):
                    p.op('pe', lambda E, fc=fc: E.matmul(
                        ps_sc, hinfT[:, fc * 128:(fc + 1) * 128],
                        w2b[:, fc:fc + 1], start=(fc == 0), stop=(fc == 3)),
                        reads=[f'hinfT{fc}', 'w2b'], writes=['ps_sc'])
                p.op('act', lambda E: E.activation(sig[:, :], ps_sc,
                                                   AF.Sigmoid),
                     reads=['ps_sc'], writes=['a0_sig'])
                p.op('dve', lambda E, tc=tc: E.tensor_single_scalar(
                    gate_sb[:, tc:tc + 1], sig[:, :], DELTA_I, ALU.is_gt),
                    reads=['a0_sig'], writes=['gate'])
                p.op('dve', lambda E, tc=tc: E.tensor_copy(
                    gbft[:, :], gate_sb[:, tc:tc + 1]),
                    reads=['gate'], writes=['a0_gbft'])
                transpose_tile(gbft[:, :], ['a0_gbft'],
                               gate_row[:, tc * 128:(tc + 1) * 128],
                               [f'gate_row{tc}'], kpart=128, mfree=1)
            maybe_dump('gate_dbg', gate_sb[:, :], [128, NTC], ['gate'])
            p.barrier()

    if 'a1' in PHASES:
        with ExitStack() as stack:
            zb = lambda n, s, d: stack.enter_context(nc.sbuf_tensor(n, s, d))
            kT_sb = zb('kT_sb', [128, HH * T], BF16)
            v_sb = zb('v_sb', [128, NTC * HH * 128], BF16)
            wblk = zb('wblk', [128, NCC * 512], BF16)
            wld = [zb(f'a1_wld{i}', [128, C], F32) for i in range(2)]
            wbf = [zb('a1_wbf0', [128, C], BF16)] * 2
            xtc1 = [zb(f'a1_xtc{i}', [128, C], BF16) for i in range(2)]
            rst = [zb(f'a1_rst_{i}', [128, 512], F32) for i in range(2)]
            rst2 = [zb(f'a1_rst2_{i}', [128, 512], F32) for i in range(2)]
            ropeo = [zb(f'a1_ropeo{i}', [128, 512], BF16) for i in range(2)]
            rpar = [0]
            qTst = zb('a1_qTst', [128, 512], BF16)
            qT_h = zb('qT_h', [128, T], BF16)
            krT = zb('krT', [64, T], BF16)
            qrT = zb('qrT', [64, T], BF16)
            vT_h = zb('vT_h', [128, T], BF16)
            vr = zb('vr', [128, NTC * 64], BF16)
            vrst = zb('vrst', [64, 512], BF16)
            ptg = zb('ptg', [128, NTC * 512], BF16)
            ptl = zb('ptl', [128, 8 * 512], BF16)
            cgr_sb = zb('cgr_sb', [64, 512], BF16)
            ctxst = zb('ctxst', [128, 512], BF16)
            lg_row = zb('lg_row', [1, 512], F32)
            ll_row = zb('ll_row', [1, 512], F32)
            rlg_row = zb('rlg_row', [1, 512], F32)
            rll_row = zb('rll_row', [1, 512], F32)
            f2_row = zb('f2_row', [1, 512], F32)
            bf2_sb = zb('bf2_sb', [128, 512], F32)
            brl_sb = zb('brl_sb', [128, 512], F32)
            ps_qkv = [PS0[:, i * 512:(i + 1) * 512] for i in range(4)]
            ps_a = PS1[:, 0:512]
            psrow = PS2[0:1, 0:512]
            psbc = PS1[:, 512:1024]
            ps_ctx = PS2

            def rope_evac(psum, pskey, tc):
                rp = rpar[0] = 1 - rpar[0]
                cs = cos_sb[:, tc * D:(tc + 1) * D]
                sn = sin_sb[:, tc * D:(tc + 1) * D]
                for j in range(4):
                    o = j * 128
                    p.op('dve', lambda E, o=o, rp=rp: E.tensor_tensor(
                        rst[rp][:, o:o + 128], psum[:, o:o + 128], cs,
                        ALU.mult),
                        reads=[pskey, 'cos'], writes=[f'a1_rst{rp}_{j}'])
                    p.op('dve', lambda E, o=o, rp=rp: E.tensor_tensor(
                        rst2[rp][:, o:o + 128:2], psum[:, o + 1:o + 128:2],
                        sn[:, 0:128:2], ALU.mult),
                        reads=[pskey, 'sin'], writes=[f'a1_rst2e{rp}_{j}'])
                    p.op('dve', lambda E, o=o, rp=rp: E.tensor_tensor(
                        rst2[rp][:, o + 1:o + 128:2], psum[:, o:o + 128:2],
                        sn[:, 1:128:2], ALU.mult),
                        reads=[pskey, 'sin'], writes=[f'a1_rst2o{rp}_{j}'])
                p.op('pool', lambda E, rp=rp: E.tensor_tensor(
                    ropeo[rp][:, :], rst[rp][:, :], rst2[rp][:, :], ALU.add),
                    reads=[k for j in range(4) for k in
                           (f'a1_rst{rp}_{j}', f'a1_rst2e{rp}_{j}',
                            f'a1_rst2o{rp}_{j}')],
                    writes=[f'a1_ropeo{rp}'])
                return rp

            for X in range(2):
                for proj, off in ((('v', 0), ('v', 512), ('k', 0), ('k', 512),
                                   ('q', 0), ('q', 512)) if A1A else ()):
                    base = {'q': 0, 'k': C, 'v': 2 * C}[proj] + X * 1024 + off
                    for s4 in range(4):
                        b = s4 % 2
                        p.dma('sp', wld[b][:, :],
                              w_qkv.ap()[base + s4 * 128: base + (s4 + 1) * 128, :],
                              writes=[f'a1_wld{b}'], stream=f'wld{b}')
                        p.op('dve', lambda E, b=b: E.tensor_copy(
                            wbf[b][:, :], wld[b][:, :]),
                            reads=[f'a1_wld{b}'], writes=['a1_wbf0'])
                        for cc in range(NCC):
                            transpose_tile(wbf[b][:, cc * 128:(cc + 1) * 128],
                                           ['a1_wbf0'],
                                           wblk[:, cc * 512 + s4 * 128:
                                                cc * 512 + (s4 + 1) * 128],
                                           [f'wblk_{cc}_{s4}'])
                    for tc in range(NTC):
                        b = tc % 2
                        qb = tc % 4
                        p.dma('pool', xtc1[b][:, :], xT_scr.ap()[tc],
                              reads=[f'xT_{tc}'], writes=[f'a1_xtc{b}'],
                              stream=f'xtc{b}')
                        for cc in range(NCC):
                            p.op('pe', lambda E, b=b, cc=cc, qb=qb: E.matmul(
                                ps_qkv[qb],
                                xtc1[b][:, cc * 128:(cc + 1) * 128],
                                wblk[:, cc * 512:(cc + 1) * 512],
                                start=(cc == 0), stop=(cc == NCC - 1)),
                                reads=[f'a1_xtc{b}'] +
                                [f'wblk_{cc}_{s4}' for s4 in range(4)],
                                writes=[f'ps_qkv{qb}'])
                        if proj == 'v':
                            p.op('dve', lambda E, qb=qb, tc=tc, off=off:
                                 E.tensor_copy(
                                     v_sb[:, tc * HH * 128 + off:
                                          tc * HH * 128 + off + 512],
                                     ps_qkv[qb]),
                                 reads=[f'ps_qkv{qb}'], writes=[f'v_{tc}_{off}'])
                        else:
                            rp = rope_evac(ps_qkv[qb], f'ps_qkv{qb}', tc)
                            for j in range(4):
                                hh = off // 128 + j
                                if proj == 'k':
                                    transpose_tile(
                                        ropeo[rp][:, j * 128:(j + 1) * 128],
                                        [f'a1_ropeo{rp}'],
                                        kT_sb[:, hh * T + tc * 128:
                                              hh * T + (tc + 1) * 128],
                                        [f'kT_{hh}_{tc}'])
                                else:
                                    transpose_tile(
                                        ropeo[rp][:, j * 128:(j + 1) * 128],
                                        [f'a1_ropeo{rp}'],
                                        qTst[:, j * 128:(j + 1) * 128],
                                        [f'a1_qTst{j}'])
                            if proj == 'q':
                                h0 = X * HH + off // 128
                                p.dma('pool',
                                      q_scr.ap()[h0:h0 + 4, :,
                                                 tc * 128:(tc + 1) * 128]
                                      .rearrange('h p t -> p h t'),
                                      qTst[:, :].rearrange(
                                          'p (h t) -> p h t', t=128),
                                      reads=[f'a1_qTst{j}'
                                             for j in range(4)],
                                      writes=[f'q_{h0}_{tc}'],
                                      stream='q_w')
                if X == 0:
                    maybe_dump('v_dbg', v_sb[:, :], [128, NTC * HH * 128],
                               [f'v_{tc}_{o}' for tc in range(NTC)
                                for o in (0, 512)], dtype=BF16)
                    maybe_dump('kT_dbg', kT_sb[:, :], [128, HH * T],
                               [f'kT_{hh}_{tc}' for hh in range(HH)
                                for tc in range(NTC)], dtype=BF16)
                p.barrier()

                for hh in (range(HH) if A1B else []):
                    h_abs = X * HH + hh
                    h0 = X * HH + (hh // 4) * 4
                    p.dma('pool', qT_h[:, :], q_scr.ap()[h_abs],
                          reads=[f'q_{h0}_{tc}' for tc in range(NTC)],
                          writes=['qT_h'], stream='qTh')
                    for sgp in (range(NG) if A1B_KR else []):
                        p.op('pe', lambda E, sgp=sgp, hh=hh: E.matmul(
                            ps_a[0:64, :], pkT_sb[:, :],
                            kT_sb[:, hh * T + sgp * 512:
                                  hh * T + (sgp + 1) * 512],
                            start=True, stop=True),
                            reads=['pkT'], writes=['ps_a'])
                        p.op('dve', lambda E, sgp=sgp: E.tensor_copy(
                            krT[:, sgp * 512:(sgp + 1) * 512], ps_a[0:64, :]),
                            reads=['ps_a'], writes=['krT'])
                        p.op('pe', lambda E, sgp=sgp: E.matmul(
                            ps_a[0:64, :], pkT_sb[:, :],
                            qT_h[:, sgp * 512:(sgp + 1) * 512],
                            start=True, stop=True),
                            reads=['pkT', 'qT_h'], writes=['ps_a'])
                        p.op('dve', lambda E, sgp=sgp: E.tensor_copy(
                            qrT[:, sgp * 512:(sgp + 1) * 512], ps_a[0:64, :]),
                            reads=['ps_a'], writes=['qrT'])
                    voff = (hh // 4) * 512
                    for tc in (range(NTC) if A1B_VR else []):
                        transpose_tile(
                            v_sb[:, tc * HH * 128 + hh * 128:
                                 tc * HH * 128 + (hh + 1) * 128],
                            [f'v_{tc}_{voff}'],
                            vT_h[:, tc * 128:(tc + 1) * 128], [f'vT_{tc}'])
                    for sgp in (range(NG) if (A1B_VR and A1B_VR2) else []):
                        p.op('pe', lambda E, sgp=sgp: E.matmul(
                            ps_a[0:64, :], pvT_sb[:, :],
                            vT_h[:, sgp * 512:(sgp + 1) * 512],
                            start=True, stop=True),
                            reads=['pvT'] + [f'vT_{sgp * 4 + j}'
                                             for j in range(4)],
                            writes=['ps_a'])
                        p.op('dve', lambda E: E.tensor_copy(
                            vrst[:, :], ps_a[0:64, :]),
                            reads=['ps_a'], writes=['vrst'])
                        for j in range(4):
                            transpose_tile(
                                vrst[:, j * 128:(j + 1) * 128], ['vrst'],
                                vr[:, (sgp * 4 + j) * 64:
                                   (sgp * 4 + j + 1) * 64],
                                [f'vr_{sgp * 4 + j}'], kpart=64, mfree=128)

                    for g in (range(NG) if A1B_ROWS else []):
                        G0 = g * 512
                        g4 = g * 4
                        nch = min(4, NTC - g4)
                        assert nch == 4
                        for k2 in (range(NTC // 2) if A1B_G else []):
                            for j in range(2):
                                sc = 2 * k2 + j
                                q = sc % 4
                                p.op('pe', lambda E, sc=sc, q=q, G0=G0:
                                     E.matmul(
                                         PS0[:, q * 512:(q + 1) * 512],
                                         krT[:, sc * 128:(sc + 1) * 128],
                                         qrT[:, G0:G0 + 512],
                                         start=True, stop=True),
                                     reads=['krT', 'qrT'],
                                     writes=[f'PS0q{q}'])
                            q0 = (2 * k2) % 4
                            p.op('act', lambda E, k2=k2, q0=q0: E.activation(
                                ptg[:, 2 * k2 * 512: 2 * (k2 + 1) * 512],
                                PS0[:, q0 * 512:(q0 + 2) * 512],
                                AF.Exp, scale=SCALE_G),
                                reads=[f'PS0q{q0}', f'PS0q{q0 + 1}'],
                                writes=[f'ptg_{2 * k2}', f'ptg_{2 * k2 + 1}'])
                        for sc in (range(NTC) if A1B_G else []):
                            p.op('pe', lambda E, sc=sc: E.matmul(
                                psrow, ones128[:, :],
                                ptg[:, sc * 512:(sc + 1) * 512],
                                start=(sc == 0), stop=(sc == NTC - 1)),
                                reads=[f'ptg_{sc}', 'ones128'],
                                writes=['ps_ctx'])
                        p.op('dve', lambda E: E.tensor_copy(lg_row[:, :],
                                                            psrow),
                             reads=['ps_ctx'], writes=['lg_row'])
                        lvalid = []
                        for sc_abs in range(max(0, g4 - 4), g4 + nch):
                            slot = sc_abs - (g4 - 4)
                            lo = max(0, sc_abs - g4)
                            hi = min(nch - 1, sc_abs - g4 + 4)
                            lvalid.append((sc_abs, slot, lo, hi))
                        lvalid.sort(key=lambda v: v[0] != g4)
                        for n_i, (sc_abs, slot, lo, hi) in (
                                enumerate(lvalid) if A1B_L else []):
                            w = (hi - lo + 1) * 128
                            q = slot % 4
                            p.op('pe', lambda E, sc_abs=sc_abs, lo=lo, w=w,
                                 q=q, hh=hh, g4=g4: E.matmul(
                                     PS0[:, q * 512: q * 512 + w],
                                     kT_sb[:, hh * T + sc_abs * 128:
                                           hh * T + (sc_abs + 1) * 128],
                                     qT_h[:, (g4 + lo) * 128:
                                          (g4 + lo) * 128 + w],
                                     start=True, stop=True),
                                 reads=['qT_h'], writes=[f'PS0q{q}'])
                            if g4 <= sc_abs:
                                col = (sc_abs - g4 - lo) * 128
                                p.op('dve', lambda E, q=q, col=col:
                                     E.tensor_tensor(
                                         PS0[:, q * 512 + col:
                                             q * 512 + col + 128],
                                         PS0[:, q * 512 + col:
                                             q * 512 + col + 128],
                                         mb_sb[:, :], ALU.add),
                                     reads=['mb', f'PS0q{q}'],
                                     writes=[f'PS0q{q}'])
                            else:
                                col = (sc_abs + 4 - g4 - lo) * 128
                                p.op('dve', lambda E, q=q, col=col:
                                     E.tensor_tensor(
                                         PS0[:, q * 512 + col:
                                             q * 512 + col + 128],
                                         PS0[:, q * 512 + col:
                                             q * 512 + col + 128],
                                         ma_sb[:, :], ALU.add),
                                     reads=['ma', f'PS0q{q}'],
                                     writes=[f'PS0q{q}'])
                            p.op('act', lambda E, slot=slot, lo=lo, w=w, q=q:
                                 E.activation(
                                     ptl[:, slot * 512 + lo * 128:
                                         slot * 512 + lo * 128 + w],
                                     PS0[:, q * 512: q * 512 + w],
                                     AF.Exp, scale=SCALE),
                                 reads=[f'PS0q{q}'], writes=[f'ptl_{slot}'])
                        for n_i, (sc_abs, slot, lo, hi) in (
                                enumerate(lvalid) if A1B_L else []):
                            w = (hi - lo + 1) * 128
                            p.op('pe', lambda E, slot=slot, lo=lo, w=w,
                                 st=(n_i == 0),
                                 sp=(n_i == len(lvalid) - 1): E.matmul(
                                     psrow[:, lo * 128: lo * 128 + w],
                                     ones128[:, :],
                                     ptl[:, slot * 512 + lo * 128:
                                         slot * 512 + lo * 128 + w],
                                     start=st, stop=sp),
                                 reads=[f'ptl_{slot}', 'ones128'],
                                 writes=['ps_ctx'])
                        p.op('dve', lambda E: E.tensor_copy(ll_row[:, :],
                                                            psrow),
                             reads=['ps_ctx'], writes=['ll_row'])
                        p.op('dve', lambda E: E.reciprocal(rll_row[:, :],
                                                           ll_row[:, :]),
                             reads=['ll_row'], writes=['rll_row'])
                        p.op('dve', lambda E: E.reciprocal(rlg_row[:, :],
                                                           lg_row[:, :]),
                             reads=['lg_row'], writes=['rlg_row'])
                        p.op('dve', lambda E, G0=G0: E.tensor_tensor(
                            f2_row[:, :], ll_row[:, :],
                            gate_row[:, G0:G0 + 512], ALU.mult),
                            reads=['ll_row'] + [f'gate_row{g4 + c}'
                                                for c in range(4)],
                            writes=['f2_row'])
                        p.op('dve', lambda E: E.tensor_tensor(
                            f2_row[:, :], f2_row[:, :], rlg_row[:, :],
                            ALU.mult),
                            reads=['f2_row', 'rlg_row'], writes=['f2_row'])
                        p.op('pe', lambda E: E.matmul(
                            psbc, ones1f[:, :], f2_row[:, :],
                            start=True, stop=True),
                            reads=['f2_row', 'ones1f'], writes=['ps1b'])
                        p.op('dve', lambda E: E.tensor_copy(bf2_sb[:, :],
                                                            psbc),
                             reads=['ps1b'], writes=['bf2'])
                        p.op('pe', lambda E: E.matmul(
                            psbc, ones1f[:, :], rll_row[:, :],
                            start=True, stop=True),
                            reads=['rll_row', 'ones1f'], writes=['ps1b'])
                        p.op('act', lambda E: E.copy(brl_sb[:, :], psbc),
                             reads=['ps1b'], writes=['brl'])
                        for sc in (range(NTC) if A1B_G else []):
                            p.op('pe', lambda E, sc=sc: E.matmul(
                                ps_a[0:64, :],
                                vr[:, sc * 64:(sc + 1) * 64],
                                ptg[:, sc * 512:(sc + 1) * 512],
                                start=(sc == 0), stop=(sc == NTC - 1)),
                                reads=[f'vr_{sc}', f'ptg_{sc}'],
                                writes=['ps_a'])
                        p.op('dve', lambda E: E.tensor_tensor(
                            cgr_sb[:, :], ps_a[0:64, :], bf2_sb[0:64, :],
                            ALU.mult),
                            reads=['ps_a', 'bf2'], writes=['cgr'])
                        p.op('pe', lambda E: E.matmul(
                            ps_ctx[:, :], uoT_sb[:, :], cgr_sb[:, :],
                            start=True, stop=not A1B_L),
                            reads=['uoT', 'cgr'], writes=['ps_ctx'])
                        for n_i, (sc_abs, slot, lo, hi) in (
                                enumerate(lvalid) if A1B_L else []):
                            p.op('pe', lambda E, sc_abs=sc_abs, slot=slot,
                                 lo=lo, hi=hi, hh=hh,
                                 last=(n_i == len(lvalid) - 1): E.matmul(
                                ps_ctx[:, lo * 128:(hi + 1) * 128],
                                v_sb[:, sc_abs * HH * 128 + hh * 128:
                                     sc_abs * HH * 128 + (hh + 1) * 128],
                                ptl[:, slot * 512 + lo * 128:
                                    slot * 512 + (hi + 1) * 128],
                                start=False, stop=last),
                                reads=[f'ptl_{slot}'], writes=['ps_ctx'])
                        p.op('dve', lambda E: E.tensor_tensor(
                            ctxst[:, :], ps_ctx[:, :], brl_sb[:, :], ALU.mult),
                            reads=['ps_ctx', 'brl'], writes=['ctxst'])
                        p.dma('pool', ctx_scr.ap()[h_abs][:, G0:G0 + 512],
                              ctxst[:, :], reads=['ctxst'],
                              writes=[f'ctx_{h_abs}_{g}'], stream='ctx_w')
                p.barrier()
    astack.close()

    if 'a2' in PHASES:
        with ExitStack() as stack:
            zb = lambda n, s, d: stack.enter_context(nc.sbuf_tensor(n, s, d))
            woT = zb('woT', [128, NCC * C], BF16)
            wld2 = [zb(f'a2_wld{i}', [128, C], F32) for i in range(2)]
            wbf2 = [zb(f'a2_wbf{i}', [128, C], BF16) for i in range(2)]
            ctxc = [zb(f'a2_ctxc{i}', [128, C], BF16) for i in range(2)]
            xrow = [zb(f'a2_xrow{i}', [128, C], F32) for i in range(2)]
            r_sb = zb('a2_r', [128, C], F32)
            x1f = zb('a2_x1f', [128, C], F32)
            x1b = zb('a2_x1b', [128, C], BF16)
            x1Tst = zb('a2_x1Tst', [128, C], BF16)
            junk = zb('a2_junk', [128, C], F32)

            for s in range(NCC):
                b = s % 2
                p.dma('sp', wld2[b][:, :], w_o.ap()[s * 128:(s + 1) * 128, :],
                      writes=[f'a2_wld{b}'], stream=f'wld{b}')
                p.op('dve', lambda E, b=b: E.tensor_copy(wbf2[b][:, :],
                                                         wld2[b][:, :]),
                     reads=[f'a2_wld{b}'], writes=[f'a2_wbf{b}'])
                for cc in range(NCC):
                    transpose_tile(wbf2[b][:, cc * 128:(cc + 1) * 128],
                                   [f'a2_wbf{b}'],
                                   woT[:, cc * C + s * 128:
                                       cc * C + (s + 1) * 128],
                                   [f'woT_{cc}_{s}'])

            for tc in range(NTC):
                b = tc % 2
                p.dma('pool', ctxc[b][:, :].rearrange('p (h t) -> p h t', t=128),
                      ctx_scr.ap()[:, :, tc * 128:(tc + 1) * 128]
                      .rearrange('h p t -> p h t'),
                      reads=[f'ctx_{h}_{tc // 4}' for h in range(H)],
                      writes=[f'a2_ctxc{b}'], stream=f'ctxc{b}')
                p.dma('sp', xrow[b][:, :], x_in.ap()[tc * 128:(tc + 1) * 128, :],
                      writes=[f'a2_xrow{b}'], stream=f'xrow{b}')
                for cog in range(4):
                    for cc in range(NCC):
                        p.op('pe', lambda E, b=b, cc=cc, cog=cog: E.matmul(
                            PS0[:, cog * 512:(cog + 1) * 512],
                            ctxc[b][:, cc * 128:(cc + 1) * 128],
                            woT[:, cc * C + cog * 512: cc * C + (cog + 1) * 512],
                            start=(cc == 0), stop=(cc == NCC - 1)),
                            reads=[f'a2_ctxc{b}'] +
                            [f'woT_{cc}_{s}' for s in range(cog * 4, cog * 4 + 4)],
                            writes=['PS0a'])
                p.op('dve', lambda E, b=b: E.tensor_tensor(
                    r_sb[:, :], PS0[:, :], xrow[b][:, :], ALU.add),
                    reads=['PS0a', f'a2_xrow{b}'], writes=['a2_r'])
                ln_epilogue(r_sb[:, :], ['a2_r'], x1f[:, :], 'a2_x1f',
                            junk[:, :], 'a2_junk')
                p.op('dve', lambda E: E.tensor_copy(x1b[:, :], x1f[:, :]),
                     reads=['a2_x1f'], writes=['a2_x1b'])
                p.dma('sp', x1_scr.ap()[tc * 128:(tc + 1) * 128, :], x1f[:, :],
                      reads=['a2_x1f'], writes=[f'x1_{tc}'], stream='x1_w')
                for cc in range(NCC):
                    transpose_tile(x1b[:, cc * 128:(cc + 1) * 128],
                                   ['a2_x1b'],
                                   x1Tst[:, cc * 128:(cc + 1) * 128],
                                   [f'a2_x1Tst{cc}'])
                p.dma('pool',
                      x1T_scr.ap()[:, :, tc * 128:(tc + 1) * 128]
                      .rearrange('cc p t -> p cc t'),
                      x1Tst[:, :].rearrange('p (cc t) -> p cc t', t=128),
                      reads=[f'a2_x1Tst{cc}' for cc in range(NCC)],
                      writes=[f'x1T_{tc}'], stream='x1T_w')
            p.barrier()

    if 'f' in PHASES:
        with ExitStack() as stack:
            zb = lambda n, s, d: stack.enter_context(nc.sbuf_tensor(n, s, d))
            TH = T // 2
            NTC_H = TH // 128
            NTT = TH // 512
            routT = zb('f_routT', [128, NCC * 16], BF16)
            shw = zb('f_shw', [128, NCC], BF16)
            with nc.sbuf_tensor('f_rTl', [16, C], F32) as rTl, \
                    nc.sbuf_tensor('f_rTb', [16, C], BF16) as rTb, \
                    nc.sbuf_tensor('f_shwf', [128, NCC], F32) as shwf:
                p.dma('sp', rTl[:, :], router_w.ap(), writes=['f_rTl'], stream='rTl')
                p.op('dve', lambda E: E.tensor_copy(rTb[:, :], rTl[:, :]),
                     reads=['f_rTl'], writes=['f_rTb'])
                for cc in range(NCC):
                    transpose_tile(rTb[0:16, cc * 128:(cc + 1) * 128],
                                   ['f_rTb'],
                                   routT[:, cc * 16:(cc + 1) * 16],
                                   [f'routT{cc}'], kpart=16, mfree=128)
                p.dma('sp', shwf[:, :].rearrange('p (cc one) -> p cc one', one=1),
                      sh_gate_w.ap().rearrange('(cc p) one -> p cc one', p=128),
                      writes=['f_shwf'], slow=True, stream='shwf')
                p.op('dve', lambda E: E.tensor_copy(shw[:, :], shwf[:, :]),
                     reads=['f_shwf'], writes=['f_shw'])
                p.barrier()

            x1Th = zb('x1Th', [128, NCC * TH], BF16)
            acc = zb('acc', [128, NTC_H * C], F32)
            aT = zb('aT', [128, NIC * TH], BF16)
            wgl = [zb(f'f_wgl{i}', [128, C], F32) for i in range(2)]
            wgb = [[zb(f'f_wgb{i}_{j}', [128, C], BF16) for j in range(2)]
                   for i in range(2)]
            wdb = zb('f_wdb', [128, NIC * C], BF16)
            wdl = [zb(f'f_wdl{i}', [128, 1024], F32) for i in range(2)]
            prb = zb('f_prb', [128, 16], F32)
            prw = zb('f_prw', [128, 16], F32)
            j16 = zb('f_j16', [128, 16], F32)
            sgst = zb('f_sgst', [128, 512], BF16)
            x1r = [wgl[0]]
            ps_g = [PS0[:, 0:512], PS0[:, 512:1024]]
            ps_u = [PS0[:, 1024:1536], PS0[:, 1536:2048]]
            ps_d = [PS1[:, 0:512], PS1[:, 512:1024]]
            ps_r = PS2[:, 0:16]
            ps_s = PS2[:, 128:129]

            for half in range(2):
                t0 = half * TH
                tc0 = t0 // 128
                for cc in range(NCC):
                    p.dma('pool', x1Th[:, cc * TH:(cc + 1) * TH],
                          x1T_scr.ap()[cc][:, t0:t0 + TH],
                          reads=[f'x1T_{tc0 + i}' for i in range(NTC_H)],
                          writes=[f'x1Th_{cc}'], stream='x1Th')
                x1Tkeys = [f'x1Th_{cc}' for cc in range(NCC)]
                for tl in range(NTC_H):
                    tc = tc0 + tl
                    for cc in range(NCC):
                        p.op('pe', lambda E, cc=cc, tl=tl: E.matmul(
                            ps_r, x1Th[:, cc * TH + tl * 128:
                                       cc * TH + (tl + 1) * 128],
                            routT[:, cc * 16:(cc + 1) * 16],
                            start=(cc == 0), stop=(cc == NCC - 1)),
                            reads=[f'x1Th_{cc}', f'routT{cc}'],
                            writes=['ps_r'])
                    for cc in range(NCC):
                        p.op('pe', lambda E, cc=cc, tl=tl: E.matmul(
                            ps_s, x1Th[:, cc * TH + tl * 128:
                                       cc * TH + (tl + 1) * 128],
                            shw[:, cc:cc + 1],
                            start=(cc == 0), stop=(cc == NCC - 1)),
                            reads=[f'x1Th_{cc}', 'f_shw'], writes=['ps_s'])
                    p.op('act', lambda E, tc=tc: E.activation(
                        sg_sb[:, tc:tc + 1], ps_s, AF.Sigmoid),
                        reads=['ps_s'], writes=['sg'])
                    p.op('dve', lambda E: E.reduce_max(
                        st_sb[:, 0:1], ps_r, axis=AX.X, negate=True),
                        reads=['ps_r'], writes=['st0'])
                    p.op('act', lambda E: E.activation(
                        prb[:, :], ps_r, AF.Exp, bias=st_sb[:, 0:1]),
                        reads=['ps_r', 'st0'], writes=['f_prb'])
                    p.op('dve', lambda E: E.tensor_copy(prw[:, :], prb[:, :]),
                         reads=['f_prb'], writes=['f_prw'])
                    for it in range(4):
                        p.op('dve', lambda E: E.reduce_max(
                            st_sb[:, 1:2], prw[:, :], axis=AX.X),
                            reads=['f_prw'], writes=['st1'])
                        if it < 3:
                            p.op('dve', lambda E: E.tensor_single_scalar(
                                j16[:, :], prw[:, :], st_sb[:, 1:2],
                                ALU.is_ge), reads=['f_prw', 'st1'],
                                writes=['f_j16'])
                            p.op('dve', lambda E: E.scalar_tensor_tensor(
                                prw[:, :], j16[:, :], NEG, prw[:, :],
                                ALU.mult, ALU.add),
                                reads=['f_j16', 'f_prw'], writes=['f_prw'])
                    p.op('dve', lambda E: E.tensor_single_scalar(
                        prw[:, :], prb[:, :], st_sb[:, 1:2], ALU.is_ge),
                        reads=['f_prb', 'st1'], writes=['f_prw'])
                    p.op('dve', lambda E: E.tensor_tensor(
                        prw[:, :], prw[:, :], prb[:, :], ALU.mult),
                        reads=['f_prw', 'f_prb'], writes=['f_prw'])
                    p.op('dve', lambda E: E.reduce_sum(
                        st_sb[:, 2:3], prw[:, :], axis=AX.X),
                        reads=['f_prw'], writes=['st2'])
                    p.op('dve', lambda E: E.reciprocal(
                        st_sb[:, 3:4], st_sb[:, 2:3]),
                        reads=['st2'], writes=['st3'])
                    p.op('dve', lambda E, tc=tc: E.tensor_scalar_mul(
                        comb_sb[:, tc * 16:(tc + 1) * 16], prw[:, :],
                        st_sb[:, 3:4]),
                        reads=['f_prw', 'st3'], writes=['comb'])

                experts = [('moe', e) for e in range(E_MOE)] + \
                          [('sh', v) for v in range(4)]
                for ei, (kind, e) in enumerate(experts):
                    for ic in range(NIC):
                        wp = ic % 2
                        for gi, (wt, sh_wt) in enumerate(
                                ((w_gate, sh_gate), (w_up, sh_up))):
                            if kind == 'moe':
                                wsrc = wt.ap()[e][ic].rearrange(
                                    'p cc i -> p (cc i)')
                            else:
                                wsrc = sh_wt.ap()[e * NIC + ic].rearrange(
                                    'p cc i -> p (cc i)')
                            p.dma('sp' if gi == 0 else 'pool',
                                  wgl[gi][:, :], wsrc,
                                  writes=[f'f_wgl{gi}'],
                                  stream='wld0' if gi == 0 else 'wup')
                            p.op('dve', lambda E, gi=gi, wp=wp: E.tensor_copy(
                                wgb[gi][wp][:, :], wgl[gi][:, :]),
                                reads=[f'f_wgl{gi}'], writes=[f'f_wgb{gi}_{wp}'])
                        for tt in range(NTT):
                            pb = (ic * NTT + tt) % 2
                            for cc in range(NCC):
                                p.op('pe', lambda E, cc=cc, tt=tt, pb=pb,
                                     wp=wp: E.matmul(
                                         ps_g[pb],
                                         wgb[0][wp][:, cc * 128:(cc + 1) * 128],
                                         x1Th[:, cc * TH + tt * 512:
                                              cc * TH + (tt + 1) * 512],
                                         start=(cc == 0),
                                         stop=(cc == NCC - 1)),
                                     reads=[f'f_wgb0_{wp}', f'x1Th_{cc}'],
                                     writes=[f'ps_g{pb}'])
                            for cc in range(NCC):
                                p.op('pe', lambda E, cc=cc, tt=tt, pb=pb,
                                     wp=wp: E.matmul(
                                         ps_u[pb],
                                         wgb[1][wp][:, cc * 128:(cc + 1) * 128],
                                         x1Th[:, cc * TH + tt * 512:
                                              cc * TH + (tt + 1) * 512],
                                         start=(cc == 0),
                                         stop=(cc == NCC - 1)),
                                     reads=[f'f_wgb1_{wp}', f'x1Th_{cc}'],
                                     writes=[f'ps_u{pb}'])
                            p.op('act', lambda E, pb=pb: E.activation(
                                sgst[:, :], ps_g[pb], AF.Silu),
                                reads=[f'ps_g{pb}'], writes=['f_sgst'])
                            p.op('dve', lambda E, ic=ic, tt=tt, pb=pb:
                                 E.tensor_tensor(
                                     aT[:, ic * TH + tt * 512:
                                        ic * TH + (tt + 1) * 512],
                                     sgst[:, :], ps_u[pb], ALU.mult),
                                 reads=['f_sgst', f'ps_u{pb}'],
                                 writes=[f'aT_{ic}_{tt}'])
                        if kind == 'moe':
                            wsrc = w_down.ap()[e][ic * 128:(ic + 1) * 128, :]
                        else:
                            i0 = e * I_MOE + ic * 128
                            wsrc = sh_down.ap()[i0:i0 + 128, :]
                        for hf in range(2):
                            gi = hf
                            p.dma('pool', wdl[gi][:, :],
                                  wsrc[:, hf * 1024:(hf + 1) * 1024],
                                  writes=[f'f_wdl{gi}'], stream=f'wdn{gi}')
                            p.op('dve', lambda E, gi=gi, ic=ic, hf=hf:
                                 E.tensor_copy(
                                     wdb[:, ic * C + hf * 1024:
                                         ic * C + (hf + 1) * 1024],
                                     wdl[gi][:, :]),
                                 reads=[f'f_wdl{gi}'],
                                 writes=[f'f_wdb_{ic}_{hf}'])
                    for ts in range(NTC_H):
                        tc = tc0 + ts
                        if kind == 'moe':
                            scol = comb_sb[:, tc * 16 + e: tc * 16 + e + 1]
                            skey = 'comb'
                        else:
                            scol = sg_sb[:, tc:tc + 1]
                            skey = 'sg'
                        for cog in range(4):
                            pb = (ts * 4 + cog) % 2
                            for ic in range(NIC):
                                p.op('pe', lambda E, ic=ic, ts=ts, cog=cog,
                                     pb=pb: E.matmul(
                                         ps_d[pb],
                                         aT[:, ic * TH + ts * 128:
                                            ic * TH + (ts + 1) * 128],
                                         wdb[:, ic * C + cog * 512:
                                             ic * C + (cog + 1) * 512],
                                         start=(ic == 0),
                                         stop=(ic == NIC - 1)),
                                     reads=[f'f_wdb_{ic}_{cog // 2}',
                                            f'aT_{ic}_{ts // 4}'],
                                     writes=[f'ps_d{pb}'])
                            if ei == 0:
                                p.op('dve', lambda E, ts=ts, cog=cog, pb=pb,
                                     scol=scol: E.tensor_scalar_mul(
                                         acc[:, ts * C + cog * 512:
                                             ts * C + (cog + 1) * 512],
                                         ps_d[pb], scol),
                                     reads=[f'ps_d{pb}', skey],
                                     writes=[f'acc_{ts}_{cog}'])
                            else:
                                p.op('dve', lambda E, ts=ts, cog=cog, pb=pb,
                                     scol=scol: E.scalar_tensor_tensor(
                                         acc[:, ts * C + cog * 512:
                                             ts * C + (cog + 1) * 512],
                                         ps_d[pb], scol,
                                         acc[:, ts * C + cog * 512:
                                             ts * C + (cog + 1) * 512],
                                         ALU.mult, ALU.add),
                                     reads=[f'ps_d{pb}', skey,
                                            f'acc_{ts}_{cog}'],
                                     writes=[f'acc_{ts}_{cog}'])

                for ts in range(NTC_H):
                    tc = tc0 + ts
                    b = 0
                    p.dma('pool', x1r[b][:, :],
                          x1_scr.ap()[tc * 128:(tc + 1) * 128, :],
                          reads=[f'x1_{tc}'], writes=['f_wgl0'],
                          stream='x1row')
                    acck = [f'acc_{ts}_{cog}' for cog in range(4)]
                    p.op('dve', lambda E, ts=ts, b=b: E.tensor_tensor(
                        acc[:, ts * C:(ts + 1) * C],
                        acc[:, ts * C:(ts + 1) * C],
                        x1r[b][:, :], ALU.add),
                        reads=acck + ['f_wgl0'], writes=acck)
                    ln_epilogue(acc[:, ts * C:(ts + 1) * C], acck,
                                x1r[b][:, :], 'f_wgl0',
                                x1r[b][:, :], 'f_wgl0')
                    p.dma('pool', out.ap()[tc * 128:(tc + 1) * 128, :],
                          x1r[b][:, :],
                          reads=['f_wgl0'], writes=[f'out_{tc}'],
                          stream='out_w')
                p.barrier()

        maybe_dump('comb_dbg', comb_sb[:, :], [128, NTC * E_MOE], ['comb'])
        maybe_dump('sg_dbg', sg_sb[:, :], [128, NTC], ['sg'])

    p.finalize()
    p.emit()
    return nc


def _zero_init_acc():
    pass


def _tile_moe(w):
    E, Cx, Ix = w.shape
    return np.ascontiguousarray(
        w.reshape(E, NCC, 128, Ix // 128, 128).transpose(0, 3, 2, 1, 4))


def _tile_sh(w):
    Cx, Ix = w.shape
    return np.ascontiguousarray(
        w.reshape(NCC, 128, Ix // 128, 128).transpose(2, 1, 0, 3))


def kernel(L0_attn_x, L0_attn_w_qkv, L0_attn_w_o, L0_attn_pk, L0_attn_pv,
           L0_attn_uo, L0_attn_info_w1, L0_attn_info_w2, L0_ffn_router_w,
           L0_ffn_w_gate, L0_ffn_w_up, L0_ffn_w_down, L0_ffn_sh_gate,
           L0_ffn_sh_up, L0_ffn_sh_down, L0_ffn_sh_gate_w):
    xs = np.asarray(L0_attn_x)
    T = xs.shape[1]
    nc = build(T)
    consts = make_consts(T)
    common = dict(
        w_qkv=np.asarray(L0_attn_w_qkv), w_o=np.asarray(L0_attn_w_o),
        pk=np.asarray(L0_attn_pk), pv=np.asarray(L0_attn_pv),
        uo=np.asarray(L0_attn_uo), info_w1=np.asarray(L0_attn_info_w1),
        info_w2=np.asarray(L0_attn_info_w2),
        router_w=np.asarray(L0_ffn_router_w),
        w_gate=_tile_moe(np.asarray(L0_ffn_w_gate)),
        w_up=_tile_moe(np.asarray(L0_ffn_w_up)),
        w_down=np.asarray(L0_ffn_w_down),
        sh_gate=_tile_sh(np.asarray(L0_ffn_sh_gate)),
        sh_up=_tile_sh(np.asarray(L0_ffn_sh_up)),
        sh_down=np.asarray(L0_ffn_sh_down),
        sh_gate_w=np.asarray(L0_ffn_sh_gate_w), **consts)
    B = xs.shape[0]
    in_maps = [dict(common, x=np.ascontiguousarray(xs[b])) for b in range(B)]
    res = run_bass_kernel_spmd(nc, in_maps, core_ids=list(range(B)))
    return np.stack([res.results[b]['out'] for b in range(B)], axis=0)
